# version 25
# baseline (speedup 1.0000x reference)
import math
import numpy as np
import ml_dtypes
import concourse.bass as bass
import concourse.mybir as mybir
from concourse.bass_utils import run_bass_kernel_spmd

F32 = mybir.dt.float32
BF16 = mybir.dt.bfloat16
I32 = mybir.dt.int32
AF = mybir.ActivationFunctionType
ALU = mybir.AluOpType
AX = mybir.AxisListType

T = 2048
D = 1024
NDC = 8
DCKV = 128
DR = 32
NH = 16
HD = 64
HID = 3072
NHT = 24
NTT = 16
SCALE = 1.0 / math.sqrt(HD + DR)
S127 = 1.0 / 127.0
S2 = 1.0 / (127.0 * 127.0)
VSTRIDE = NH * 65


class Scope:
    def __init__(self):
        self.cms = []

    def al(self, cm):
        v = cm.__enter__()
        self.cms.append(cm)
        return v

    def close(self):
        while self.cms:
            self.cms.pop().__exit__(None, None, None)

ENGINES = ("tensor", "vector", "scalar", "gpsimd", "sync")
ENG2CTR = {"tensor": "PE", "vector": "DV", "scalar": "AC", "gpsimd": "GP", "sync": "SY"}


class Plan:

    def __init__(self, nc, sems, counts):
        self.nc = nc
        self.sems = sems
        self.counts = counts
        self.items = []

    def emit(self, eng, fn, waits=(), inc=True):
        own = ENG2CTR[eng]
        amt = 16 if eng in ("sync", "gpsimd") else 1
        ws = []
        for (n, v) in waits:
            if v is None or v <= 0:
                continue
            assert v <= self.counts[n], f"wait on future: {n} {v} > {self.counts[n]}"
            ws.append((n, v))
        if inc:
            self.counts[own] += amt
        self.items.append((eng, tuple(ws), fn, own if inc else None, amt))
        return self.counts[own]

    def dma(self, eng, out, in_, waits=(), ch=None, **kw):
        own = ch or ENG2CTR[eng]
        ws = []
        for (n, v) in waits:
            if v is None or v <= 0:
                continue
            assert v <= self.counts[n], f"wait on future: {n} {v} > {self.counts[n]}"
            ws.append((n, v))
        self.counts[own] += 16
        self.items.append((eng, tuple(ws), lambda: getattr(self.nc, eng).dma_start(out=out, in_=in_, **kw),
                           own, 16))
        return self.counts[own]

    def dmat(self, out, in_, waits=(), ch=None):
        own = ch or "SY"
        ws = []
        for (n, v) in waits:
            if v is None or v <= 0:
                continue
            assert v <= self.counts[n], f"wait on future: {n} {v} > {self.counts[n]}"
            ws.append((n, v))
        self.counts[own] += 16
        self.items.append(("sync", tuple(ws), lambda: self.nc.sync.dma_start_transpose(out, in_), own, 16))
        return self.counts[own]

    def replay(self):
        nc = self.nc
        with nc.Block() as blk:
            for eng_name in ENGINES:
                self._section(blk, eng_name)

    def _section(self, blk, eng_name):
        items = self.items
        sems = self.sems

        def body(eng):
            hi = {}
            for (e, ws, fn, incn, amt) in items:
                if e != eng_name:
                    continue
                for (n, v) in ws:
                    if hi.get(n, 0) < v:
                        eng.wait_ge(sems[n], v)
                        hi[n] = v
                inst = fn()
                if incn:
                    inst.then_inc(sems[incn], amt)

        getattr(blk, eng_name)(body)


def build(debug=False):
    nc = bass.Bass("TRN2", target_bir_lowering=False)

    dx = nc.dram_tensor("x", [T, D], F32, kind="ExternalInput").ap()
    dwq = nc.dram_tensor("wq", [128, NDC * D], F32, kind="ExternalInput").ap()
    dwkv = nc.dram_tensor("wkv", [128, NDC * DCKV], F32, kind="ExternalInput").ap()
    dwqr = nc.dram_tensor("wqr", [128, NDC * DR], F32, kind="ExternalInput").ap()
    dwkup = nc.dram_tensor("wkup", [128, D], F32, kind="ExternalInput").ap()
    dwvup = nc.dram_tensor("wvup", [128, D], F32, kind="ExternalInput").ap()
    dwkr = nc.dram_tensor("wkr", [128, DR], F32, kind="ExternalInput").ap()
    dwo = nc.dram_tensor("wo", [128, NDC * D], F32, kind="ExternalInput").ap()
    dwg = nc.dram_tensor("wg", [128, NHT * D], I32, kind="ExternalInput").ap()
    dwu = nc.dram_tensor("wu", [128, NHT * D], I32, kind="ExternalInput").ap()
    dwd = nc.dram_tensor("wd", [128, NHT * D], I32, kind="ExternalInput").ap()
    dcos = nc.dram_tensor("tcos", [128, NTT * 16], F32, kind="ExternalInput").ap()
    dsin = nc.dram_tensor("tsin", [128, NTT * 16], F32, kind="ExternalInput").ap()
    dcosq = nc.dram_tensor("tcosq", [128, NTT * 16], F32, kind="ExternalInput").ap()
    dsinq = nc.dram_tensor("tsinq", [128, NTT * 16], F32, kind="ExternalInput").ap()
    dmask = nc.dram_tensor("mask", [128, 128], BF16, kind="ExternalInput").ap()
    dident = nc.dram_tensor("ident", [128, 128], BF16, kind="ExternalInput").ap()
    dhmask = nc.dram_tensor("hmask", [128, 2], BF16, kind="ExternalInput").ap()
    doc2 = nc.dram_tensor("oc2", [128, 2], BF16, kind="ExternalInput").ap()

    dy = nc.dram_tensor("y", [T, D], F32, kind="ExternalOutput").ap()
    xnk = "ExternalOutput" if debug else "Internal"
    dxn = nc.dram_tensor("xn_d", [T, D], BF16, kind=xnk).ap()
    datt = nc.dram_tensor("att_d", [T, D], BF16, kind=xnk).ap()
    if debug:
        dqf = nc.dram_tensor("qf_d", [128, NDC * T], BF16, kind="ExternalOutput").ap()
        dkf = nc.dram_tensor("kf_d", [128, NDC * T], BF16, kind="ExternalOutput").ap()
        dck = nc.dram_tensor("ck_d", [128, T], BF16, kind="ExternalOutput").ap()
        dva = nc.dram_tensor("va_d", [128, NTT * VSTRIDE], BF16, kind="ExternalOutput").ap()
        dqr = nc.dram_tensor("qr_d", [32, T], BF16, kind="ExternalOutput").ap()
        dkr = nc.dram_tensor("kr_d", [32, T], BF16, kind="ExternalOutput").ap()
        dnm = nc.dram_tensor("nm_d", [128, 16], F32, kind="ExternalOutput").ap()
        dha = nc.dram_tensor("ha_d", [128, NHT * 1024], BF16, kind="ExternalOutput").ap()

    CHANNELS = tuple(["PE", "DV", "AC", "GP", "SY", "XL0", "XL1", "XS0", "XS1", "CONST",
                     "WL", "NM", "RR", "AT0", "AT1", "WD", "WG0", "WG1",
                     "CX0", "CX1", "YS0", "YS1"]
                    + ["TP%d" % i for i in range(8)] + ["TA%d" % i for i in range(8)])
    counts = {c: 0 for c in CHANNELS}
    ctx = []

    def alloc(cm):
        v = cm.__enter__()
        ctx.append(cm)
        return v

    sems = {}
    for name in CHANNELS:
        sems[name] = alloc(nc.semaphore("s" + name))

    XN = alloc(nc.sbuf_tensor("XN", [128, NDC * T], BF16))
    WO2 = alloc(nc.sbuf_tensor("WO2", [128, NDC * D], BF16))
    MSK = alloc(nc.sbuf_tensor("MSK", [128, 128], BF16))
    IDN = alloc(nc.sbuf_tensor("IDN", [128, 128], BF16))
    HMK = alloc(nc.sbuf_tensor("HMK", [128, 2], BF16))
    OC2 = alloc(nc.sbuf_tensor("OC2", [128, 2], BF16))
    NEGM = alloc(nc.sbuf_tensor("NEGM", [128, 16], F32))

    scA = Scope()
    QF = scA.al(nc.sbuf_tensor("QF", [128, NDC * T], BF16))
    KF = scA.al(nc.sbuf_tensor("KF", [128, NDC * T], BF16))
    CKV = scA.al(nc.sbuf_tensor("CKV", [128, T], BF16))
    VA = scA.al(nc.sbuf_tensor("VA", [128, NTT * VSTRIDE], BF16))
    QRF = scA.al(nc.sbuf_tensor("QRF", [32, T], BF16))
    KRF = scA.al(nc.sbuf_tensor("KRF", [32, T], BF16))
    CT = scA.al(nc.sbuf_tensor("CT", [128, NTT * 16], F32))
    ST = scA.al(nc.sbuf_tensor("ST", [128, NTT * 16], F32))
    CQ = scA.al(nc.sbuf_tensor("CQ", [128, NTT * 16], F32))
    SQ = scA.al(nc.sbuf_tensor("SQ", [128, NTT * 16], F32))
    if True:
        sc1 = Scope()
        WQ = sc1.al(nc.sbuf_tensor("WQ", [128, NDC * D], BF16))
        WKV = sc1.al(nc.sbuf_tensor("WKV", [128, NDC * DCKV], BF16))
        WQR = sc1.al(nc.sbuf_tensor("WQR", [128, NDC * DR], BF16))
        WKUP = sc1.al(nc.sbuf_tensor("WKUP", [128, D], BF16))
        WVUP = sc1.al(nc.sbuf_tensor("WVUP", [128, D], BF16))
        WKR = sc1.al(nc.sbuf_tensor("WKR", [128, DR], BF16))
        LNX = sc1.al(nc.sbuf_tensor("LNX", [128, 2 * D], F32))
        XNT = sc1.al(nc.sbuf_tensor("XNT", [128, 2 * D], BF16))
        BN6 = sc1.al(nc.sbuf_tensor("BN6", [128, 12], F32))
        MV = sc1.al(nc.sbuf_tensor("MV", [128, 2], F32))
        SMT = sc1.al(nc.sbuf_tensor("SMT", [128, 4], F32))
        RSC = sc1.al(nc.sbuf_tensor("RSC", [128, 128], F32))
        QRT = sc1.al(nc.sbuf_tensor("QRT", [128, NTT * DR], BF16))
        KRT = sc1.al(nc.sbuf_tensor("KRT", [128, NTT * DR], BF16))
        SQS = sc1.al(nc.sbuf_tensor("SQS", [128, T], BF16))
        NMX = sc1.al(nc.sbuf_tensor("NMX", [2, 32], F32))
        NSC = sc1.al(nc.sbuf_tensor("NSC", [2, 32], F32))
        PA = sc1.al(nc.psum_tensor("PA", [128, 512], F32))
        PB = sc1.al(nc.psum_tensor("PB", [128, 512], F32))
        PVh = sc1.al(nc.psum_tensor("PVh", [128, 512], F32))
        PR = sc1.al(nc.psum_tensor("PR", [128, 512], F32))
        PT = sc1.al(nc.psum_tensor("PT", [128, 512], BF16))
        PN = sc1.al(nc.psum_tensor("PN", [128, 512], F32))
        PJ = sc1.al(nc.psum_tensor("PJ", [128, 512], F32))
        if True:
            p = Plan(nc, sems, counts)

            p.dma("sync", CT[:], dcos, ch="CONST")
            p.dma("sync", ST[:], dsin, ch="CONST")
            p.dma("sync", CQ[:], dcosq, ch="CONST")
            p.dma("sync", SQ[:], dsinq, ch="CONST")
            p.dma("sync", MSK[:], dmask, ch="CONST")
            p.dma("sync", IDN[:], dident, ch="CONST")
            p.dma("sync", HMK[:], dhmask, ch="CONST")
            c_const = p.dma("sync", OC2[:], doc2, ch="CONST")
            c_tabs = ("CONST", c_const)
            p.dma("gpsimd", WQ[:], dwq, ch="WL")
            p.dma("gpsimd", WKV[:], dwkv, ch="WL")
            p.dma("gpsimd", WQR[:], dwqr, ch="WL")
            p.dma("gpsimd", WKUP[:], dwkup, ch="WL")
            p.dma("gpsimd", WVUP[:], dwvup, ch="WL")
            p.dma("gpsimd", WKR[:], dwkr, ch="WL")
            c_wl = p.dma("gpsimd", WO2[:], dwo, ch="WL")
            c_wo2 = c_wl

            def filler(waits=()):
                p.emit("tensor", lambda: nc.tensor.matmul(
                    PJ[:, 0:128], IDN[:], IDN[:], start=True, stop=True), waits=waits, inc=False)

            for i in range(30):
                filler([("CONST", c_const)] if i == 0 else ())

            c_vaones = p.emit("vector", lambda: nc.vector.memset(VA[:], 1.0))

            ln_xn = [0] * NTT
            ln_store = [0] * NTT
            ln_exp = [0] * NTT
            c_aggr_prev = 0
            for tt in range(NTT):
                b = tt % 2
                w = [("DV", ln_xn[tt - 2])] if tt >= 2 else []
                c_ld = p.dma("sync", LNX[:, b * D:(b + 1) * D], dx[tt * 128:(tt + 1) * 128, :],
                             waits=w, ch="XL%d" % b)
                p.emit("vector", lambda b=b: nc.vector.bn_stats(BN6[:, 0:6], LNX[:, b * D: b * D + 512]),
                       waits=[("XL%d" % b, c_ld), ("DV", c_aggr_prev)])
                c_bn2 = p.emit("vector", lambda b=b: nc.vector.bn_stats(BN6[:, 6:12], LNX[:, b * D + 512: b * D + 1024]))
                c_aggr_prev = p.emit("vector", lambda: nc.vector.bn_aggr(MV[:, 0:2], BN6[:, 0:12]),
                                     waits=[("DV", c_bn2)])
                wv = [("DV", c_aggr_prev)]
                if tt >= 1:
                    wv.append(("AC", ln_exp[tt - 1]))
                c_v = p.emit("vector", lambda: nc.vector.tensor_scalar(SMT[:, 0:1], MV[:, 1:2], 1e-5, None, ALU.add),
                             waits=wv)
                c_ln = p.emit("scalar", lambda: nc.scalar.activation(SMT[:, 1:2], SMT[:, 0:1], AF.Ln),
                              waits=[("DV", c_v)])
                ln_exp[tt] = p.emit("scalar", lambda: nc.scalar.activation(
                    SMT[:, 2:3], SMT[:, 1:2], AF.Exp, scale=-0.5), waits=[("AC", c_ln)])
                c_stt = p.emit("vector", lambda: nc.vector.scalar_tensor_tensor(
                    SMT[:, 3:4], SMT[:, 2:3], -1.0, MV[:, 0:1], ALU.mult, ALU.mult), waits=[("AC", ln_exp[tt])])
                wst = [("DV", c_stt)]
                if tt >= 2:
                    wst.append(("XS%d" % b, ln_store[tt - 2]))
                ln_xn[tt] = p.emit("vector", lambda b=b: nc.vector.tensor_scalar(
                    XNT[:, b * D:(b + 1) * D], LNX[:, b * D:(b + 1) * D], SMT[:, 2:3], SMT[:, 3:4],
                    ALU.mult, ALU.add), waits=wst)
                ln_store[tt] = p.dma("sync", dxn[tt * 128:(tt + 1) * 128, :], XNT[:, b * D:(b + 1) * D],
                                     waits=[("DV", ln_xn[tt])], ch="XS%d" % b)
                filler([("XL%d" % b, c_ld)])
                for _ in range(4):
                    filler()

            c_tp = 0
            for dc in range(NDC):
                w = [("XS0", counts["XS0"]), ("XS1", counts["XS1"]), ("XL0", counts["XL0"]),
                     ("XL1", counts["XL1"]), ("CONST", counts["CONST"]), ("WL", c_wl)] if dc == 0 \
                    else [("TP%d" % (dc - 1), 16)]
                c_tp = p.dmat(XN[:, dc * T:(dc + 1) * T], dxn[:, dc * 128:(dc + 1) * 128],
                              waits=w, ch="TP%d" % dc)
                filler([("TP%d" % dc, 16)])
                for _ in range(12):
                    filler()
            c_xn_ready = [("TP%d" % i, 16) for i in range(8)]

            def proj_group(psum_ap, lhsT_list, rhs_list, first_waits):
                last = 0
                n = len(lhsT_list)
                for i, (lh, rh) in enumerate(zip(lhsT_list, rhs_list)):
                    w = first_waits if i == 0 else []
                    last = p.emit("tensor", lambda lh=lh, rh=rh, i=i, n=n, ps=psum_ap: nc.tensor.matmul(
                        ps, lh, rh, start=(i == 0), stop=(i == n - 1)), waits=w, inc=(i == n - 1))
                return last

            psel = [PA, PB]
            evac = [0, 0]
            for qb in range(4):
                i = qb % 2
                w = c_xn_ready + [("WL", c_wl), ("AC", evac[i])]
                c_mm = proj_group(psel[i][:, 0:512], [WKV[:, dc * 128:(dc + 1) * 128] for dc in range(NDC)],
                                  [XN[:, dc * T + qb * 512: dc * T + (qb + 1) * 512] for dc in range(NDC)], w)
                evac[i] = p.emit("scalar", lambda qb=qb, i=i: nc.scalar.activation(
                    CKV[:, qb * 512:(qb + 1) * 512], psel[i][:, 0:512], AF.Copy), waits=[("PE", c_mm)])
            c_ckv_ready = max(evac)

            for nt in range(NDC):
                for qb in range(4):
                    i = (nt * 4 + qb) % 2
                    c_mm = proj_group(psel[i][:, 0:512],
                                      [WQ[:, dc * D + nt * 128: dc * D + (nt + 1) * 128] for dc in range(NDC)],
                                      [XN[:, dc * T + qb * 512: dc * T + (qb + 1) * 512] for dc in range(NDC)],
                                      [("AC", evac[i])])
                    evac[i] = p.emit("scalar", lambda nt=nt, qb=qb, i=i: nc.scalar.activation(
                        QF[:, nt * T + qb * 512: nt * T + (qb + 1) * 512], psel[i][:, 0:512], AF.Copy,
                        scale=SCALE), waits=[("PE", c_mm)])

            for nt in range(NDC):
                for qb in range(4):
                    i = (nt * 4 + qb) % 2
                    c_mm = proj_group(psel[i][:, 0:512], [WKUP[:, nt * 128:(nt + 1) * 128]],
                                      [CKV[:, qb * 512:(qb + 1) * 512]],
                                      [("AC", max(evac[i], c_ckv_ready))])
                    evac[i] = p.emit("scalar", lambda nt=nt, qb=qb, i=i: nc.scalar.activation(
                        KF[:, nt * T + qb * 512: nt * T + (qb + 1) * 512], psel[i][:, 0:512], AF.Copy),
                        waits=[("PE", c_mm)])

            va_v = VA[:].rearrange("p (tt h j) -> p tt h j", tt=NTT, h=NH)
            c_v_dv = 0
            for tt in range(NTT):
                w0 = [("AC", c_ckv_ready), ("DV", c_vaones)]
                if tt >= 1:
                    w0.append(("DV", c_v_dv))
                c_mm1 = p.emit("tensor", lambda tt=tt: nc.tensor.matmul(
                    PVh[:, 0:512], CKV[:, tt * 128:(tt + 1) * 128], WVUP[:, 0:512],
                    start=True, stop=True), waits=w0)
                c_mm2 = p.emit("tensor", lambda tt=tt: nc.tensor.matmul(
                    PR[:, 0:512], CKV[:, tt * 128:(tt + 1) * 128], WVUP[:, 512:1024],
                    start=True, stop=True))
                wv1 = [("PE", c_mm1)]
                if tt == 0:
                    wv1.append(("DV", c_vaones))
                p.emit("vector", lambda tt=tt: nc.vector.tensor_copy(
                    va_v[:, tt, 0:8, 0:64], PVh[:, 0:512]), waits=wv1)
                c_v_dv = p.emit("vector", lambda tt=tt: nc.vector.tensor_copy(
                    va_v[:, tt, 8:16, 0:64], PR[:, 0:512]), waits=[("PE", c_mm2)])

            def rope_block(tt, psum, rb, ctab, stab, dstT, w_first):
                x1 = psum[:, 0:16]
                x2 = psum[:, 16:32]
                c = ctab[:, tt * 16:(tt + 1) * 16]
                s = stab[:, tt * 16:(tt + 1) * 16]
                o = rb * 64
                p.emit("vector", lambda: nc.vector.tensor_tensor(RSC[:, o:o + 16], x1, c, ALU.mult),
                       waits=w_first)
                c_b = p.emit("vector", lambda: nc.vector.tensor_tensor(RSC[:, o + 16:o + 32], x2, s, ALU.mult))
                p.emit("vector", lambda dstT=dstT, tt=tt: nc.vector.tensor_tensor(
                    dstT[:, tt * 32: tt * 32 + 16], RSC[:, o:o + 16], RSC[:, o + 16:o + 32], ALU.subtract),
                    waits=[("DV", c_b)])
                p.emit("vector", lambda: nc.vector.tensor_tensor(RSC[:, o + 32:o + 48], x2, c, ALU.mult))
                c_d = p.emit("vector", lambda: nc.vector.tensor_tensor(RSC[:, o + 48:o + 64], x1, s, ALU.mult))
                return p.emit("vector", lambda dstT=dstT, tt=tt: nc.vector.tensor_tensor(
                    dstT[:, tt * 32 + 16: tt * 32 + 32], RSC[:, o + 32:o + 48], RSC[:, o + 48:o + 64],
                    ALU.add), waits=[("DV", c_d)])

            c_rq = [0] * (NTT + 1)
            c_rk = [0] * (NTT + 1)
            c_rf_dv = 0
            QPS = [PR, PA]
            KPS = [PVh, PB]
            for tt in range(NTT):
                qp = QPS[tt % 2]
                kp = KPS[tt % 2]
                wq = [("DV", c_rq[tt - 2])] if tt >= 2 else (
                    [c_tabs, ("DV", c_v_dv), ("AC", counts["AC"])])
                c_mmq = proj_group(qp[:, 0:32],
                                   [XN[:, dc * T + tt * 128: dc * T + (tt + 1) * 128] for dc in range(NDC)],
                                   [WQR[:, dc * DR:(dc + 1) * DR] for dc in range(NDC)], wq)
                wk = [("DV", c_rk[tt - 2])] if tt >= 2 else [("DV", c_v_dv), ("AC", counts["AC"])]
                c_mmk = proj_group(kp[:, 0:32], [CKV[:, tt * 128:(tt + 1) * 128]], [WKR[:, 0:DR]], wk)
                if tt >= 1:
                    c_t1 = p.emit("tensor", lambda tt=tt: nc.tensor.transpose(
                        PT[0:32, 0:128], QRT[:, (tt - 1) * 32: tt * 32], IDN[:]),
                        waits=[("DV", c_rq[tt - 1]), ("DV", c_rf_dv)])
                    c_t2 = p.emit("tensor", lambda tt=tt: nc.tensor.transpose(
                        PT[0:32, 128:256], KRT[:, (tt - 1) * 32: tt * 32], IDN[:]),
                        waits=[("DV", c_rk[tt - 1])])
                c_rq[tt] = rope_block(tt, qp, 0, CQ, SQ, QRT, [("PE", c_mmq), ("DV", c_rq[tt - 1])])
                c_rk[tt] = rope_block(tt, kp, 1, CT, ST, KRT, [("PE", c_mmk), ("DV", c_rk[tt - 1])])
                if tt >= 1:
                    p.emit("vector", lambda tt=tt: nc.vector.tensor_copy(
                        QRF[:, (tt - 1) * 128: tt * 128], PT[0:32, 0:128]), waits=[("PE", c_t1)])
                    c_rf_dv = p.emit("vector", lambda tt=tt: nc.vector.tensor_copy(
                        KRF[:, (tt - 1) * 128: tt * 128], PT[0:32, 128:256]), waits=[("PE", c_t2)])
            c_t1 = p.emit("tensor", lambda: nc.tensor.transpose(
                PT[0:32, 0:128], QRT[:, (NTT - 1) * 32: NTT * 32], IDN[:]),
                waits=[("DV", c_rq[NTT - 1]), ("DV", c_rf_dv)])
            c_t2 = p.emit("tensor", lambda: nc.tensor.transpose(
                PT[0:32, 128:256], KRT[:, (NTT - 1) * 32: NTT * 32], IDN[:]),
                waits=[("DV", c_rk[NTT - 1])])
            p.emit("vector", lambda: nc.vector.tensor_copy(
                QRF[:, (NTT - 1) * 128: NTT * 128], PT[0:32, 0:128]), waits=[("PE", c_t1)])
            c_rf_dv = p.emit("vector", lambda: nc.vector.tensor_copy(
                KRF[:, (NTT - 1) * 128: NTT * 128], PT[0:32, 128:256]), waits=[("PE", c_t2)])

            c_sqs_rd_pe = 0

            def norm_max(SRC, dst, w_src_dv):
                nonlocal c_sqs_rd_pe
                cdv = counts["DV"]
                for dc in range(NDC):
                    csq = p.emit("vector", lambda dc=dc, SRC=SRC: nc.vector.tensor_tensor(
                        SQS[:, 0:T], SRC[:, dc * T:(dc + 1) * T], SRC[:, dc * T:(dc + 1) * T], ALU.mult),
                        waits=[("PE", c_sqs_rd_pe), ("AC", w_src_dv), ("DV", cdv)])
                    for j in range(4):
                        c_mm = p.emit("tensor", lambda j=j: nc.tensor.matmul(
                            PN[0:2, 0:512], HMK[:, 0:2], SQS[:, j * 512:(j + 1) * 512], start=True, stop=True),
                            waits=[("DV", csq)] if j == 0 else [("DV", cdv)])
                        c_sqs_rd_pe = c_mm
                        cdv = p.emit("vector", lambda dc=dc, j=j: nc.vector.tensor_reduce(
                            NMX[0:2, dc * 4 + j: dc * 4 + j + 1], PN[0:2, 0:512], AX.X, ALU.max),
                            waits=[("PE", c_mm)])
                nmx_v = NMX[0:2, 0:32].rearrange("p (dc j) -> p dc j", dc=8)
                return p.emit("vector", lambda nmx_v=nmx_v, dst=dst: nc.vector.tensor_reduce(
                    dst, nmx_v, AX.X, ALU.max), waits=[("DV", cdv)])

            c_qn = norm_max(QF, NSC[0:2, 0:8], counts["AC"])
            c_kn = norm_max(KF, NSC[0:2, 8:16], counts["AC"])

            def rope_norm_max(SRC, dst, w_src_dv):
                nonlocal c_sqs_rd_pe
                cdv = counts["DV"]
                csq = p.emit("vector", lambda SRC=SRC: nc.vector.tensor_tensor(
                    SQS[0:32, 0:T], SRC[0:32, 0:T], SRC[0:32, 0:T], ALU.mult),
                    waits=[("PE", c_sqs_rd_pe), ("DV", max(w_src_dv, cdv))])
                for j in range(4):
                    c_mm = p.emit("tensor", lambda j=j: nc.tensor.matmul(
                        PN[0:2, 0:512], OC2[0:32, 0:2], SQS[0:32, j * 512:(j + 1) * 512], start=True, stop=True),
                        waits=[("DV", csq), ("CONST", c_const)] if j == 0 else [("DV", cdv)])
                    c_sqs_rd_pe = c_mm
                    cdv = p.emit("vector", lambda j=j: nc.vector.tensor_reduce(
                        NMX[0:2, j:j + 1], PN[0:2, 0:512], AX.X, ALU.max), waits=[("PE", c_mm)])
                return p.emit("vector", lambda dst=dst: nc.vector.tensor_reduce(
                    dst, NMX[0:2, 0:4], AX.X, ALU.max), waits=[("DV", cdv)])

            c_qr2 = rope_norm_max(QRF, NSC[0:2, 16:17], c_rf_dv)
            c_kr2 = rope_norm_max(KRF, NSC[0:2, 17:18], c_rf_dv)

            c1 = p.emit("vector", lambda: nc.vector.tensor_tensor(
                NSC[0:2, 18:26], NSC[0:2, 0:8], NSC[0:2, 8:16], ALU.mult), waits=[("DV", max(c_qn, c_kn))])
            c2 = p.emit("vector", lambda: nc.vector.tensor_tensor(
                NSC[0:2, 26:27], NSC[0:2, 16:17], NSC[0:2, 17:18], ALU.mult),
                waits=[("DV", max(c_qr2, c_kr2))])
            c3 = p.emit("scalar", lambda: nc.scalar.activation(NSC[0:2, 18:26], NSC[0:2, 18:26], AF.Sqrt),
                        waits=[("DV", c2)])
            c4 = p.emit("scalar", lambda: nc.scalar.activation(NSC[0:2, 26:27], NSC[0:2, 26:27], AF.Sqrt))
            c5 = p.emit("vector", lambda: nc.vector.tensor_scalar(
                NSC[0:2, 18:26], NSC[0:2, 18:26], NSC[0:2, 26:27], -1.0, ALU.add, ALU.mult),
                waits=[("AC", c4)])
            p.dma("gpsimd", NEGM[32:33, 0:8], NSC[0:1, 18:26], waits=[("DV", c5)], ch="NM")
            p.dma("gpsimd", NEGM[32:33, 8:16], NSC[1:2, 18:26], ch="NM")
            p.dma("gpsimd", NEGM[96:97, 0:8], NSC[0:1, 18:26], ch="NM")
            c_negm = p.dma("gpsimd", NEGM[96:97, 8:16], NSC[1:2, 18:26], ch="NM")

            if debug:
                wdbg = [("DV", counts["DV"]), ("AC", counts["AC"]), ("NM", counts["NM"])]
                p.dma("sync", dqf, QF[:], waits=wdbg, ch="CONST")
                p.dma("sync", dkf, KF[:], ch="CONST")
                p.dma("sync", dck, CKV[:], ch="CONST")
                p.dma("sync", dva, VA[:], ch="CONST")
                p.dma("sync", dqr, QRF[:], ch="CONST")
                p.dma("sync", dkr, KRF[:], ch="CONST")
                p.dma("sync", dnm, NEGM[:], ch="CONST")

            p.replay()
        sc1.close()

        c_a1_dv = counts["DV"]
        c_a1_ac = counts["AC"]
        c_a1_pe = counts["PE"]
        c_negm = counts["GP"]

        sc2 = Scope()
        QC = sc2.al(nc.sbuf_tensor("QC", [128, 2 * T], BF16))
        KC = sc2.al(nc.sbuf_tensor("KC", [128, 2 * T], BF16))
        ET = sc2.al(nc.sbuf_tensor("ET", [128, 5 * 512], BF16))
        ATS = sc2.al(nc.sbuf_tensor("ATS", [128, 2 * 1024], BF16))
        RCP = sc2.al(nc.sbuf_tensor("RCP", [128, 16], F32))
        STR0 = sc2.al(nc.psum_tensor("STR0", [128, 512], F32))
        STR1 = sc2.al(nc.psum_tensor("STR1", [128, 512], F32))
        STR2 = sc2.al(nc.psum_tensor("STR2", [128, 512], F32))
        STR3 = sc2.al(nc.psum_tensor("STR3", [128, 512], F32))
        STR4 = sc2.al(nc.psum_tensor("STR4", [128, 512], F32))
        AVB0 = sc2.al(nc.psum_tensor("AVB0", [128, 512], F32))
        AVB1 = sc2.al(nc.psum_tensor("AVB1", [128, 512], F32))
        AVB2 = sc2.al(nc.psum_tensor("AVB2", [128, 512], F32))
        if True:
            STRIPS = [STR0, STR1, STR2, STR3, STR4]
            AVB = [AVB0, AVB1, AVB2]
            p = Plan(nc, sems, counts)

            cz1 = p.emit("vector", lambda: nc.vector.memset(QC[:], 0.0),
                         waits=[("DV", c_a1_dv), ("AC", c_a1_ac), ("PE", c_a1_pe)])
            cz2 = p.emit("vector", lambda: nc.vector.memset(KC[:], 0.0))
            p.emit("vector", lambda: nc.vector.memset(KC[96:97, 0:T], 1.0), waits=[("DV", cz2)])
            p.emit("vector", lambda: nc.vector.memset(KC[32:33, T:2 * T], 1.0))
            p.dma("gpsimd", QC[64:96, 0:T], QRF[0:32, 0:T], waits=[("DV", cz1)], ch="RR")
            c_gp_rope = p.dma("gpsimd", KC[64:96, 0:T], KRF[0:32, 0:T], waits=[("DV", cz2)], ch="RR")
            p.emit("vector", lambda: nc.vector.tensor_copy(QC[0:32, T:2 * T], QRF[0:32, 0:T]),
                   waits=[("DV", cz2)])
            c_setup_dv = p.emit("vector", lambda: nc.vector.tensor_copy(KC[0:32, T:2 * T], KRF[0:32, 0:T]))

            slot_exp = [0] * 5
            slot_avdone = [0] * 5
            qk_done = [0] * (NH + 1)
            head_evac_dv = 0
            att_dma = [0, 0]
            datt_v = datt.rearrange("(qt p) n -> p qt n", p=128)
            AVLAG = 3

            def emit_assembly(h):
                s = h % 2
                dc = h // 2
                col = s * 8 + dc
                rows = slice(64 * s, 64 * s + 64)
                mrow = 96 if s == 0 else 32
                base = s * T
                w_cp = [("PE", qk_done[h - 2] if h >= 2 else 0), ("RR", c_gp_rope), ("DV", c_setup_dv)]
                p.emit("vector", lambda rows=rows, base=base, dc=dc: nc.vector.tensor_copy(
                    QC[rows, base: base + T], QF[rows, dc * T:(dc + 1) * T]), waits=w_cp)
                p.emit("vector", lambda rows=rows, base=base, dc=dc: nc.vector.tensor_copy(
                    KC[rows, base: base + T], KF[rows, dc * T:(dc + 1) * T]))
                return p.emit("vector", lambda mrow=mrow, base=base, col=col: nc.vector.tensor_scalar(
                    QC[mrow:mrow + 1, base: base + T], KC[mrow:mrow + 1, base: base + T],
                    NEGM[mrow:mrow + 1, col:col + 1], None, ALU.mult),
                    waits=[("NM", c_negm), ("DV", c_setup_dv)])

            c_asm_arr = [0] * NH
            c_asm_arr[0] = emit_assembly(0)
            evac_bank_prev = [0, 0, 0]

            for h in range(NH):
                s = h % 2
                base = s * T
                av_started = [False, False, False]
                pend = []

                def flush_av(info):
                    slot, kt, c_exp, c_mask, qts, first, ebase = info
                    for idx, qt in enumerate(qts):
                        bank = qt // 7
                        off7 = (qt % 7) * 66
                        eoff = ebase + qt * 128
                        wav = []
                        if idx == 0:
                            wav = [("AC", c_exp)]
                            if c_mask:
                                wav.append(("DV", c_mask))
                        st = not av_started[bank]
                        av_started[bank] = True
                        if st:
                            wav.append(("DV", evac_bank_prev[bank]))
                        is_last = (idx == len(qts) - 1)
                        c_av = p.emit("tensor",
                                      lambda bank=bank, off7=off7, eoff=eoff, st=st, kt=kt, h=h, qt=qt:
                                      nc.tensor.matmul(
                                          AVB[bank][:, off7: off7 + 65], ET[:, eoff: eoff + 128],
                                          VA[:, kt * VSTRIDE + h * 65: kt * VSTRIDE + (h + 1) * 65],
                                          start=st, stop=(qt == kt), skip_group_check=True),
                                      waits=wav, inc=is_last)
                        if is_last:
                            slot_avdone[slot] = c_av

                chunks = [(kt, jb) for kt in range(NTT) for jb in range(kt // 4, 4)]
                for ci, (kt, jb) in enumerate(chunks):
                    slot = ci % 5
                    wqk = [("DV", c_asm_arr[h]), ("AC", slot_exp[slot])]
                    c_qk = p.emit("tensor", lambda slot=slot, kt=kt, jb=jb, base=base: nc.tensor.matmul(
                        STRIPS[slot][:, 0:512], KC[:, base + kt * 128: base + (kt + 1) * 128],
                        QC[:, base + jb * 512: base + (jb + 1) * 512], start=True, stop=True), waits=wqk)
                    c_exp = p.emit("scalar", lambda slot=slot: nc.scalar.activation(
                        ET[:, slot * 512:(slot + 1) * 512], STRIPS[slot][:, 0:512], AF.Exp),
                        waits=[("PE", max(c_qk, slot_avdone[slot]))])
                    slot_exp[slot] = c_exp
                    c_mask = 0
                    if jb == kt // 4:
                        off = slot * 512 + kt * 128 - jb * 512
                        c_mask = p.emit("vector", lambda off=off: nc.vector.tensor_tensor(
                            ET[:, off: off + 128], ET[:, off: off + 128], MSK[:], ALU.mult),
                            waits=[("AC", c_exp)])
                    qts = [qt for qt in range(jb * 4, jb * 4 + 4) if qt >= kt]
                    pend.append((slot, kt, c_exp, c_mask, qts, (ci == 0), slot * 512 - jb * 512))
                    if ci == 1 and h + 1 < NH:
                        c_asm_arr[h + 1] = emit_assembly(h + 1)
                    if len(pend) > AVLAG:
                        flush_av(pend.pop(0))
                while pend:
                    flush_av(pend.pop(0))
                qk_done[h] = counts["PE"]
                c_av_all = counts["PE"]

                c_ats_dv = 0
                for qt in range(NTT):
                    bank = qt // 7
                    off7 = (qt % 7) * 66
                    w = [("PE", c_av_all)]
                    if qt == 0:
                        w.append(("DV", head_evac_dv))
                        if h >= 2:
                            w.append(("AT%d" % s, att_dma[s]))
                    c_rc = p.emit("vector", lambda bank=bank, off7=off7, qt=qt: nc.vector.reciprocal(
                        RCP[:, qt:qt + 1], AVB[bank][:, off7 + 64: off7 + 65]), waits=w)
                    c_ats_dv = p.emit("vector", lambda bank=bank, off7=off7, qt=qt, s=s: nc.vector.tensor_scalar(
                        ATS[:, s * 1024 + qt * 64: s * 1024 + (qt + 1) * 64],
                        AVB[bank][:, off7: off7 + 64], RCP[:, qt:qt + 1], None, ALU.mult),
                        waits=[("DV", c_rc)])
                    if qt in (6, 13, 15):
                        evac_bank_prev[qt // 7] = c_ats_dv
                head_evac_dv = c_ats_dv
                ats_v = ATS[:, s * 1024:(s + 1) * 1024].rearrange("p (qt c) -> p qt c", qt=NTT)
                att_dma[s] = p.dma("sync", datt_v[:, :, h * 64:(h + 1) * 64], ats_v,
                                   waits=[("DV", c_ats_dv)], ch="AT%d" % s)

            p.replay()
            sc2.close()
    scA.close()

    c_att_sy = counts["SY"]
    c_a2_dv = counts["DV"]
    c_a2_pe = counts["PE"]
    c_a2_ac = counts["AC"]
    c_a2_gp = counts["GP"]

    scC = Scope()
    AFM = scC.al(nc.sbuf_tensor("AFM", [128, NDC * T], BF16))
    XT1 = scC.al(nc.sbuf_tensor("XT1", [128, 2 * D], F32))
    YT = scC.al(nc.sbuf_tensor("YT", [128, 2 * D], F32))
    WGT = scC.al(nc.sbuf_tensor("WGT", [128, 2 * D], BF16))
    WUT = scC.al(nc.sbuf_tensor("WUT", [128, 2 * D], BF16))
    WDT = scC.al(nc.sbuf_tensor("WDT", [128, NHT * D], BF16))
    HA = scC.al(nc.sbuf_tensor("HA", [128, NHT * 1024], BF16))
    SIL = scC.al(nc.sbuf_tensor("SIL", [128, 1024], BF16))
    PS = [scC.al(nc.psum_tensor(f"PC{i}", [128, 512], F32)) for i in range(8)]
    if True:
        GPs = [PS[0], PS[1]]
        UPs = [PS[2], PS[3]]
        WPS = [[PS[0], PS[1]], [PS[2], PS[3]]]
        DPS = [[PS[4], PS[5]], [PS[6], PS[7]]]
        p = Plan(nc, sems, counts)

        def fillerC(waits=()):
            p.emit("tensor", lambda: nc.tensor.matmul(
                PS[4][:, 0:128], WO2[:, 0:128], WO2[:, 0:128], start=True, stop=True),
                waits=waits, inc=False)

        c_atp = 0
        for dc in range(NDC):
            w = [("AT0", counts["AT0"]), ("AT1", counts["AT1"]), ("NM", counts["NM"]),
                 ("RR", counts["RR"]), ("WL", counts["WL"]), ("XL0", counts["XL0"]),
                 ("XL1", counts["XL1"]), ("XS0", counts["XS0"]), ("XS1", counts["XS1"]),
                 ("CONST", counts["CONST"])] if dc == 0 else [("TA%d" % (dc - 1), 16)]
            c_atp = p.dmat(AFM[:, dc * T:(dc + 1) * T], datt[:, dc * 128:(dc + 1) * 128],
                           waits=w, ch="TA%d" % dc)
            fillerC([("TA%d" % dc, 16)])
            for _ in range(12):
                fillerC()

        c_atp = [("TA%d" % i, 16) for i in range(8)]
        c_wdt = p.dma("gpsimd", WDT[:], dwd, waits=c_atp, ch="WD")
        wg_ld = [0] * NHT
        wu_ld = [0] * NHT
        for ht in range(2):
            wg_ld[ht] = p.dma("gpsimd", WGT[:, ht * D:(ht + 1) * D], dwg[:, ht * D:(ht + 1) * D],
                              ch="WG%d" % (ht % 2))
            wu_ld[ht] = p.dma("gpsimd", WUT[:, ht * D:(ht + 1) * D], dwu[:, ht * D:(ht + 1) * D],
                              ch="WG%d" % (ht % 2))

        gate_done = [0, 0]
        sl_done = [0, 0]
        ha_done = [0, 0]
        ydone = [0] * NTT
        yst = [0, 0]
        xld = [0, 0]
        down_rd = 0
        c_ha_dv = 0

        for half in range(2):
            for ht in range(NHT):
                b = ht % 2
                nxt = ht + 1 if ht + 1 < NHT else None
                if half == 0 and nxt is not None and nxt >= 2:
                    w = [("PE", gate_done[nxt % 2])]
                    wg_ld[nxt] = p.dma("gpsimd", WGT[:, (nxt % 2) * D:((nxt % 2) + 1) * D],
                                       dwg[:, nxt * D:(nxt + 1) * D], waits=w, ch="WG%d" % (nxt % 2))
                    wu_ld[nxt] = p.dma("gpsimd", WUT[:, (nxt % 2) * D:((nxt % 2) + 1) * D],
                                       dwu[:, nxt * D:(nxt + 1) * D], ch="WG%d" % (nxt % 2))
                elif half == 1:
                    w = [("PE", gate_done[b])]
                    wg_ld[ht] = p.dma("gpsimd", WGT[:, b * D:(b + 1) * D],
                                      dwg[:, ht * D:(ht + 1) * D], waits=w, ch="WG%d" % b)
                    wu_ld[ht] = p.dma("gpsimd", WUT[:, b * D:(b + 1) * D],
                                      dwu[:, ht * D:(ht + 1) * D], ch="WG%d" % b)
                for tb in range(2):
                    t0 = half * 1024 + tb * 512
                    c_gm = 0
                    c_um = 0
                    for dcc in range(NDC):
                        wm = []
                        if dcc == 0:
                            wm = [("WG%d" % b, wu_ld[ht]), ("AC", sl_done[tb])]
                            wm.append(("DV", yst[0]))
                        c_gm = p.emit("tensor", lambda b=b, dcc=dcc, t0=t0, tb=tb: nc.tensor.matmul(
                            GPs[tb][:, 0:512], WGT[:, b * D + dcc * 128: b * D + (dcc + 1) * 128],
                            XN[:, dcc * T + t0: dcc * T + t0 + 512], start=(dcc == 0), stop=(dcc == NDC - 1)),
                            waits=wm, inc=(dcc == NDC - 1))
                    for dcc in range(NDC):
                        wm = []
                        if dcc == 0:
                            wm = [("WG%d" % b, wu_ld[ht]), ("DV", max(ha_done[tb], yst[1]))]
                        c_um = p.emit("tensor", lambda b=b, dcc=dcc, t0=t0, tb=tb: nc.tensor.matmul(
                            UPs[tb][:, 0:512], WUT[:, b * D + dcc * 128: b * D + (dcc + 1) * 128],
                            XN[:, dcc * T + t0: dcc * T + t0 + 512], start=(dcc == 0), stop=(dcc == NDC - 1)),
                            waits=wm, inc=(dcc == NDC - 1))
                    gate_done[b] = c_um
                    wsl = [("PE", c_gm), ("DV", ha_done[tb])]
                    sl_done[tb] = p.emit("scalar", lambda tb=tb: nc.scalar.activation(
                        SIL[:, tb * 512:(tb + 1) * 512], GPs[tb][:, 0:512], AF.Silu, scale=S127), waits=wsl)
                    wha = [("AC", sl_done[tb]), ("PE", c_um)]
                    if half == 1 and ht == 0 and tb == 0:
                        wha.append(("PE", down_rd))
                    c_ha_dv = p.emit("vector", lambda ht=ht, tb=tb: nc.vector.tensor_tensor(
                        HA[:, ht * 1024 + tb * 512: ht * 1024 + (tb + 1) * 512],
                        SIL[:, tb * 512:(tb + 1) * 512], UPs[tb][:, 0:512], ALU.mult), waits=wha)
                    ha_done[tb] = c_ha_dv

            for tti in range(8):
                tt = half * 8 + tti
                b = tti % 2
                w = c_atp + [("DV", yst[b])]
                if tt >= 2:
                    w.append(("YS%d" % b, ydone[tt - 2]))
                xld[b] = p.dma("sync", XT1[:, b * D:(b + 1) * D], dx[tt * 128:(tt + 1) * 128, :],
                               waits=w, ch="CX%d" % b)
                cwo = [0, 0]
                for hf2 in range(2):
                    c_mm = 0
                    for dcc in range(NDC):
                        wm = []
                        if dcc == 0:
                            wm = [("WL", c_wo2), ("DV", yst[b])]
                            if tti == 0 and hf2 == 0:
                                wm.append(("AC", max(sl_done)))
                                wm.append(("DV", c_ha_dv))
                        c_mm = p.emit("tensor", lambda tt=tt, hf2=hf2, dcc=dcc, b=b: nc.tensor.matmul(
                            WPS[b][hf2][:, 0:512], AFM[:, dcc * T + tt * 128: dcc * T + (tt + 1) * 128],
                            WO2[:, dcc * D + hf2 * 512: dcc * D + (hf2 + 1) * 512],
                            start=(dcc == 0), stop=(dcc == NDC - 1)), waits=wm, inc=(dcc == NDC - 1))
                    cwo[hf2] = c_mm
                cdn = [0, 0]
                for hd2 in range(2):
                    c_mm = 0
                    for ht in range(NHT):
                        wm = []
                        if ht == 0:
                            wm = [("DV", max(c_ha_dv, yst[b])), ("WD", c_wdt)]
                        c_mm = p.emit("tensor", lambda b=b, hd2=hd2, ht=ht, tti=tti: nc.tensor.matmul(
                            DPS[b][hd2][:, 0:512], HA[:, ht * 1024 + tti * 128: ht * 1024 + (tti + 1) * 128],
                            WDT[:, ht * D + hd2 * 512: ht * D + (hd2 + 1) * 512],
                            start=(ht == 0), stop=(ht == NHT - 1)), waits=wm, inc=(ht == NHT - 1))
                    cdn[hd2] = c_mm
                down_rd = max(down_rd, cdn[1])
                for hf2 in range(2):
                    c_t1 = p.emit("vector", lambda b=b, hf2=hf2: nc.vector.scalar_tensor_tensor(
                        YT[:, b * D + hf2 * 512: b * D + (hf2 + 1) * 512], DPS[b][hf2][:, 0:512], S2,
                        XT1[:, b * D + hf2 * 512: b * D + (hf2 + 1) * 512], ALU.mult, ALU.add),
                        waits=[("PE", cdn[hf2]), ("CX%d" % b, xld[b])])
                    yst[b] = p.emit("vector", lambda b=b, hf2=hf2: nc.vector.scalar_tensor_tensor(
                        YT[:, b * D + hf2 * 512: b * D + (hf2 + 1) * 512], WPS[b][hf2][:, 0:512], 0.0,
                        YT[:, b * D + hf2 * 512: b * D + (hf2 + 1) * 512], ALU.add, ALU.add),
                        waits=[("PE", cwo[hf2]), ("DV", c_t1)])
                ydone[tt] = p.dma("sync", dy[tt * 128:(tt + 1) * 128, :], YT[:, b * D:(b + 1) * D],
                                  waits=[("DV", yst[b])], ch="YS%d" % b)

        p.emit("vector", lambda: nc.vector.memset(SIL[0:1, 0:1], 0.0),
               waits=[("YS0", counts["YS0"]), ("YS1", counts["YS1"]),
                      ("CX0", counts["CX0"]), ("CX1", counts["CX1"]),
                      ("WD", counts["WD"]), ("WG0", counts["WG0"]), ("WG1", counts["WG1"])]
                     + [("TA%d" % i, counts["TA%d" % i]) for i in range(8)]
                     + [("TP%d" % i, counts["TP%d" % i]) for i in range(8)], inc=False)
        p.replay()
        scC.close()

    while ctx:
        ctx.pop().__exit__(None, None, None)
    return nc


def _host_layouts(W_kv_down, W_k_up, W_v_up, W_q, W_q_rope, W_k_rope, W_o, w_gate, w_up, w_down):
    def chunk_pd(w, inner):
        return np.ascontiguousarray(np.asarray(w, np.float32).reshape(NDC, 128, inner)
                                    .transpose(1, 0, 2).reshape(128, NDC * inner))

    wq = chunk_pd(W_q, D)
    wkv = chunk_pd(W_kv_down, DCKV)
    wqr = chunk_pd(W_q_rope, DR)
    wo = chunk_pd(W_o, D)
    wkup = np.ascontiguousarray(np.asarray(W_k_up, np.float32))
    wvup = np.ascontiguousarray(np.asarray(W_v_up, np.float32))
    wkr = np.ascontiguousarray(np.asarray(W_k_rope, np.float32))
    wg = np.ascontiguousarray(np.asarray(w_gate, np.int32).reshape(NHT, 128, NDC, 128)
                              .transpose(3, 0, 2, 1).reshape(128, NHT * D))
    wu = np.ascontiguousarray(np.asarray(w_up, np.int32).reshape(NHT, 128, NDC, 128)
                              .transpose(3, 0, 2, 1).reshape(128, NHT * D))
    wd = np.ascontiguousarray(np.asarray(w_down, np.int32).T.reshape(NHT, 128, D)
                              .transpose(1, 0, 2).reshape(128, NHT * D))
    return dict(wq=wq, wkv=wkv, wqr=wqr, wkup=wkup, wvup=wvup, wkr=wkr, wo=wo, wg=wg, wu=wu, wd=wd)


def _host_consts():
    inv_freq = 1.0 / 10000.0 ** (np.arange(0, DR, 2, dtype=np.float64) / DR)
    t = np.arange(T, dtype=np.float64)
    freqs = np.outer(t, inv_freq)
    cos = np.cos(freqs).astype(np.float32)
    sin = np.sin(freqs).astype(np.float32)

    def tile_tab(tab):
        return np.ascontiguousarray(tab.reshape(NTT, 128, 16).transpose(1, 0, 2).reshape(128, NTT * 16))

    mask = (np.arange(128)[None, :] >= np.arange(128)[:, None]).astype(ml_dtypes.bfloat16)
    ident = np.eye(128).astype(ml_dtypes.bfloat16)
    hmask = np.zeros((128, 2), ml_dtypes.bfloat16)
    hmask[0:64, 0] = 1
    hmask[64:128, 1] = 1
    oc2 = np.ones((128, 2), ml_dtypes.bfloat16)
    return dict(tcos=tile_tab(cos), tsin=tile_tab(sin),
                tcosq=tile_tab(cos * np.float32(SCALE)), tsinq=tile_tab(sin * np.float32(SCALE)),
                mask=mask, ident=ident, hmask=hmask, oc2=oc2)


def kernel(L0_attn_x, L0_attn_W_kv_down, L0_attn_W_k_up, L0_attn_W_v_up, L0_attn_W_q,
           L0_attn_W_q_rope, L0_attn_W_k_rope, L0_attn_W_o, L0_ffn_w_gate, L0_ffn_w_up,
           L0_ffn_w_down):
    x = np.ascontiguousarray(np.asarray(L0_attn_x, np.float32))
    weights = _host_layouts(L0_attn_W_kv_down, L0_attn_W_k_up, L0_attn_W_v_up, L0_attn_W_q,
                            L0_attn_W_q_rope, L0_attn_W_k_rope, L0_attn_W_o,
                            L0_ffn_w_gate, L0_ffn_w_up, L0_ffn_w_down)
    consts = _host_consts()
    nc = build(debug=False)
    in_maps = []
    for b in range(8):
        m = {"x": np.ascontiguousarray(x[b])}
        m.update(weights)
        m.update(consts)
        in_maps.append(m)
    res = run_bass_kernel_spmd(nc, in_maps, core_ids=list(range(8)))
    out = np.stack([np.asarray(res.results[b]["y"]) for b in range(8)], axis=0)
    return out.astype(np.float32)


# version 26
# speedup vs baseline: 1.0062x; 1.0062x over previous
import math
import numpy as np
import ml_dtypes
import concourse.bass as bass
import concourse.mybir as mybir
from concourse.bass_utils import run_bass_kernel_spmd

F32 = mybir.dt.float32
BF16 = mybir.dt.bfloat16
I32 = mybir.dt.int32
AF = mybir.ActivationFunctionType
ALU = mybir.AluOpType
AX = mybir.AxisListType

T = 2048
D = 1024
NDC = 8
DCKV = 128
DR = 32
NH = 16
HD = 64
HID = 3072
NHT = 24
NTT = 16
SCALE = 1.0 / math.sqrt(HD + DR)
S127 = 1.0 / 127.0
S2 = 1.0 / (127.0 * 127.0)
VSTRIDE = NH * 65


class Scope:
    def __init__(self):
        self.cms = []

    def al(self, cm):
        v = cm.__enter__()
        self.cms.append(cm)
        return v

    def close(self):
        while self.cms:
            self.cms.pop().__exit__(None, None, None)

ENGINES = ("tensor", "vector", "scalar", "gpsimd", "sync")
ENG2CTR = {"tensor": "PE", "vector": "DV", "scalar": "AC", "gpsimd": "GP", "sync": "SY"}


class Plan:

    def __init__(self, nc, sems, counts):
        self.nc = nc
        self.sems = sems
        self.counts = counts
        self.items = []

    def emit(self, eng, fn, waits=(), inc=True):
        own = ENG2CTR[eng]
        amt = 16 if eng in ("sync", "gpsimd") else 1
        ws = []
        for (n, v) in waits:
            if v is None or v <= 0:
                continue
            assert v <= self.counts[n], f"wait on future: {n} {v} > {self.counts[n]}"
            ws.append((n, v))
        if inc:
            self.counts[own] += amt
        self.items.append((eng, tuple(ws), fn, own if inc else None, amt))
        return self.counts[own]

    def dma(self, eng, out, in_, waits=(), ch=None, **kw):
        own = ch or ENG2CTR[eng]
        ws = []
        for (n, v) in waits:
            if v is None or v <= 0:
                continue
            assert v <= self.counts[n], f"wait on future: {n} {v} > {self.counts[n]}"
            ws.append((n, v))
        self.counts[own] += 16
        self.items.append((eng, tuple(ws), lambda: getattr(self.nc, eng).dma_start(out=out, in_=in_, **kw),
                           own, 16))
        return self.counts[own]

    def dmat(self, out, in_, waits=(), ch=None):
        own = ch or "SY"
        ws = []
        for (n, v) in waits:
            if v is None or v <= 0:
                continue
            assert v <= self.counts[n], f"wait on future: {n} {v} > {self.counts[n]}"
            ws.append((n, v))
        self.counts[own] += 16
        self.items.append(("sync", tuple(ws), lambda: self.nc.sync.dma_start_transpose(out, in_), own, 16))
        return self.counts[own]

    def replay(self):
        nc = self.nc
        with nc.Block() as blk:
            for eng_name in ENGINES:
                self._section(blk, eng_name)

    def _section(self, blk, eng_name):
        items = self.items
        sems = self.sems

        def body(eng):
            hi = {}
            for (e, ws, fn, incn, amt) in items:
                if e != eng_name:
                    continue
                for (n, v) in ws:
                    if hi.get(n, 0) < v:
                        eng.wait_ge(sems[n], v)
                        hi[n] = v
                inst = fn()
                if incn:
                    inst.then_inc(sems[incn], amt)

        getattr(blk, eng_name)(body)


def build(debug=False):
    nc = bass.Bass("TRN2", target_bir_lowering=False)

    dx = nc.dram_tensor("x", [T, D], F32, kind="ExternalInput").ap()
    dwq = nc.dram_tensor("wq", [128, NDC * D], F32, kind="ExternalInput").ap()
    dwkv = nc.dram_tensor("wkv", [128, NDC * DCKV], F32, kind="ExternalInput").ap()
    dwqr = nc.dram_tensor("wqr", [128, NDC * DR], F32, kind="ExternalInput").ap()
    dwkup = nc.dram_tensor("wkup", [128, D], F32, kind="ExternalInput").ap()
    dwvup = nc.dram_tensor("wvup", [128, D], F32, kind="ExternalInput").ap()
    dwkr = nc.dram_tensor("wkr", [128, DR], F32, kind="ExternalInput").ap()
    dwo = nc.dram_tensor("wo", [128, NDC * D], F32, kind="ExternalInput").ap()
    dwg = nc.dram_tensor("wg", [128, NHT * D], I32, kind="ExternalInput").ap()
    dwu = nc.dram_tensor("wu", [128, NHT * D], I32, kind="ExternalInput").ap()
    dwd = nc.dram_tensor("wd", [128, NHT * D], I32, kind="ExternalInput").ap()
    dcos = nc.dram_tensor("tcos", [128, NTT * 16], F32, kind="ExternalInput").ap()
    dsin = nc.dram_tensor("tsin", [128, NTT * 16], F32, kind="ExternalInput").ap()
    dcosq = nc.dram_tensor("tcosq", [128, NTT * 16], F32, kind="ExternalInput").ap()
    dsinq = nc.dram_tensor("tsinq", [128, NTT * 16], F32, kind="ExternalInput").ap()
    dmask = nc.dram_tensor("mask", [128, 128], BF16, kind="ExternalInput").ap()
    dident = nc.dram_tensor("ident", [128, 128], BF16, kind="ExternalInput").ap()
    dhmask = nc.dram_tensor("hmask", [128, 2], BF16, kind="ExternalInput").ap()
    doc2 = nc.dram_tensor("oc2", [128, 2], BF16, kind="ExternalInput").ap()

    dy = nc.dram_tensor("y", [T, D], F32, kind="ExternalOutput").ap()
    xnk = "ExternalOutput" if debug else "Internal"
    dxn = nc.dram_tensor("xn_d", [T, D], BF16, kind=xnk).ap()
    datt = nc.dram_tensor("att_d", [T, D], BF16, kind=xnk).ap()
    if debug:
        dqf = nc.dram_tensor("qf_d", [128, NDC * T], BF16, kind="ExternalOutput").ap()
        dkf = nc.dram_tensor("kf_d", [128, NDC * T], BF16, kind="ExternalOutput").ap()
        dck = nc.dram_tensor("ck_d", [128, T], BF16, kind="ExternalOutput").ap()
        dva = nc.dram_tensor("va_d", [128, NTT * VSTRIDE], BF16, kind="ExternalOutput").ap()
        dqr = nc.dram_tensor("qr_d", [32, T], BF16, kind="ExternalOutput").ap()
        dkr = nc.dram_tensor("kr_d", [32, T], BF16, kind="ExternalOutput").ap()
        dnm = nc.dram_tensor("nm_d", [128, 16], F32, kind="ExternalOutput").ap()
        dha = nc.dram_tensor("ha_d", [128, NHT * 1024], BF16, kind="ExternalOutput").ap()

    CHANNELS = tuple(["PE", "DV", "AC", "GP", "SY", "XL0", "XL1", "XS0", "XS1", "CONST",
                     "WL", "NM", "RR", "AT0", "AT1", "WD", "WG0", "WG1",
                     "CX0", "CX1", "YS0", "YS1"]
                    + ["TP%d" % i for i in range(8)] + ["TA%d" % i for i in range(8)])
    counts = {c: 0 for c in CHANNELS}
    ctx = []

    def alloc(cm):
        v = cm.__enter__()
        ctx.append(cm)
        return v

    sems = {}
    for name in CHANNELS:
        sems[name] = alloc(nc.semaphore("s" + name))

    XN = alloc(nc.sbuf_tensor("XN", [128, NDC * T], BF16))
    WO2 = alloc(nc.sbuf_tensor("WO2", [128, NDC * D], BF16))
    MSK = alloc(nc.sbuf_tensor("MSK", [128, 128], BF16))
    IDN = alloc(nc.sbuf_tensor("IDN", [128, 128], BF16))
    HMK = alloc(nc.sbuf_tensor("HMK", [128, 2], BF16))
    OC2 = alloc(nc.sbuf_tensor("OC2", [128, 2], BF16))
    NEGM = alloc(nc.sbuf_tensor("NEGM", [128, 16], F32))

    scA = Scope()
    QF = scA.al(nc.sbuf_tensor("QF", [128, NDC * T], BF16))
    KF = scA.al(nc.sbuf_tensor("KF", [128, NDC * T], BF16))
    CKV = scA.al(nc.sbuf_tensor("CKV", [128, T], BF16))
    VA = scA.al(nc.sbuf_tensor("VA", [128, NTT * VSTRIDE], BF16))
    QRF = scA.al(nc.sbuf_tensor("QRF", [32, T], BF16))
    KRF = scA.al(nc.sbuf_tensor("KRF", [32, T], BF16))
    CT = scA.al(nc.sbuf_tensor("CT", [128, NTT * 16], F32))
    ST = scA.al(nc.sbuf_tensor("ST", [128, NTT * 16], F32))
    CQ = scA.al(nc.sbuf_tensor("CQ", [128, NTT * 16], F32))
    SQ = scA.al(nc.sbuf_tensor("SQ", [128, NTT * 16], F32))
    if True:
        sc1 = Scope()
        WQ = sc1.al(nc.sbuf_tensor("WQ", [128, NDC * D], BF16))
        WKV = sc1.al(nc.sbuf_tensor("WKV", [128, NDC * DCKV], BF16))
        WQR = sc1.al(nc.sbuf_tensor("WQR", [128, NDC * DR], BF16))
        WKUP = sc1.al(nc.sbuf_tensor("WKUP", [128, D], BF16))
        WVUP = sc1.al(nc.sbuf_tensor("WVUP", [128, D], BF16))
        WKR = sc1.al(nc.sbuf_tensor("WKR", [128, DR], BF16))
        LNX = sc1.al(nc.sbuf_tensor("LNX", [128, 2 * D], F32))
        XNT = sc1.al(nc.sbuf_tensor("XNT", [128, 2 * D], BF16))
        BN6 = sc1.al(nc.sbuf_tensor("BN6", [128, 12], F32))
        MV = sc1.al(nc.sbuf_tensor("MV", [128, 2], F32))
        SMT = sc1.al(nc.sbuf_tensor("SMT", [128, 4], F32))
        RSC = sc1.al(nc.sbuf_tensor("RSC", [128, 128], F32))
        QRT = sc1.al(nc.sbuf_tensor("QRT", [128, NTT * DR], BF16))
        KRT = sc1.al(nc.sbuf_tensor("KRT", [128, NTT * DR], BF16))
        SQS = sc1.al(nc.sbuf_tensor("SQS", [128, T], BF16))
        NMX = sc1.al(nc.sbuf_tensor("NMX", [2, 32], F32))
        NSC = sc1.al(nc.sbuf_tensor("NSC", [2, 32], F32))
        PA = sc1.al(nc.psum_tensor("PA", [128, 512], F32))
        PB = sc1.al(nc.psum_tensor("PB", [128, 512], F32))
        PVh = sc1.al(nc.psum_tensor("PVh", [128, 512], F32))
        PR = sc1.al(nc.psum_tensor("PR", [128, 512], F32))
        PT = sc1.al(nc.psum_tensor("PT", [128, 512], BF16))
        PN = sc1.al(nc.psum_tensor("PN", [128, 512], F32))
        PJ = sc1.al(nc.psum_tensor("PJ", [128, 512], F32))
        if True:
            p = Plan(nc, sems, counts)

            p.dma("sync", CT[:], dcos, ch="CONST")
            p.dma("sync", ST[:], dsin, ch="CONST")
            p.dma("sync", CQ[:], dcosq, ch="CONST")
            p.dma("sync", SQ[:], dsinq, ch="CONST")
            p.dma("sync", MSK[:], dmask, ch="CONST")
            p.dma("sync", IDN[:], dident, ch="CONST")
            p.dma("sync", HMK[:], dhmask, ch="CONST")
            c_const = p.dma("sync", OC2[:], doc2, ch="CONST")
            c_tabs = ("CONST", c_const)
            p.dma("gpsimd", WQ[:], dwq, ch="WL")
            p.dma("gpsimd", WKV[:], dwkv, ch="WL")
            p.dma("gpsimd", WQR[:], dwqr, ch="WL")
            p.dma("gpsimd", WKUP[:], dwkup, ch="WL")
            p.dma("gpsimd", WVUP[:], dwvup, ch="WL")
            p.dma("gpsimd", WKR[:], dwkr, ch="WL")
            c_wl = p.dma("gpsimd", WO2[:], dwo, ch="WL")
            c_wo2 = c_wl

            def filler(waits=()):
                p.emit("tensor", lambda: nc.tensor.matmul(
                    PJ[:, 0:128], IDN[:], IDN[:], start=True, stop=True), waits=waits, inc=False)

            for i in range(30):
                filler([("CONST", c_const)] if i == 0 else ())

            c_vaones = p.emit("vector", lambda: nc.vector.memset(VA[:], 1.0))

            ln_xn = [0] * NTT
            ln_store = [0] * NTT
            ln_exp = [0] * NTT
            c_aggr_prev = 0
            for tt in range(NTT):
                b = tt % 2
                w = [("DV", ln_xn[tt - 2])] if tt >= 2 else []
                c_ld = p.dma("sync", LNX[:, b * D:(b + 1) * D], dx[tt * 128:(tt + 1) * 128, :],
                             waits=w, ch="XL%d" % b)
                p.emit("vector", lambda b=b: nc.vector.bn_stats(BN6[:, 0:6], LNX[:, b * D: b * D + 512]),
                       waits=[("XL%d" % b, c_ld), ("DV", c_aggr_prev)])
                c_bn2 = p.emit("vector", lambda b=b: nc.vector.bn_stats(BN6[:, 6:12], LNX[:, b * D + 512: b * D + 1024]))
                c_aggr_prev = p.emit("vector", lambda: nc.vector.bn_aggr(MV[:, 0:2], BN6[:, 0:12]),
                                     waits=[("DV", c_bn2)])
                wv = [("DV", c_aggr_prev)]
                if tt >= 1:
                    wv.append(("AC", ln_exp[tt - 1]))
                c_v = p.emit("vector", lambda: nc.vector.tensor_scalar(SMT[:, 0:1], MV[:, 1:2], 1e-5, None, ALU.add),
                             waits=wv)
                c_ln = p.emit("scalar", lambda: nc.scalar.activation(SMT[:, 1:2], SMT[:, 0:1], AF.Ln),
                              waits=[("DV", c_v)])
                ln_exp[tt] = p.emit("scalar", lambda: nc.scalar.activation(
                    SMT[:, 2:3], SMT[:, 1:2], AF.Exp, scale=-0.5), waits=[("AC", c_ln)])
                c_stt = p.emit("vector", lambda: nc.vector.scalar_tensor_tensor(
                    SMT[:, 3:4], SMT[:, 2:3], -1.0, MV[:, 0:1], ALU.mult, ALU.mult), waits=[("AC", ln_exp[tt])])
                wst = [("DV", c_stt)]
                if tt >= 2:
                    wst.append(("XS%d" % b, ln_store[tt - 2]))
                ln_xn[tt] = p.emit("vector", lambda b=b: nc.vector.tensor_scalar(
                    XNT[:, b * D:(b + 1) * D], LNX[:, b * D:(b + 1) * D], SMT[:, 2:3], SMT[:, 3:4],
                    ALU.mult, ALU.add), waits=wst)
                ln_store[tt] = p.dma("sync", dxn[tt * 128:(tt + 1) * 128, :], XNT[:, b * D:(b + 1) * D],
                                     waits=[("DV", ln_xn[tt])], ch="XS%d" % b)
                filler([("XL%d" % b, c_ld)])
                for _ in range(4):
                    filler()

            c_tp = 0
            for dc in range(NDC):
                w = [("XS0", counts["XS0"]), ("XS1", counts["XS1"]), ("XL0", counts["XL0"]),
                     ("XL1", counts["XL1"]), ("CONST", counts["CONST"]), ("WL", c_wl)] if dc == 0 \
                    else [("TP%d" % (dc - 1), 16)]
                c_tp = p.dmat(XN[:, dc * T:(dc + 1) * T], dxn[:, dc * 128:(dc + 1) * 128],
                              waits=w, ch="TP%d" % dc)
                filler([("TP%d" % dc, 16)])
                for _ in range(12):
                    filler()
            c_xn_ready = [("TP%d" % i, 16) for i in range(8)]

            def proj_group(psum_ap, lhsT_list, rhs_list, first_waits):
                last = 0
                n = len(lhsT_list)
                for i, (lh, rh) in enumerate(zip(lhsT_list, rhs_list)):
                    w = first_waits if i == 0 else []
                    last = p.emit("tensor", lambda lh=lh, rh=rh, i=i, n=n, ps=psum_ap: nc.tensor.matmul(
                        ps, lh, rh, start=(i == 0), stop=(i == n - 1)), waits=w, inc=(i == n - 1))
                return last

            psel = [PA, PB]
            evac = [0, 0]
            for qb in range(4):
                i = qb % 2
                w = c_xn_ready + [("WL", c_wl), ("AC", evac[i])]
                c_mm = proj_group(psel[i][:, 0:512], [WKV[:, dc * 128:(dc + 1) * 128] for dc in range(NDC)],
                                  [XN[:, dc * T + qb * 512: dc * T + (qb + 1) * 512] for dc in range(NDC)], w)
                evac[i] = p.emit("scalar", lambda qb=qb, i=i: nc.scalar.activation(
                    CKV[:, qb * 512:(qb + 1) * 512], psel[i][:, 0:512], AF.Copy), waits=[("PE", c_mm)])
            c_ckv_ready = max(evac)

            for nt in range(NDC):
                for qb in range(4):
                    i = (nt * 4 + qb) % 2
                    c_mm = proj_group(psel[i][:, 0:512],
                                      [WQ[:, dc * D + nt * 128: dc * D + (nt + 1) * 128] for dc in range(NDC)],
                                      [XN[:, dc * T + qb * 512: dc * T + (qb + 1) * 512] for dc in range(NDC)],
                                      [("AC", evac[i])])
                    evac[i] = p.emit("scalar", lambda nt=nt, qb=qb, i=i: nc.scalar.activation(
                        QF[:, nt * T + qb * 512: nt * T + (qb + 1) * 512], psel[i][:, 0:512], AF.Copy,
                        scale=SCALE), waits=[("PE", c_mm)])

            for nt in range(NDC):
                for qb in range(4):
                    i = (nt * 4 + qb) % 2
                    c_mm = proj_group(psel[i][:, 0:512], [WKUP[:, nt * 128:(nt + 1) * 128]],
                                      [CKV[:, qb * 512:(qb + 1) * 512]],
                                      [("AC", max(evac[i], c_ckv_ready))])
                    evac[i] = p.emit("scalar", lambda nt=nt, qb=qb, i=i: nc.scalar.activation(
                        KF[:, nt * T + qb * 512: nt * T + (qb + 1) * 512], psel[i][:, 0:512], AF.Copy),
                        waits=[("PE", c_mm)])

            va_v = VA[:].rearrange("p (tt h j) -> p tt h j", tt=NTT, h=NH)
            c_v_dv = 0
            for tt in range(NTT):
                w0 = [("AC", c_ckv_ready), ("DV", c_vaones)]
                if tt >= 1:
                    w0.append(("DV", c_v_dv))
                c_mm1 = p.emit("tensor", lambda tt=tt: nc.tensor.matmul(
                    PVh[:, 0:512], CKV[:, tt * 128:(tt + 1) * 128], WVUP[:, 0:512],
                    start=True, stop=True), waits=w0)
                c_mm2 = p.emit("tensor", lambda tt=tt: nc.tensor.matmul(
                    PR[:, 0:512], CKV[:, tt * 128:(tt + 1) * 128], WVUP[:, 512:1024],
                    start=True, stop=True))
                wv1 = [("PE", c_mm1)]
                if tt == 0:
                    wv1.append(("DV", c_vaones))
                p.emit("vector", lambda tt=tt: nc.vector.tensor_copy(
                    va_v[:, tt, 0:8, 0:64], PVh[:, 0:512]), waits=wv1)
                c_v_dv = p.emit("vector", lambda tt=tt: nc.vector.tensor_copy(
                    va_v[:, tt, 8:16, 0:64], PR[:, 0:512]), waits=[("PE", c_mm2)])

            def rope_block(tt, psum, rb, ctab, stab, dstT, w_first):
                x1 = psum[:, 0:16]
                x2 = psum[:, 16:32]
                c = ctab[:, tt * 16:(tt + 1) * 16]
                s = stab[:, tt * 16:(tt + 1) * 16]
                o = rb * 64
                p.emit("vector", lambda: nc.vector.tensor_tensor(RSC[:, o:o + 16], x1, c, ALU.mult),
                       waits=w_first)
                c_b = p.emit("vector", lambda: nc.vector.tensor_tensor(RSC[:, o + 16:o + 32], x2, s, ALU.mult))
                p.emit("vector", lambda dstT=dstT, tt=tt: nc.vector.tensor_tensor(
                    dstT[:, tt * 32: tt * 32 + 16], RSC[:, o:o + 16], RSC[:, o + 16:o + 32], ALU.subtract),
                    waits=[("DV", c_b)])
                p.emit("vector", lambda: nc.vector.tensor_tensor(RSC[:, o + 32:o + 48], x2, c, ALU.mult))
                c_d = p.emit("vector", lambda: nc.vector.tensor_tensor(RSC[:, o + 48:o + 64], x1, s, ALU.mult))
                return p.emit("vector", lambda dstT=dstT, tt=tt: nc.vector.tensor_tensor(
                    dstT[:, tt * 32 + 16: tt * 32 + 32], RSC[:, o + 32:o + 48], RSC[:, o + 48:o + 64],
                    ALU.add), waits=[("DV", c_d)])

            c_rq = [0] * (NTT + 1)
            c_rk = [0] * (NTT + 1)
            c_rf_dv = 0
            QPS = [PR, PA]
            KPS = [PVh, PB]
            for tt in range(NTT):
                qp = QPS[tt % 2]
                kp = KPS[tt % 2]
                wq = [("DV", c_rq[tt - 2])] if tt >= 2 else (
                    [c_tabs, ("DV", c_v_dv), ("AC", counts["AC"])])
                c_mmq = proj_group(qp[:, 0:32],
                                   [XN[:, dc * T + tt * 128: dc * T + (tt + 1) * 128] for dc in range(NDC)],
                                   [WQR[:, dc * DR:(dc + 1) * DR] for dc in range(NDC)], wq)
                wk = [("DV", c_rk[tt - 2])] if tt >= 2 else [("DV", c_v_dv), ("AC", counts["AC"])]
                c_mmk = proj_group(kp[:, 0:32], [CKV[:, tt * 128:(tt + 1) * 128]], [WKR[:, 0:DR]], wk)
                if tt >= 1:
                    c_t1 = p.emit("tensor", lambda tt=tt: nc.tensor.transpose(
                        PT[0:32, 0:128], QRT[:, (tt - 1) * 32: tt * 32], IDN[:]),
                        waits=[("DV", c_rq[tt - 1]), ("DV", c_rf_dv)])
                    c_t2 = p.emit("tensor", lambda tt=tt: nc.tensor.transpose(
                        PT[0:32, 128:256], KRT[:, (tt - 1) * 32: tt * 32], IDN[:]),
                        waits=[("DV", c_rk[tt - 1])])
                c_rq[tt] = rope_block(tt, qp, 0, CQ, SQ, QRT, [("PE", c_mmq), ("DV", c_rq[tt - 1])])
                c_rk[tt] = rope_block(tt, kp, 1, CT, ST, KRT, [("PE", c_mmk), ("DV", c_rk[tt - 1])])
                if tt >= 1:
                    p.emit("vector", lambda tt=tt: nc.vector.tensor_copy(
                        QRF[:, (tt - 1) * 128: tt * 128], PT[0:32, 0:128]), waits=[("PE", c_t1)])
                    c_rf_dv = p.emit("vector", lambda tt=tt: nc.vector.tensor_copy(
                        KRF[:, (tt - 1) * 128: tt * 128], PT[0:32, 128:256]), waits=[("PE", c_t2)])
            c_t1 = p.emit("tensor", lambda: nc.tensor.transpose(
                PT[0:32, 0:128], QRT[:, (NTT - 1) * 32: NTT * 32], IDN[:]),
                waits=[("DV", c_rq[NTT - 1]), ("DV", c_rf_dv)])
            c_t2 = p.emit("tensor", lambda: nc.tensor.transpose(
                PT[0:32, 128:256], KRT[:, (NTT - 1) * 32: NTT * 32], IDN[:]),
                waits=[("DV", c_rk[NTT - 1])])
            p.emit("vector", lambda: nc.vector.tensor_copy(
                QRF[:, (NTT - 1) * 128: NTT * 128], PT[0:32, 0:128]), waits=[("PE", c_t1)])
            c_rf_dv = p.emit("vector", lambda: nc.vector.tensor_copy(
                KRF[:, (NTT - 1) * 128: NTT * 128], PT[0:32, 128:256]), waits=[("PE", c_t2)])

            c_sqs_rd_pe = 0

            def norm_max(SRC, dst, w_src_dv):
                nonlocal c_sqs_rd_pe
                cdv = counts["DV"]
                for dc in range(NDC):
                    csq = p.emit("vector", lambda dc=dc, SRC=SRC: nc.vector.tensor_tensor(
                        SQS[:, 0:T], SRC[:, dc * T:(dc + 1) * T], SRC[:, dc * T:(dc + 1) * T], ALU.mult),
                        waits=[("PE", c_sqs_rd_pe), ("AC", w_src_dv), ("DV", cdv)])
                    for j in range(4):
                        c_mm = p.emit("tensor", lambda j=j: nc.tensor.matmul(
                            PN[0:2, 0:512], HMK[:, 0:2], SQS[:, j * 512:(j + 1) * 512], start=True, stop=True),
                            waits=[("DV", csq)] if j == 0 else [("DV", cdv)])
                        c_sqs_rd_pe = c_mm
                        cdv = p.emit("vector", lambda dc=dc, j=j: nc.vector.tensor_reduce(
                            NMX[0:2, dc * 4 + j: dc * 4 + j + 1], PN[0:2, 0:512], AX.X, ALU.max),
                            waits=[("PE", c_mm)])
                nmx_v = NMX[0:2, 0:32].rearrange("p (dc j) -> p dc j", dc=8)
                return p.emit("vector", lambda nmx_v=nmx_v, dst=dst: nc.vector.tensor_reduce(
                    dst, nmx_v, AX.X, ALU.max), waits=[("DV", cdv)])

            c_qn = norm_max(QF, NSC[0:2, 0:8], counts["AC"])
            c_kn = norm_max(KF, NSC[0:2, 8:16], counts["AC"])

            def rope_norm_max(SRC, dst, w_src_dv):
                nonlocal c_sqs_rd_pe
                cdv = counts["DV"]
                csq = p.emit("vector", lambda SRC=SRC: nc.vector.tensor_tensor(
                    SQS[0:32, 0:T], SRC[0:32, 0:T], SRC[0:32, 0:T], ALU.mult),
                    waits=[("PE", c_sqs_rd_pe), ("DV", max(w_src_dv, cdv))])
                for j in range(4):
                    c_mm = p.emit("tensor", lambda j=j: nc.tensor.matmul(
                        PN[0:2, 0:512], OC2[0:32, 0:2], SQS[0:32, j * 512:(j + 1) * 512], start=True, stop=True),
                        waits=[("DV", csq), ("CONST", c_const)] if j == 0 else [("DV", cdv)])
                    c_sqs_rd_pe = c_mm
                    cdv = p.emit("vector", lambda j=j: nc.vector.tensor_reduce(
                        NMX[0:2, j:j + 1], PN[0:2, 0:512], AX.X, ALU.max), waits=[("PE", c_mm)])
                return p.emit("vector", lambda dst=dst: nc.vector.tensor_reduce(
                    dst, NMX[0:2, 0:4], AX.X, ALU.max), waits=[("DV", cdv)])

            c_qr2 = rope_norm_max(QRF, NSC[0:2, 16:17], c_rf_dv)
            c_kr2 = rope_norm_max(KRF, NSC[0:2, 17:18], c_rf_dv)

            c1 = p.emit("vector", lambda: nc.vector.tensor_tensor(
                NSC[0:2, 18:26], NSC[0:2, 0:8], NSC[0:2, 8:16], ALU.mult), waits=[("DV", max(c_qn, c_kn))])
            c2 = p.emit("vector", lambda: nc.vector.tensor_tensor(
                NSC[0:2, 26:27], NSC[0:2, 16:17], NSC[0:2, 17:18], ALU.mult),
                waits=[("DV", max(c_qr2, c_kr2))])
            c3 = p.emit("scalar", lambda: nc.scalar.activation(NSC[0:2, 18:26], NSC[0:2, 18:26], AF.Sqrt),
                        waits=[("DV", c2)])
            c4 = p.emit("scalar", lambda: nc.scalar.activation(NSC[0:2, 26:27], NSC[0:2, 26:27], AF.Sqrt))
            c5 = p.emit("vector", lambda: nc.vector.tensor_scalar(
                NSC[0:2, 18:26], NSC[0:2, 18:26], NSC[0:2, 26:27], -1.0, ALU.add, ALU.mult),
                waits=[("AC", c4)])
            p.dma("gpsimd", NEGM[32:33, 0:8], NSC[0:1, 18:26], waits=[("DV", c5)], ch="NM")
            p.dma("gpsimd", NEGM[32:33, 8:16], NSC[1:2, 18:26], ch="NM")
            p.dma("gpsimd", NEGM[96:97, 0:8], NSC[0:1, 18:26], ch="NM")
            c_negm = p.dma("gpsimd", NEGM[96:97, 8:16], NSC[1:2, 18:26], ch="NM")

            if debug:
                wdbg = [("DV", counts["DV"]), ("AC", counts["AC"]), ("NM", counts["NM"])]
                p.dma("sync", dqf, QF[:], waits=wdbg, ch="CONST")
                p.dma("sync", dkf, KF[:], ch="CONST")
                p.dma("sync", dck, CKV[:], ch="CONST")
                p.dma("sync", dva, VA[:], ch="CONST")
                p.dma("sync", dqr, QRF[:], ch="CONST")
                p.dma("sync", dkr, KRF[:], ch="CONST")
                p.dma("sync", dnm, NEGM[:], ch="CONST")

            p.replay()
        sc1.close()

        c_a1_dv = counts["DV"]
        c_a1_ac = counts["AC"]
        c_a1_pe = counts["PE"]
        c_negm = counts["NM"]

        sc2 = Scope()
        QC = sc2.al(nc.sbuf_tensor("QC", [128, 2 * T], BF16))
        KC = sc2.al(nc.sbuf_tensor("KC", [128, 2 * T], BF16))
        ET = sc2.al(nc.sbuf_tensor("ET", [128, 5 * 512], BF16))
        ATS = sc2.al(nc.sbuf_tensor("ATS", [128, 2 * 1024], BF16))
        RCP = sc2.al(nc.sbuf_tensor("RCP", [128, 16], F32))
        STR0 = sc2.al(nc.psum_tensor("STR0", [128, 512], F32))
        STR1 = sc2.al(nc.psum_tensor("STR1", [128, 512], F32))
        STR2 = sc2.al(nc.psum_tensor("STR2", [128, 512], F32))
        STR3 = sc2.al(nc.psum_tensor("STR3", [128, 512], F32))
        STR4 = sc2.al(nc.psum_tensor("STR4", [128, 512], F32))
        AVB0 = sc2.al(nc.psum_tensor("AVB0", [128, 512], F32))
        AVB1 = sc2.al(nc.psum_tensor("AVB1", [128, 512], F32))
        AVB2 = sc2.al(nc.psum_tensor("AVB2", [128, 512], F32))
        if True:
            STRIPS = [STR0, STR1, STR2, STR3, STR4]
            AVB = [AVB0, AVB1, AVB2]
            p = Plan(nc, sems, counts)

            cz1 = p.emit("vector", lambda: nc.vector.memset(QC[:], 0.0),
                         waits=[("DV", c_a1_dv), ("AC", c_a1_ac), ("PE", c_a1_pe)])
            cz2 = p.emit("vector", lambda: nc.vector.memset(KC[:], 0.0))
            p.emit("vector", lambda: nc.vector.memset(KC[96:97, 0:T], 1.0), waits=[("DV", cz2)])
            p.emit("vector", lambda: nc.vector.memset(KC[32:33, T:2 * T], 1.0))
            p.dma("gpsimd", QC[64:96, 0:T], QRF[0:32, 0:T], waits=[("DV", cz1)], ch="RR")
            c_gp_rope = p.dma("gpsimd", KC[64:96, 0:T], KRF[0:32, 0:T], waits=[("DV", cz2)], ch="RR")
            p.emit("vector", lambda: nc.vector.tensor_copy(QC[0:32, T:2 * T], QRF[0:32, 0:T]),
                   waits=[("DV", cz2)])
            c_setup_dv = p.emit("vector", lambda: nc.vector.tensor_copy(KC[0:32, T:2 * T], KRF[0:32, 0:T]))

            slot_exp = [0] * 5
            slot_avdone = [0] * 5
            qk_done = [0] * (NH + 1)
            head_evac_dv = 0
            att_dma = [0, 0]
            datt_v = datt.rearrange("(qt p) n -> p qt n", p=128)
            AVLAG = 3

            def emit_assembly(h):
                s = h % 2
                dc = h // 2
                col = s * 8 + dc
                rows = slice(64 * s, 64 * s + 64)
                mrow = 96 if s == 0 else 32
                base = s * T
                w_cp = [("PE", qk_done[h - 2] if h >= 2 else 0), ("RR", c_gp_rope), ("DV", c_setup_dv)]
                p.emit("vector", lambda rows=rows, base=base, dc=dc: nc.vector.tensor_copy(
                    QC[rows, base: base + T], QF[rows, dc * T:(dc + 1) * T]), waits=w_cp)
                p.emit("vector", lambda rows=rows, base=base, dc=dc: nc.vector.tensor_copy(
                    KC[rows, base: base + T], KF[rows, dc * T:(dc + 1) * T]))
                return p.emit("vector", lambda mrow=mrow, base=base, col=col: nc.vector.tensor_scalar(
                    QC[mrow:mrow + 1, base: base + T], KC[mrow:mrow + 1, base: base + T],
                    NEGM[mrow:mrow + 1, col:col + 1], None, ALU.mult),
                    waits=[("NM", c_negm), ("DV", c_setup_dv)])

            c_asm_arr = [0] * NH
            c_asm_arr[0] = emit_assembly(0)
            evac_bank_prev = [0, 0, 0]

            for h in range(NH):
                s = h % 2
                base = s * T
                av_started = [False, False, False]
                pend = []

                def flush_av(info):
                    slot, kt, c_exp, c_mask, qts, first, ebase = info
                    for idx, qt in enumerate(qts):
                        bank = qt // 7
                        off7 = (qt % 7) * 66
                        eoff = ebase + qt * 128
                        wav = []
                        if idx == 0:
                            wav = [("AC", c_exp)]
                            if c_mask:
                                wav.append(("DV", c_mask))
                        st = not av_started[bank]
                        av_started[bank] = True
                        if st:
                            wav.append(("DV", evac_bank_prev[bank]))
                        is_last = (idx == len(qts) - 1)
                        c_av = p.emit("tensor",
                                      lambda bank=bank, off7=off7, eoff=eoff, st=st, kt=kt, h=h, qt=qt:
                                      nc.tensor.matmul(
                                          AVB[bank][:, off7: off7 + 65], ET[:, eoff: eoff + 128],
                                          VA[:, kt * VSTRIDE + h * 65: kt * VSTRIDE + (h + 1) * 65],
                                          start=st, stop=(qt == kt), skip_group_check=True),
                                      waits=wav, inc=is_last)
                        if is_last:
                            slot_avdone[slot] = c_av

                chunks = [(kt, jb) for kt in range(NTT) for jb in range(kt // 4, 4)]
                for ci, (kt, jb) in enumerate(chunks):
                    slot = ci % 5
                    wqk = [("DV", c_asm_arr[h]), ("AC", slot_exp[slot])]
                    c_qk = p.emit("tensor", lambda slot=slot, kt=kt, jb=jb, base=base: nc.tensor.matmul(
                        STRIPS[slot][:, 0:512], KC[:, base + kt * 128: base + (kt + 1) * 128],
                        QC[:, base + jb * 512: base + (jb + 1) * 512], start=True, stop=True), waits=wqk)
                    c_exp = p.emit("scalar", lambda slot=slot: nc.scalar.activation(
                        ET[:, slot * 512:(slot + 1) * 512], STRIPS[slot][:, 0:512], AF.Exp),
                        waits=[("PE", max(c_qk, slot_avdone[slot]))])
                    slot_exp[slot] = c_exp
                    c_mask = 0
                    if jb == kt // 4:
                        off = slot * 512 + kt * 128 - jb * 512
                        c_mask = p.emit("vector", lambda off=off: nc.vector.tensor_tensor(
                            ET[:, off: off + 128], ET[:, off: off + 128], MSK[:], ALU.mult),
                            waits=[("AC", c_exp)])
                    qts = [qt for qt in range(jb * 4, jb * 4 + 4) if qt >= kt]
                    pend.append((slot, kt, c_exp, c_mask, qts, (ci == 0), slot * 512 - jb * 512))
                    if ci == 1 and h + 1 < NH:
                        c_asm_arr[h + 1] = emit_assembly(h + 1)
                    if len(pend) > AVLAG:
                        flush_av(pend.pop(0))
                while pend:
                    flush_av(pend.pop(0))
                qk_done[h] = counts["PE"]
                c_av_all = counts["PE"]

                c_ats_dv = 0
                for qt in range(NTT):
                    bank = qt // 7
                    off7 = (qt % 7) * 66
                    w = [("PE", c_av_all)]
                    if qt == 0:
                        w.append(("DV", head_evac_dv))
                        if h >= 2:
                            w.append(("AT%d" % s, att_dma[s]))
                    c_rc = p.emit("vector", lambda bank=bank, off7=off7, qt=qt: nc.vector.reciprocal(
                        RCP[:, qt:qt + 1], AVB[bank][:, off7 + 64: off7 + 65]), waits=w)
                    c_ats_dv = p.emit("vector", lambda bank=bank, off7=off7, qt=qt, s=s: nc.vector.tensor_scalar(
                        ATS[:, s * 1024 + qt * 64: s * 1024 + (qt + 1) * 64],
                        AVB[bank][:, off7: off7 + 64], RCP[:, qt:qt + 1], None, ALU.mult),
                        waits=[("DV", c_rc)])
                    if qt in (6, 13, 15):
                        evac_bank_prev[qt // 7] = c_ats_dv
                head_evac_dv = c_ats_dv
                ats_v = ATS[:, s * 1024:(s + 1) * 1024].rearrange("p (qt c) -> p qt c", qt=NTT)
                att_dma[s] = p.dma("sync", datt_v[:, :, h * 64:(h + 1) * 64], ats_v,
                                   waits=[("DV", c_ats_dv)], ch="AT%d" % s)

            p.replay()
            sc2.close()
    scA.close()

    c_att_sy = counts["SY"]
    c_a2_dv = counts["DV"]
    c_a2_pe = counts["PE"]
    c_a2_ac = counts["AC"]
    c_a2_gp = counts["GP"]

    scC = Scope()
    AFM = scC.al(nc.sbuf_tensor("AFM", [128, NDC * T], BF16))
    XT1 = scC.al(nc.sbuf_tensor("XT1", [128, 2 * D], F32))
    YT = scC.al(nc.sbuf_tensor("YT", [128, 2 * D], F32))
    WGT = scC.al(nc.sbuf_tensor("WGT", [128, 2 * D], BF16))
    WUT = scC.al(nc.sbuf_tensor("WUT", [128, 2 * D], BF16))
    WDT = scC.al(nc.sbuf_tensor("WDT", [128, NHT * D], BF16))
    HA = scC.al(nc.sbuf_tensor("HA", [128, NHT * 1024], BF16))
    SIL = scC.al(nc.sbuf_tensor("SIL", [128, 1024], BF16))
    PS = [scC.al(nc.psum_tensor(f"PC{i}", [128, 512], F32)) for i in range(8)]
    if True:
        GPs = [PS[0], PS[1]]
        UPs = [PS[2], PS[3]]
        WPS = [[PS[0], PS[1]], [PS[2], PS[3]]]
        DPS = [[PS[4], PS[5]], [PS[6], PS[7]]]
        p = Plan(nc, sems, counts)

        def fillerC(waits=()):
            p.emit("tensor", lambda: nc.tensor.matmul(
                PS[4][:, 0:128], WO2[:, 0:128], WO2[:, 0:128], start=True, stop=True),
                waits=waits, inc=False)

        c_atp = 0
        for dc in range(NDC):
            w = [("AT0", counts["AT0"]), ("AT1", counts["AT1"]), ("NM", counts["NM"]),
                 ("RR", counts["RR"]), ("WL", counts["WL"]), ("XL0", counts["XL0"]),
                 ("XL1", counts["XL1"]), ("XS0", counts["XS0"]), ("XS1", counts["XS1"]),
                 ("CONST", counts["CONST"])] if dc == 0 else [("TA%d" % (dc - 1), 16)]
            c_atp = p.dmat(AFM[:, dc * T:(dc + 1) * T], datt[:, dc * 128:(dc + 1) * 128],
                           waits=w, ch="TA%d" % dc)
            fillerC([("TA%d" % dc, 16)])
            for _ in range(12):
                fillerC()

        c_atp = [("TA%d" % i, 16) for i in range(8)]
        c_wdt = p.dma("gpsimd", WDT[:], dwd, waits=c_atp, ch="WD")
        wg_ld = [0] * NHT
        wu_ld = [0] * NHT
        for ht in range(2):
            wg_ld[ht] = p.dma("gpsimd", WGT[:, ht * D:(ht + 1) * D], dwg[:, ht * D:(ht + 1) * D],
                              ch="WG%d" % (ht % 2))
            wu_ld[ht] = p.dma("gpsimd", WUT[:, ht * D:(ht + 1) * D], dwu[:, ht * D:(ht + 1) * D],
                              ch="WG%d" % (ht % 2))

        gate_done = [0, 0]
        sl_done = [0, 0]
        ha_done = [0, 0]
        ydone = [0] * NTT
        yst = [0, 0]
        xld = [0, 0]
        down_rd = 0
        c_ha_dv = 0

        for half in range(2):
            for ht in range(NHT):
                b = ht % 2
                nxt = ht + 1 if ht + 1 < NHT else None
                if half == 0 and nxt is not None and nxt >= 2:
                    w = [("PE", gate_done[nxt % 2])]
                    wg_ld[nxt] = p.dma("gpsimd", WGT[:, (nxt % 2) * D:((nxt % 2) + 1) * D],
                                       dwg[:, nxt * D:(nxt + 1) * D], waits=w, ch="WG%d" % (nxt % 2))
                    wu_ld[nxt] = p.dma("gpsimd", WUT[:, (nxt % 2) * D:((nxt % 2) + 1) * D],
                                       dwu[:, nxt * D:(nxt + 1) * D], ch="WG%d" % (nxt % 2))
                elif half == 1:
                    w = [("PE", gate_done[b])]
                    wg_ld[ht] = p.dma("gpsimd", WGT[:, b * D:(b + 1) * D],
                                      dwg[:, ht * D:(ht + 1) * D], waits=w, ch="WG%d" % b)
                    wu_ld[ht] = p.dma("gpsimd", WUT[:, b * D:(b + 1) * D],
                                      dwu[:, ht * D:(ht + 1) * D], ch="WG%d" % b)
                for tb in range(2):
                    t0 = half * 1024 + tb * 512
                    c_gm = 0
                    c_um = 0
                    for dcc in range(NDC):
                        wm = []
                        if dcc == 0:
                            wm = [("WG%d" % b, wu_ld[ht]), ("AC", sl_done[tb])]
                            wm.append(("DV", yst[0]))
                        c_gm = p.emit("tensor", lambda b=b, dcc=dcc, t0=t0, tb=tb: nc.tensor.matmul(
                            GPs[tb][:, 0:512], WGT[:, b * D + dcc * 128: b * D + (dcc + 1) * 128],
                            XN[:, dcc * T + t0: dcc * T + t0 + 512], start=(dcc == 0), stop=(dcc == NDC - 1)),
                            waits=wm, inc=(dcc == NDC - 1))
                    for dcc in range(NDC):
                        wm = []
                        if dcc == 0:
                            wm = [("WG%d" % b, wu_ld[ht]), ("DV", max(ha_done[tb], yst[1]))]
                        c_um = p.emit("tensor", lambda b=b, dcc=dcc, t0=t0, tb=tb: nc.tensor.matmul(
                            UPs[tb][:, 0:512], WUT[:, b * D + dcc * 128: b * D + (dcc + 1) * 128],
                            XN[:, dcc * T + t0: dcc * T + t0 + 512], start=(dcc == 0), stop=(dcc == NDC - 1)),
                            waits=wm, inc=(dcc == NDC - 1))
                    gate_done[b] = c_um
                    wsl = [("PE", c_gm), ("DV", ha_done[tb])]
                    sl_done[tb] = p.emit("scalar", lambda tb=tb: nc.scalar.activation(
                        SIL[:, tb * 512:(tb + 1) * 512], GPs[tb][:, 0:512], AF.Silu, scale=S127), waits=wsl)
                    wha = [("AC", sl_done[tb]), ("PE", c_um)]
                    if half == 1 and ht == 0 and tb == 0:
                        wha.append(("PE", down_rd))
                    c_ha_dv = p.emit("vector", lambda ht=ht, tb=tb: nc.vector.tensor_tensor(
                        HA[:, ht * 1024 + tb * 512: ht * 1024 + (tb + 1) * 512],
                        SIL[:, tb * 512:(tb + 1) * 512], UPs[tb][:, 0:512], ALU.mult), waits=wha)
                    ha_done[tb] = c_ha_dv

            for tti in range(8):
                tt = half * 8 + tti
                b = tti % 2
                w = c_atp + [("DV", yst[b])]
                if tt >= 2:
                    w.append(("YS%d" % b, ydone[tt - 2]))
                xld[b] = p.dma("sync", XT1[:, b * D:(b + 1) * D], dx[tt * 128:(tt + 1) * 128, :],
                               waits=w, ch="CX%d" % b)
                cwo = [0, 0]
                for hf2 in range(2):
                    c_mm = 0
                    for dcc in range(NDC):
                        wm = []
                        if dcc == 0:
                            wm = [("WL", c_wo2), ("DV", yst[b])]
                            if tti == 0 and hf2 == 0:
                                wm.append(("AC", max(sl_done)))
                                wm.append(("DV", c_ha_dv))
                        c_mm = p.emit("tensor", lambda tt=tt, hf2=hf2, dcc=dcc, b=b: nc.tensor.matmul(
                            WPS[b][hf2][:, 0:512], AFM[:, dcc * T + tt * 128: dcc * T + (tt + 1) * 128],
                            WO2[:, dcc * D + hf2 * 512: dcc * D + (hf2 + 1) * 512],
                            start=(dcc == 0), stop=(dcc == NDC - 1)), waits=wm, inc=(dcc == NDC - 1))
                    cwo[hf2] = c_mm
                cdn = [0, 0]
                for hd2 in range(2):
                    c_mm = 0
                    for ht in range(NHT):
                        wm = []
                        if ht == 0:
                            wm = [("DV", max(c_ha_dv, yst[b])), ("WD", c_wdt)]
                        c_mm = p.emit("tensor", lambda b=b, hd2=hd2, ht=ht, tti=tti: nc.tensor.matmul(
                            DPS[b][hd2][:, 0:512], HA[:, ht * 1024 + tti * 128: ht * 1024 + (tti + 1) * 128],
                            WDT[:, ht * D + hd2 * 512: ht * D + (hd2 + 1) * 512],
                            start=(ht == 0), stop=(ht == NHT - 1)), waits=wm, inc=(ht == NHT - 1))
                    cdn[hd2] = c_mm
                down_rd = max(down_rd, cdn[1])
                for hf2 in range(2):
                    c_t1 = p.emit("vector", lambda b=b, hf2=hf2: nc.vector.scalar_tensor_tensor(
                        YT[:, b * D + hf2 * 512: b * D + (hf2 + 1) * 512], DPS[b][hf2][:, 0:512], S2,
                        XT1[:, b * D + hf2 * 512: b * D + (hf2 + 1) * 512], ALU.mult, ALU.add),
                        waits=[("PE", cdn[hf2]), ("CX%d" % b, xld[b])])
                    yst[b] = p.emit("vector", lambda b=b, hf2=hf2: nc.vector.scalar_tensor_tensor(
                        YT[:, b * D + hf2 * 512: b * D + (hf2 + 1) * 512], WPS[b][hf2][:, 0:512], 0.0,
                        YT[:, b * D + hf2 * 512: b * D + (hf2 + 1) * 512], ALU.add, ALU.add),
                        waits=[("PE", cwo[hf2]), ("DV", c_t1)])
                ydone[tt] = p.dma("sync", dy[tt * 128:(tt + 1) * 128, :], YT[:, b * D:(b + 1) * D],
                                  waits=[("DV", yst[b])], ch="YS%d" % b)

        p.emit("vector", lambda: nc.vector.memset(SIL[0:1, 0:1], 0.0),
               waits=[("YS0", counts["YS0"]), ("YS1", counts["YS1"]),
                      ("CX0", counts["CX0"]), ("CX1", counts["CX1"]),
                      ("WD", counts["WD"]), ("WG0", counts["WG0"]), ("WG1", counts["WG1"])]
                     + [("TA%d" % i, counts["TA%d" % i]) for i in range(8)]
                     + [("TP%d" % i, counts["TP%d" % i]) for i in range(8)], inc=False)
        p.replay()
        scC.close()

    while ctx:
        ctx.pop().__exit__(None, None, None)
    return nc


def _host_layouts(W_kv_down, W_k_up, W_v_up, W_q, W_q_rope, W_k_rope, W_o, w_gate, w_up, w_down):
    def chunk_pd(w, inner):
        return np.ascontiguousarray(np.asarray(w, np.float32).reshape(NDC, 128, inner)
                                    .transpose(1, 0, 2).reshape(128, NDC * inner))

    wq = chunk_pd(W_q, D)
    wkv = chunk_pd(W_kv_down, DCKV)
    wqr = chunk_pd(W_q_rope, DR)
    wo = chunk_pd(W_o, D)
    wkup = np.ascontiguousarray(np.asarray(W_k_up, np.float32))
    wvup = np.ascontiguousarray(np.asarray(W_v_up, np.float32))
    wkr = np.ascontiguousarray(np.asarray(W_k_rope, np.float32))
    wg = np.ascontiguousarray(np.asarray(w_gate, np.int32).reshape(NHT, 128, NDC, 128)
                              .transpose(3, 0, 2, 1).reshape(128, NHT * D))
    wu = np.ascontiguousarray(np.asarray(w_up, np.int32).reshape(NHT, 128, NDC, 128)
                              .transpose(3, 0, 2, 1).reshape(128, NHT * D))
    wd = np.ascontiguousarray(np.asarray(w_down, np.int32).T.reshape(NHT, 128, D)
                              .transpose(1, 0, 2).reshape(128, NHT * D))
    return dict(wq=wq, wkv=wkv, wqr=wqr, wkup=wkup, wvup=wvup, wkr=wkr, wo=wo, wg=wg, wu=wu, wd=wd)


def _host_consts():
    inv_freq = 1.0 / 10000.0 ** (np.arange(0, DR, 2, dtype=np.float64) / DR)
    t = np.arange(T, dtype=np.float64)
    freqs = np.outer(t, inv_freq)
    cos = np.cos(freqs).astype(np.float32)
    sin = np.sin(freqs).astype(np.float32)

    def tile_tab(tab):
        return np.ascontiguousarray(tab.reshape(NTT, 128, 16).transpose(1, 0, 2).reshape(128, NTT * 16))

    mask = (np.arange(128)[None, :] >= np.arange(128)[:, None]).astype(ml_dtypes.bfloat16)
    ident = np.eye(128).astype(ml_dtypes.bfloat16)
    hmask = np.zeros((128, 2), ml_dtypes.bfloat16)
    hmask[0:64, 0] = 1
    hmask[64:128, 1] = 1
    oc2 = np.ones((128, 2), ml_dtypes.bfloat16)
    return dict(tcos=tile_tab(cos), tsin=tile_tab(sin),
                tcosq=tile_tab(cos * np.float32(SCALE)), tsinq=tile_tab(sin * np.float32(SCALE)),
                mask=mask, ident=ident, hmask=hmask, oc2=oc2)


def kernel(L0_attn_x, L0_attn_W_kv_down, L0_attn_W_k_up, L0_attn_W_v_up, L0_attn_W_q,
           L0_attn_W_q_rope, L0_attn_W_k_rope, L0_attn_W_o, L0_ffn_w_gate, L0_ffn_w_up,
           L0_ffn_w_down):
    x = np.ascontiguousarray(np.asarray(L0_attn_x, np.float32))
    weights = _host_layouts(L0_attn_W_kv_down, L0_attn_W_k_up, L0_attn_W_v_up, L0_attn_W_q,
                            L0_attn_W_q_rope, L0_attn_W_k_rope, L0_attn_W_o,
                            L0_ffn_w_gate, L0_ffn_w_up, L0_ffn_w_down)
    consts = _host_consts()
    nc = build(debug=False)
    in_maps = []
    for b in range(8):
        m = {"x": np.ascontiguousarray(x[b])}
        m.update(weights)
        m.update(consts)
        in_maps.append(m)
    res = run_bass_kernel_spmd(nc, in_maps, core_ids=list(range(8)))
    out = np.stack([np.asarray(res.results[b]["y"]) for b in range(8)], axis=0)
    return out.astype(np.float32)


# version 27
# speedup vs baseline: 1.0130x; 1.0067x over previous
import math
import numpy as np
import ml_dtypes
import concourse.bass as bass
import concourse.mybir as mybir
from concourse.bass_utils import run_bass_kernel_spmd

F32 = mybir.dt.float32
BF16 = mybir.dt.bfloat16
I32 = mybir.dt.int32
AF = mybir.ActivationFunctionType
ALU = mybir.AluOpType
AX = mybir.AxisListType

T = 2048
D = 1024
NDC = 8
DCKV = 128
DR = 32
NH = 16
HD = 64
HID = 3072
NHT = 24
NTT = 16
SCALE = 1.0 / math.sqrt(HD + DR)
S127 = 1.0 / 127.0
S2 = 1.0 / (127.0 * 127.0)
VSTRIDE = NH * 65


class Scope:
    def __init__(self):
        self.cms = []

    def al(self, cm):
        v = cm.__enter__()
        self.cms.append(cm)
        return v

    def close(self):
        while self.cms:
            self.cms.pop().__exit__(None, None, None)

ENGINES = ("tensor", "vector", "scalar", "gpsimd", "sync")
ENG2CTR = {"tensor": "PE", "vector": "DV", "scalar": "AC", "gpsimd": "GP", "sync": "SY"}


class Plan:

    def __init__(self, nc, sems, counts):
        self.nc = nc
        self.sems = sems
        self.counts = counts
        self.items = []

    def emit(self, eng, fn, waits=(), inc=True):
        own = ENG2CTR[eng]
        amt = 16 if eng in ("sync", "gpsimd") else 1
        ws = []
        for (n, v) in waits:
            if v is None or v <= 0:
                continue
            assert v <= self.counts[n], f"wait on future: {n} {v} > {self.counts[n]}"
            ws.append((n, v))
        if inc:
            self.counts[own] += amt
        self.items.append((eng, tuple(ws), fn, own if inc else None, amt))
        return self.counts[own]

    def dma(self, eng, out, in_, waits=(), ch=None, **kw):
        own = ch or ENG2CTR[eng]
        ws = []
        for (n, v) in waits:
            if v is None or v <= 0:
                continue
            assert v <= self.counts[n], f"wait on future: {n} {v} > {self.counts[n]}"
            ws.append((n, v))
        self.counts[own] += 16
        self.items.append((eng, tuple(ws), lambda: getattr(self.nc, eng).dma_start(out=out, in_=in_, **kw),
                           own, 16))
        return self.counts[own]

    def dmat(self, out, in_, waits=(), ch=None):
        own = ch or "SY"
        ws = []
        for (n, v) in waits:
            if v is None or v <= 0:
                continue
            assert v <= self.counts[n], f"wait on future: {n} {v} > {self.counts[n]}"
            ws.append((n, v))
        self.counts[own] += 16
        self.items.append(("sync", tuple(ws), lambda: self.nc.sync.dma_start_transpose(out, in_), own, 16))
        return self.counts[own]

    def replay(self):
        nc = self.nc
        with nc.Block() as blk:
            for eng_name in ENGINES:
                self._section(blk, eng_name)

    def _section(self, blk, eng_name):
        items = self.items
        sems = self.sems

        def body(eng):
            hi = {}
            for (e, ws, fn, incn, amt) in items:
                if e != eng_name:
                    continue
                for (n, v) in ws:
                    if hi.get(n, 0) < v:
                        eng.wait_ge(sems[n], v)
                        hi[n] = v
                inst = fn()
                if incn:
                    inst.then_inc(sems[incn], amt)

        getattr(blk, eng_name)(body)


def build(debug=False):
    nc = bass.Bass("TRN2", target_bir_lowering=False)

    dx = nc.dram_tensor("x", [T, D], F32, kind="ExternalInput").ap()
    dwq = nc.dram_tensor("wq", [128, NDC * D], F32, kind="ExternalInput").ap()
    dwkv = nc.dram_tensor("wkv", [128, NDC * DCKV], F32, kind="ExternalInput").ap()
    dwqr = nc.dram_tensor("wqr", [128, NDC * DR], F32, kind="ExternalInput").ap()
    dwkup = nc.dram_tensor("wkup", [128, D], F32, kind="ExternalInput").ap()
    dwvup = nc.dram_tensor("wvup", [128, D], F32, kind="ExternalInput").ap()
    dwkr = nc.dram_tensor("wkr", [128, DR], F32, kind="ExternalInput").ap()
    dwo = nc.dram_tensor("wo", [128, NDC * D], F32, kind="ExternalInput").ap()
    dwg = nc.dram_tensor("wg", [128, NHT * D], I32, kind="ExternalInput").ap()
    dwu = nc.dram_tensor("wu", [128, NHT * D], I32, kind="ExternalInput").ap()
    dwd = nc.dram_tensor("wd", [128, NHT * D], I32, kind="ExternalInput").ap()
    dcos = nc.dram_tensor("tcos", [128, NTT * 16], F32, kind="ExternalInput").ap()
    dsin = nc.dram_tensor("tsin", [128, NTT * 16], F32, kind="ExternalInput").ap()
    dcosq = nc.dram_tensor("tcosq", [128, NTT * 16], F32, kind="ExternalInput").ap()
    dsinq = nc.dram_tensor("tsinq", [128, NTT * 16], F32, kind="ExternalInput").ap()
    dmask = nc.dram_tensor("mask", [128, 128], BF16, kind="ExternalInput").ap()
    dident = nc.dram_tensor("ident", [128, 128], BF16, kind="ExternalInput").ap()
    dhmask = nc.dram_tensor("hmask", [128, 2], BF16, kind="ExternalInput").ap()
    doc2 = nc.dram_tensor("oc2", [128, 2], BF16, kind="ExternalInput").ap()

    dy = nc.dram_tensor("y", [T, D], F32, kind="ExternalOutput").ap()
    xnk = "ExternalOutput" if debug else "Internal"
    dxn = nc.dram_tensor("xn_d", [T, D], BF16, kind=xnk).ap()
    datt = nc.dram_tensor("att_d", [T, D], BF16, kind=xnk).ap()
    if debug:
        dqf = nc.dram_tensor("qf_d", [128, NDC * T], BF16, kind="ExternalOutput").ap()
        dkf = nc.dram_tensor("kf_d", [128, NDC * T], BF16, kind="ExternalOutput").ap()
        dck = nc.dram_tensor("ck_d", [128, T], BF16, kind="ExternalOutput").ap()
        dva = nc.dram_tensor("va_d", [128, NTT * VSTRIDE], BF16, kind="ExternalOutput").ap()
        dqr = nc.dram_tensor("qr_d", [32, T], BF16, kind="ExternalOutput").ap()
        dkr = nc.dram_tensor("kr_d", [32, T], BF16, kind="ExternalOutput").ap()
        dnm = nc.dram_tensor("nm_d", [128, 16], F32, kind="ExternalOutput").ap()
        dha = nc.dram_tensor("ha_d", [128, NHT * 1024], BF16, kind="ExternalOutput").ap()

    CHANNELS = tuple(["PE", "DV", "AC", "GP", "SY", "XL0", "XL1", "XS0", "XS1", "CONST",
                     "WL", "NM", "RR", "AT0", "AT1", "WD", "WG0", "WG1",
                     "CX0", "CX1", "YS0", "YS1"]
                    + ["TP%d" % i for i in range(8)] + ["TA%d" % i for i in range(8)])
    counts = {c: 0 for c in CHANNELS}
    ctx = []

    def alloc(cm):
        v = cm.__enter__()
        ctx.append(cm)
        return v

    sems = {}
    for name in CHANNELS:
        sems[name] = alloc(nc.semaphore("s" + name))

    XN = alloc(nc.sbuf_tensor("XN", [128, NDC * T], BF16))
    WO2 = alloc(nc.sbuf_tensor("WO2", [128, NDC * D], BF16))
    MSK = alloc(nc.sbuf_tensor("MSK", [128, 128], BF16))
    IDN = alloc(nc.sbuf_tensor("IDN", [128, 128], BF16))
    HMK = alloc(nc.sbuf_tensor("HMK", [128, 2], BF16))
    OC2 = alloc(nc.sbuf_tensor("OC2", [128, 2], BF16))
    NEGM = alloc(nc.sbuf_tensor("NEGM", [128, 16], F32))

    scA = Scope()
    QF = scA.al(nc.sbuf_tensor("QF", [128, NDC * T], BF16))
    KF = scA.al(nc.sbuf_tensor("KF", [128, NDC * T], BF16))
    CKV = scA.al(nc.sbuf_tensor("CKV", [128, T], BF16))
    VA = scA.al(nc.sbuf_tensor("VA", [128, NTT * VSTRIDE], BF16))
    QRF = scA.al(nc.sbuf_tensor("QRF", [32, T], BF16))
    KRF = scA.al(nc.sbuf_tensor("KRF", [32, T], BF16))
    CT = scA.al(nc.sbuf_tensor("CT", [128, NTT * 16], F32))
    ST = scA.al(nc.sbuf_tensor("ST", [128, NTT * 16], F32))
    CQ = scA.al(nc.sbuf_tensor("CQ", [128, NTT * 16], F32))
    SQ = scA.al(nc.sbuf_tensor("SQ", [128, NTT * 16], F32))
    if True:
        sc1 = Scope()
        WQ = sc1.al(nc.sbuf_tensor("WQ", [128, NDC * D], BF16))
        WKV = sc1.al(nc.sbuf_tensor("WKV", [128, NDC * DCKV], BF16))
        WQR = sc1.al(nc.sbuf_tensor("WQR", [128, NDC * DR], BF16))
        WKUP = sc1.al(nc.sbuf_tensor("WKUP", [128, D], BF16))
        WVUP = sc1.al(nc.sbuf_tensor("WVUP", [128, D], BF16))
        WKR = sc1.al(nc.sbuf_tensor("WKR", [128, DR], BF16))
        LNX = sc1.al(nc.sbuf_tensor("LNX", [128, 2 * D], F32))
        XNT = sc1.al(nc.sbuf_tensor("XNT", [128, 2 * D], BF16))
        BN6 = sc1.al(nc.sbuf_tensor("BN6", [128, 12], F32))
        MV = sc1.al(nc.sbuf_tensor("MV", [128, 2], F32))
        SMT = sc1.al(nc.sbuf_tensor("SMT", [128, 4], F32))
        RSC = sc1.al(nc.sbuf_tensor("RSC", [128, 128], F32))
        QRT = sc1.al(nc.sbuf_tensor("QRT", [128, NTT * DR], BF16))
        KRT = sc1.al(nc.sbuf_tensor("KRT", [128, NTT * DR], BF16))
        SQS = sc1.al(nc.sbuf_tensor("SQS", [128, T], BF16))
        NMX = sc1.al(nc.sbuf_tensor("NMX", [2, 32], F32))
        NSC = sc1.al(nc.sbuf_tensor("NSC", [2, 32], F32))
        PA = sc1.al(nc.psum_tensor("PA", [128, 512], F32))
        PB = sc1.al(nc.psum_tensor("PB", [128, 512], F32))
        PVh = sc1.al(nc.psum_tensor("PVh", [128, 512], F32))
        PR = sc1.al(nc.psum_tensor("PR", [128, 512], F32))
        PT = sc1.al(nc.psum_tensor("PT", [128, 512], BF16))
        PN = sc1.al(nc.psum_tensor("PN", [128, 512], F32))
        PJ = sc1.al(nc.psum_tensor("PJ", [128, 512], F32))
        if True:
            p = Plan(nc, sems, counts)

            p.dma("sync", CT[:], dcos, ch="CONST")
            p.dma("sync", ST[:], dsin, ch="CONST")
            p.dma("sync", CQ[:], dcosq, ch="CONST")
            p.dma("sync", SQ[:], dsinq, ch="CONST")
            p.dma("sync", MSK[:], dmask, ch="CONST")
            p.dma("sync", IDN[:], dident, ch="CONST")
            p.dma("sync", HMK[:], dhmask, ch="CONST")
            c_const = p.dma("sync", OC2[:], doc2, ch="CONST")
            c_tabs = ("CONST", c_const)
            p.dma("gpsimd", WQ[:], dwq, ch="WL")
            p.dma("gpsimd", WKV[:], dwkv, ch="WL")
            p.dma("gpsimd", WQR[:], dwqr, ch="WL")
            p.dma("gpsimd", WKUP[:], dwkup, ch="WL")
            p.dma("gpsimd", WVUP[:], dwvup, ch="WL")
            p.dma("gpsimd", WKR[:], dwkr, ch="WL")
            c_wl = p.dma("gpsimd", WO2[:], dwo, ch="WL")
            c_wo2 = c_wl

            def filler(waits=()):
                p.emit("tensor", lambda: nc.tensor.matmul(
                    PJ[:, 0:128], IDN[:], IDN[:], start=True, stop=True), waits=waits, inc=False)

            for i in range(30):
                filler([("CONST", c_const)] if i == 0 else ())

            c_vaones = p.emit("vector", lambda: nc.vector.memset(VA[:], 1.0))

            ln_xn = [0] * NTT
            ln_store = [0] * NTT
            ln_exp = [0] * NTT
            c_aggr_prev = 0
            for tt in range(NTT):
                b = tt % 2
                w = [("DV", ln_xn[tt - 2])] if tt >= 2 else []
                c_ld = p.dma("sync", LNX[:, b * D:(b + 1) * D], dx[tt * 128:(tt + 1) * 128, :],
                             waits=w, ch="XL%d" % b)
                p.emit("vector", lambda b=b: nc.vector.bn_stats(BN6[:, 0:6], LNX[:, b * D: b * D + 512]),
                       waits=[("XL%d" % b, c_ld), ("DV", c_aggr_prev)])
                c_bn2 = p.emit("vector", lambda b=b: nc.vector.bn_stats(BN6[:, 6:12], LNX[:, b * D + 512: b * D + 1024]))
                c_aggr_prev = p.emit("vector", lambda: nc.vector.bn_aggr(MV[:, 0:2], BN6[:, 0:12]),
                                     waits=[("DV", c_bn2)])
                wv = [("DV", c_aggr_prev)]
                if tt >= 1:
                    wv.append(("AC", ln_exp[tt - 1]))
                c_v = p.emit("vector", lambda: nc.vector.tensor_scalar(SMT[:, 0:1], MV[:, 1:2], 1e-5, None, ALU.add),
                             waits=wv)
                c_ln = p.emit("scalar", lambda: nc.scalar.activation(SMT[:, 1:2], SMT[:, 0:1], AF.Ln),
                              waits=[("DV", c_v)])
                ln_exp[tt] = p.emit("scalar", lambda: nc.scalar.activation(
                    SMT[:, 2:3], SMT[:, 1:2], AF.Exp, scale=-0.5), waits=[("AC", c_ln)])
                c_stt = p.emit("vector", lambda: nc.vector.scalar_tensor_tensor(
                    SMT[:, 3:4], SMT[:, 2:3], -1.0, MV[:, 0:1], ALU.mult, ALU.mult), waits=[("AC", ln_exp[tt])])
                wst = [("DV", c_stt)]
                if tt >= 2:
                    wst.append(("XS%d" % b, ln_store[tt - 2]))
                ln_xn[tt] = p.emit("vector", lambda b=b: nc.vector.tensor_scalar(
                    XNT[:, b * D:(b + 1) * D], LNX[:, b * D:(b + 1) * D], SMT[:, 2:3], SMT[:, 3:4],
                    ALU.mult, ALU.add), waits=wst)
                ln_store[tt] = p.dma("sync", dxn[tt * 128:(tt + 1) * 128, :], XNT[:, b * D:(b + 1) * D],
                                     waits=[("DV", ln_xn[tt])], ch="XS%d" % b)
                filler([("XL%d" % b, c_ld)])
                for _ in range(4):
                    filler()

            c_tp = 0
            for dc in range(NDC):
                w = [("XS0", counts["XS0"]), ("XS1", counts["XS1"]), ("XL0", counts["XL0"]),
                     ("XL1", counts["XL1"]), ("CONST", counts["CONST"]), ("WL", c_wl)] if dc == 0 \
                    else [("TP%d" % (dc - 1), 16)]
                c_tp = p.dmat(XN[:, dc * T:(dc + 1) * T], dxn[:, dc * 128:(dc + 1) * 128],
                              waits=w, ch="TP%d" % dc)
                filler([("TP%d" % dc, 16)])
                for _ in range(12):
                    filler()
            c_xn_ready = [("TP%d" % i, 16) for i in range(8)]

            def proj_group(psum_ap, lhsT_list, rhs_list, first_waits):
                last = 0
                n = len(lhsT_list)
                for i, (lh, rh) in enumerate(zip(lhsT_list, rhs_list)):
                    w = first_waits if i == 0 else []
                    last = p.emit("tensor", lambda lh=lh, rh=rh, i=i, n=n, ps=psum_ap: nc.tensor.matmul(
                        ps, lh, rh, start=(i == 0), stop=(i == n - 1)), waits=w, inc=(i == n - 1))
                return last

            psel = [PA, PB]
            evac = [0, 0]
            for qb in range(4):
                i = qb % 2
                w = c_xn_ready + [("WL", c_wl), ("AC", evac[i])]
                c_mm = proj_group(psel[i][:, 0:512], [WKV[:, dc * 128:(dc + 1) * 128] for dc in range(NDC)],
                                  [XN[:, dc * T + qb * 512: dc * T + (qb + 1) * 512] for dc in range(NDC)], w)
                evac[i] = p.emit("scalar", lambda qb=qb, i=i: nc.scalar.activation(
                    CKV[:, qb * 512:(qb + 1) * 512], psel[i][:, 0:512], AF.Copy), waits=[("PE", c_mm)])
            c_ckv_ready = max(evac)

            for nt in range(NDC):
                for qb in range(4):
                    i = (nt * 4 + qb) % 2
                    c_mm = proj_group(psel[i][:, 0:512],
                                      [WQ[:, dc * D + nt * 128: dc * D + (nt + 1) * 128] for dc in range(NDC)],
                                      [XN[:, dc * T + qb * 512: dc * T + (qb + 1) * 512] for dc in range(NDC)],
                                      [("AC", evac[i])])
                    evac[i] = p.emit("scalar", lambda nt=nt, qb=qb, i=i: nc.scalar.activation(
                        QF[:, nt * T + qb * 512: nt * T + (qb + 1) * 512], psel[i][:, 0:512], AF.Copy,
                        scale=SCALE), waits=[("PE", c_mm)])

            for nt in range(NDC):
                for qb in range(4):
                    i = (nt * 4 + qb) % 2
                    c_mm = proj_group(psel[i][:, 0:512], [WKUP[:, nt * 128:(nt + 1) * 128]],
                                      [CKV[:, qb * 512:(qb + 1) * 512]],
                                      [("AC", max(evac[i], c_ckv_ready))])
                    evac[i] = p.emit("scalar", lambda nt=nt, qb=qb, i=i: nc.scalar.activation(
                        KF[:, nt * T + qb * 512: nt * T + (qb + 1) * 512], psel[i][:, 0:512], AF.Copy),
                        waits=[("PE", c_mm)])

            va_v = VA[:].rearrange("p (tt h j) -> p tt h j", tt=NTT, h=NH)
            c_v_dv = 0
            for tt in range(NTT):
                w0 = [("AC", c_ckv_ready), ("DV", c_vaones)]
                if tt >= 1:
                    w0.append(("DV", c_v_dv))
                c_mm1 = p.emit("tensor", lambda tt=tt: nc.tensor.matmul(
                    PVh[:, 0:512], CKV[:, tt * 128:(tt + 1) * 128], WVUP[:, 0:512],
                    start=True, stop=True), waits=w0)
                c_mm2 = p.emit("tensor", lambda tt=tt: nc.tensor.matmul(
                    PR[:, 0:512], CKV[:, tt * 128:(tt + 1) * 128], WVUP[:, 512:1024],
                    start=True, stop=True))
                wv1 = [("PE", c_mm1)]
                if tt == 0:
                    wv1.append(("DV", c_vaones))
                p.emit("vector", lambda tt=tt: nc.vector.tensor_copy(
                    va_v[:, tt, 0:8, 0:64], PVh[:, 0:512]), waits=wv1)
                c_v_dv = p.emit("vector", lambda tt=tt: nc.vector.tensor_copy(
                    va_v[:, tt, 8:16, 0:64], PR[:, 0:512]), waits=[("PE", c_mm2)])

            def rope_block(tt, psum, rb, ctab, stab, dstT, w_first, qscale=None):
                x1 = psum[:, 0:16]
                x2 = psum[:, 16:32]
                c = ctab[:, tt * 16:(tt + 1) * 16]
                s = stab[:, tt * 16:(tt + 1) * 16]
                o = rb * 64

                def mul(dst, xs, tab):
                    if qscale is None:
                        return lambda: nc.vector.tensor_tensor(dst, xs, tab, ALU.mult)
                    return lambda: nc.vector.scalar_tensor_tensor(dst, xs, qscale, tab,
                                                                  ALU.mult, ALU.mult)
                p.emit("vector", mul(RSC[:, o:o + 16], x1, c), waits=w_first)
                c_b = p.emit("vector", mul(RSC[:, o + 16:o + 32], x2, s))
                p.emit("vector", lambda dstT=dstT, tt=tt: nc.vector.tensor_tensor(
                    dstT[:, tt * 32: tt * 32 + 16], RSC[:, o:o + 16], RSC[:, o + 16:o + 32], ALU.subtract),
                    waits=[("DV", c_b)])
                p.emit("vector", mul(RSC[:, o + 32:o + 48], x2, c))
                c_d = p.emit("vector", mul(RSC[:, o + 48:o + 64], x1, s))
                return p.emit("vector", lambda dstT=dstT, tt=tt: nc.vector.tensor_tensor(
                    dstT[:, tt * 32 + 16: tt * 32 + 32], RSC[:, o + 32:o + 48], RSC[:, o + 48:o + 64],
                    ALU.add), waits=[("DV", c_d)])

            c_rq = [0] * (NTT + 1)
            c_rk = [0] * (NTT + 1)
            c_rf_dv = 0
            QPS = [PR, PA]
            KPS = [PVh, PB]
            for tt in range(NTT):
                qp = QPS[tt % 2]
                kp = KPS[tt % 2]
                wq = [("DV", c_rq[tt - 2])] if tt >= 2 else (
                    [c_tabs, ("DV", c_v_dv), ("AC", counts["AC"])])
                c_mmq = proj_group(qp[:, 0:32],
                                   [XN[:, dc * T + tt * 128: dc * T + (tt + 1) * 128] for dc in range(NDC)],
                                   [WQR[:, dc * DR:(dc + 1) * DR] for dc in range(NDC)], wq)
                wk = [("DV", c_rk[tt - 2])] if tt >= 2 else [("DV", c_v_dv), ("AC", counts["AC"])]
                c_mmk = proj_group(kp[:, 0:32], [CKV[:, tt * 128:(tt + 1) * 128]], [WKR[:, 0:DR]], wk)
                if tt >= 1:
                    c_t1 = p.emit("tensor", lambda tt=tt: nc.tensor.transpose(
                        PT[0:32, 0:128], QRT[:, (tt - 1) * 32: tt * 32], IDN[:]),
                        waits=[("DV", c_rq[tt - 1]), ("DV", c_rf_dv)])
                    c_t2 = p.emit("tensor", lambda tt=tt: nc.tensor.transpose(
                        PT[0:32, 128:256], KRT[:, (tt - 1) * 32: tt * 32], IDN[:]),
                        waits=[("DV", c_rk[tt - 1])])
                c_rq[tt] = rope_block(tt, qp, 0, CQ, SQ, QRT, [("PE", c_mmq), ("DV", c_rq[tt - 1])],
                                      qscale=SCALE)
                c_rk[tt] = rope_block(tt, kp, 1, CT, ST, KRT, [("PE", c_mmk), ("DV", c_rk[tt - 1])])
                if tt >= 1:
                    p.emit("vector", lambda tt=tt: nc.vector.tensor_copy(
                        QRF[:, (tt - 1) * 128: tt * 128], PT[0:32, 0:128]), waits=[("PE", c_t1)])
                    c_rf_dv = p.emit("vector", lambda tt=tt: nc.vector.tensor_copy(
                        KRF[:, (tt - 1) * 128: tt * 128], PT[0:32, 128:256]), waits=[("PE", c_t2)])
            c_t1 = p.emit("tensor", lambda: nc.tensor.transpose(
                PT[0:32, 0:128], QRT[:, (NTT - 1) * 32: NTT * 32], IDN[:]),
                waits=[("DV", c_rq[NTT - 1]), ("DV", c_rf_dv)])
            c_t2 = p.emit("tensor", lambda: nc.tensor.transpose(
                PT[0:32, 128:256], KRT[:, (NTT - 1) * 32: NTT * 32], IDN[:]),
                waits=[("DV", c_rk[NTT - 1])])
            p.emit("vector", lambda: nc.vector.tensor_copy(
                QRF[:, (NTT - 1) * 128: NTT * 128], PT[0:32, 0:128]), waits=[("PE", c_t1)])
            c_rf_dv = p.emit("vector", lambda: nc.vector.tensor_copy(
                KRF[:, (NTT - 1) * 128: NTT * 128], PT[0:32, 128:256]), waits=[("PE", c_t2)])

            c_sqs_rd_pe = 0

            def norm_max(SRC, dst, w_src_dv):
                nonlocal c_sqs_rd_pe
                cdv = counts["DV"]
                for dc in range(NDC):
                    csq = p.emit("vector", lambda dc=dc, SRC=SRC: nc.vector.tensor_tensor(
                        SQS[:, 0:T], SRC[:, dc * T:(dc + 1) * T], SRC[:, dc * T:(dc + 1) * T], ALU.mult),
                        waits=[("PE", c_sqs_rd_pe), ("AC", w_src_dv), ("DV", cdv)])
                    for j in range(4):
                        c_mm = p.emit("tensor", lambda j=j: nc.tensor.matmul(
                            PN[0:2, 0:512], HMK[:, 0:2], SQS[:, j * 512:(j + 1) * 512], start=True, stop=True),
                            waits=[("DV", csq)] if j == 0 else [("DV", cdv)])
                        c_sqs_rd_pe = c_mm
                        cdv = p.emit("vector", lambda dc=dc, j=j: nc.vector.tensor_reduce(
                            NMX[0:2, dc * 4 + j: dc * 4 + j + 1], PN[0:2, 0:512], AX.X, ALU.max),
                            waits=[("PE", c_mm)])
                nmx_v = NMX[0:2, 0:32].rearrange("p (dc j) -> p dc j", dc=8)
                return p.emit("vector", lambda nmx_v=nmx_v, dst=dst: nc.vector.tensor_reduce(
                    dst, nmx_v, AX.X, ALU.max), waits=[("DV", cdv)])

            c_qn = norm_max(QF, NSC[0:2, 0:8], counts["AC"])
            c_kn = norm_max(KF, NSC[0:2, 8:16], counts["AC"])

            def rope_norm_max(SRC, dst, w_src_dv):
                nonlocal c_sqs_rd_pe
                cdv = counts["DV"]
                csq = p.emit("vector", lambda SRC=SRC: nc.vector.tensor_tensor(
                    SQS[0:32, 0:T], SRC[0:32, 0:T], SRC[0:32, 0:T], ALU.mult),
                    waits=[("PE", c_sqs_rd_pe), ("DV", max(w_src_dv, cdv))])
                for j in range(4):
                    c_mm = p.emit("tensor", lambda j=j: nc.tensor.matmul(
                        PN[0:2, 0:512], OC2[0:32, 0:2], SQS[0:32, j * 512:(j + 1) * 512], start=True, stop=True),
                        waits=[("DV", csq), ("CONST", c_const)] if j == 0 else [("DV", cdv)])
                    c_sqs_rd_pe = c_mm
                    cdv = p.emit("vector", lambda j=j: nc.vector.tensor_reduce(
                        NMX[0:2, j:j + 1], PN[0:2, 0:512], AX.X, ALU.max), waits=[("PE", c_mm)])
                return p.emit("vector", lambda dst=dst: nc.vector.tensor_reduce(
                    dst, NMX[0:2, 0:4], AX.X, ALU.max), waits=[("DV", cdv)])

            c_qr2 = rope_norm_max(QRF, NSC[0:2, 16:17], c_rf_dv)
            c_kr2 = rope_norm_max(KRF, NSC[0:2, 17:18], c_rf_dv)

            c1 = p.emit("vector", lambda: nc.vector.tensor_tensor(
                NSC[0:2, 18:26], NSC[0:2, 0:8], NSC[0:2, 8:16], ALU.mult), waits=[("DV", max(c_qn, c_kn))])
            c2 = p.emit("vector", lambda: nc.vector.tensor_tensor(
                NSC[0:2, 26:27], NSC[0:2, 16:17], NSC[0:2, 17:18], ALU.mult),
                waits=[("DV", max(c_qr2, c_kr2))])
            c3 = p.emit("scalar", lambda: nc.scalar.activation(NSC[0:2, 18:26], NSC[0:2, 18:26], AF.Sqrt),
                        waits=[("DV", c2)])
            c4 = p.emit("scalar", lambda: nc.scalar.activation(NSC[0:2, 26:27], NSC[0:2, 26:27], AF.Sqrt))
            c5 = p.emit("vector", lambda: nc.vector.tensor_scalar(
                NSC[0:2, 18:26], NSC[0:2, 18:26], NSC[0:2, 26:27], -1.0, ALU.add, ALU.mult),
                waits=[("AC", c4)])
            p.dma("gpsimd", NEGM[32:33, 0:8], NSC[0:1, 18:26], waits=[("DV", c5)], ch="NM")
            p.dma("gpsimd", NEGM[32:33, 8:16], NSC[1:2, 18:26], ch="NM")
            p.dma("gpsimd", NEGM[96:97, 0:8], NSC[0:1, 18:26], ch="NM")
            c_negm = p.dma("gpsimd", NEGM[96:97, 8:16], NSC[1:2, 18:26], ch="NM")

            if debug:
                wdbg = [("DV", counts["DV"]), ("AC", counts["AC"]), ("NM", counts["NM"])]
                p.dma("sync", dqf, QF[:], waits=wdbg, ch="CONST")
                p.dma("sync", dkf, KF[:], ch="CONST")
                p.dma("sync", dck, CKV[:], ch="CONST")
                p.dma("sync", dva, VA[:], ch="CONST")
                p.dma("sync", dqr, QRF[:], ch="CONST")
                p.dma("sync", dkr, KRF[:], ch="CONST")
                p.dma("sync", dnm, NEGM[:], ch="CONST")

            p.replay()
        sc1.close()

        c_a1_dv = counts["DV"]
        c_a1_ac = counts["AC"]
        c_a1_pe = counts["PE"]
        c_negm = counts["NM"]

        sc2 = Scope()
        QC = sc2.al(nc.sbuf_tensor("QC", [128, 2 * T], BF16))
        KC = sc2.al(nc.sbuf_tensor("KC", [128, 2 * T], BF16))
        ET = sc2.al(nc.sbuf_tensor("ET", [128, 5 * 512], BF16))
        ATS = sc2.al(nc.sbuf_tensor("ATS", [128, 2 * 1024], BF16))
        RCP = sc2.al(nc.sbuf_tensor("RCP", [128, 16], F32))
        STR0 = sc2.al(nc.psum_tensor("STR0", [128, 512], F32))
        STR1 = sc2.al(nc.psum_tensor("STR1", [128, 512], F32))
        STR2 = sc2.al(nc.psum_tensor("STR2", [128, 512], F32))
        STR3 = sc2.al(nc.psum_tensor("STR3", [128, 512], F32))
        STR4 = sc2.al(nc.psum_tensor("STR4", [128, 512], F32))
        AVB0 = sc2.al(nc.psum_tensor("AVB0", [128, 512], F32))
        AVB1 = sc2.al(nc.psum_tensor("AVB1", [128, 512], F32))
        AVB2 = sc2.al(nc.psum_tensor("AVB2", [128, 512], F32))
        if True:
            STRIPS = [STR0, STR1, STR2, STR3, STR4]
            AVB = [AVB0, AVB1, AVB2]
            p = Plan(nc, sems, counts)

            cz1 = p.emit("vector", lambda: nc.vector.memset(QC[:], 0.0),
                         waits=[("DV", c_a1_dv), ("AC", c_a1_ac), ("PE", c_a1_pe)])
            cz2 = p.emit("vector", lambda: nc.vector.memset(KC[:], 0.0))
            p.emit("vector", lambda: nc.vector.memset(KC[96:97, 0:T], 1.0), waits=[("DV", cz2)])
            p.emit("vector", lambda: nc.vector.memset(KC[32:33, T:2 * T], 1.0))
            p.dma("gpsimd", QC[64:96, 0:T], QRF[0:32, 0:T], waits=[("DV", cz1)], ch="RR")
            c_gp_rope = p.dma("gpsimd", KC[64:96, 0:T], KRF[0:32, 0:T], waits=[("DV", cz2)], ch="RR")
            p.emit("vector", lambda: nc.vector.tensor_copy(QC[0:32, T:2 * T], QRF[0:32, 0:T]),
                   waits=[("DV", cz2)])
            c_setup_dv = p.emit("vector", lambda: nc.vector.tensor_copy(KC[0:32, T:2 * T], KRF[0:32, 0:T]))

            slot_exp = [0] * 5
            slot_avdone = [0] * 5
            qk_done = [0] * (NH + 1)
            head_evac_dv = 0
            att_dma = [0, 0]
            datt_v = datt.rearrange("(qt p) n -> p qt n", p=128)
            AVLAG = 3

            def emit_assembly(h):
                s = h % 2
                dc = h // 2
                col = s * 8 + dc
                rows = slice(64 * s, 64 * s + 64)
                mrow = 96 if s == 0 else 32
                base = s * T
                w_cp = [("PE", qk_done[h - 2] if h >= 2 else 0), ("RR", c_gp_rope), ("DV", c_setup_dv)]
                p.emit("vector", lambda rows=rows, base=base, dc=dc: nc.vector.tensor_copy(
                    QC[rows, base: base + T], QF[rows, dc * T:(dc + 1) * T]), waits=w_cp)
                p.emit("vector", lambda rows=rows, base=base, dc=dc: nc.vector.tensor_copy(
                    KC[rows, base: base + T], KF[rows, dc * T:(dc + 1) * T]))
                return p.emit("vector", lambda mrow=mrow, base=base, col=col: nc.vector.tensor_scalar(
                    QC[mrow:mrow + 1, base: base + T], KC[mrow:mrow + 1, base: base + T],
                    NEGM[mrow:mrow + 1, col:col + 1], None, ALU.mult),
                    waits=[("NM", c_negm), ("DV", c_setup_dv)])

            c_asm_arr = [0] * NH
            c_asm_arr[0] = emit_assembly(0)
            evac_bank_prev = [0, 0, 0]

            for h in range(NH):
                s = h % 2
                base = s * T
                av_started = [False, False, False]
                pend = []

                def flush_av(info):
                    slot, kt, c_exp, c_mask, qts, first, ebase = info
                    for idx, qt in enumerate(qts):
                        bank = qt // 7
                        off7 = (qt % 7) * 66
                        eoff = ebase + qt * 128
                        wav = []
                        if idx == 0:
                            wav = [("AC", c_exp)]
                            if c_mask:
                                wav.append(("DV", c_mask))
                        st = not av_started[bank]
                        av_started[bank] = True
                        if st:
                            wav.append(("DV", evac_bank_prev[bank]))
                        is_last = (idx == len(qts) - 1)
                        c_av = p.emit("tensor",
                                      lambda bank=bank, off7=off7, eoff=eoff, st=st, kt=kt, h=h, qt=qt:
                                      nc.tensor.matmul(
                                          AVB[bank][:, off7: off7 + 65], ET[:, eoff: eoff + 128],
                                          VA[:, kt * VSTRIDE + h * 65: kt * VSTRIDE + (h + 1) * 65],
                                          start=st, stop=(qt == kt), skip_group_check=True),
                                      waits=wav, inc=is_last)
                        if is_last:
                            slot_avdone[slot] = c_av

                chunks = [(kt, jb) for kt in range(NTT) for jb in range(kt // 4, 4)]
                for ci, (kt, jb) in enumerate(chunks):
                    slot = ci % 5
                    wqk = [("DV", c_asm_arr[h]), ("AC", slot_exp[slot])]
                    c_qk = p.emit("tensor", lambda slot=slot, kt=kt, jb=jb, base=base: nc.tensor.matmul(
                        STRIPS[slot][:, 0:512], KC[:, base + kt * 128: base + (kt + 1) * 128],
                        QC[:, base + jb * 512: base + (jb + 1) * 512], start=True, stop=True), waits=wqk)
                    c_exp = p.emit("scalar", lambda slot=slot: nc.scalar.activation(
                        ET[:, slot * 512:(slot + 1) * 512], STRIPS[slot][:, 0:512], AF.Exp),
                        waits=[("PE", max(c_qk, slot_avdone[slot]))])
                    slot_exp[slot] = c_exp
                    c_mask = 0
                    if jb == kt // 4:
                        off = slot * 512 + kt * 128 - jb * 512
                        c_mask = p.emit("vector", lambda off=off: nc.vector.tensor_tensor(
                            ET[:, off: off + 128], ET[:, off: off + 128], MSK[:], ALU.mult),
                            waits=[("AC", c_exp)])
                    qts = [qt for qt in range(jb * 4, jb * 4 + 4) if qt >= kt]
                    pend.append((slot, kt, c_exp, c_mask, qts, (ci == 0), slot * 512 - jb * 512))
                    if ci == 1 and h + 1 < NH:
                        c_asm_arr[h + 1] = emit_assembly(h + 1)
                    if len(pend) > AVLAG:
                        flush_av(pend.pop(0))
                while pend:
                    flush_av(pend.pop(0))
                qk_done[h] = counts["PE"]
                c_av_all = counts["PE"]

                c_ats_dv = 0
                for qt in range(NTT):
                    bank = qt // 7
                    off7 = (qt % 7) * 66
                    w = [("PE", c_av_all)]
                    if qt == 0:
                        w.append(("DV", head_evac_dv))
                        if h >= 2:
                            w.append(("AT%d" % s, att_dma[s]))
                    c_rc = p.emit("vector", lambda bank=bank, off7=off7, qt=qt: nc.vector.reciprocal(
                        RCP[:, qt:qt + 1], AVB[bank][:, off7 + 64: off7 + 65]), waits=w)
                    c_ats_dv = p.emit("vector", lambda bank=bank, off7=off7, qt=qt, s=s: nc.vector.tensor_scalar(
                        ATS[:, s * 1024 + qt * 64: s * 1024 + (qt + 1) * 64],
                        AVB[bank][:, off7: off7 + 64], RCP[:, qt:qt + 1], None, ALU.mult),
                        waits=[("DV", c_rc)])
                    if qt in (6, 13, 15):
                        evac_bank_prev[qt // 7] = c_ats_dv
                head_evac_dv = c_ats_dv
                ats_v = ATS[:, s * 1024:(s + 1) * 1024].rearrange("p (qt c) -> p qt c", qt=NTT)
                att_dma[s] = p.dma("sync", datt_v[:, :, h * 64:(h + 1) * 64], ats_v,
                                   waits=[("DV", c_ats_dv)], ch="AT%d" % s)

            p.replay()
            sc2.close()
    scA.close()

    c_att_sy = counts["SY"]
    c_a2_dv = counts["DV"]
    c_a2_pe = counts["PE"]
    c_a2_ac = counts["AC"]
    c_a2_gp = counts["GP"]

    scC = Scope()
    AFM = scC.al(nc.sbuf_tensor("AFM", [128, NDC * T], BF16))
    XT1 = scC.al(nc.sbuf_tensor("XT1", [128, 2 * D], F32))
    YT = scC.al(nc.sbuf_tensor("YT", [128, 2 * D], F32))
    WGT = scC.al(nc.sbuf_tensor("WGT", [128, 2 * D], BF16))
    WUT = scC.al(nc.sbuf_tensor("WUT", [128, 2 * D], BF16))
    WDT = scC.al(nc.sbuf_tensor("WDT", [128, NHT * D], BF16))
    HA = scC.al(nc.sbuf_tensor("HA", [128, NHT * 1024], BF16))
    SIL = scC.al(nc.sbuf_tensor("SIL", [128, 1024], BF16))
    PS = [scC.al(nc.psum_tensor(f"PC{i}", [128, 512], F32)) for i in range(8)]
    if True:
        GPs = [PS[0], PS[1]]
        UPs = [PS[2], PS[3]]
        WPS = [[PS[0], PS[1]], [PS[2], PS[3]]]
        DPS = [[PS[4], PS[5]], [PS[6], PS[7]]]
        p = Plan(nc, sems, counts)

        def fillerC(waits=()):
            p.emit("tensor", lambda: nc.tensor.matmul(
                PS[4][:, 0:128], WO2[:, 0:128], WO2[:, 0:128], start=True, stop=True),
                waits=waits, inc=False)

        c_atp = 0
        for dc in range(NDC):
            w = [("AT0", counts["AT0"]), ("AT1", counts["AT1"]), ("NM", counts["NM"]),
                 ("RR", counts["RR"]), ("WL", counts["WL"]), ("XL0", counts["XL0"]),
                 ("XL1", counts["XL1"]), ("XS0", counts["XS0"]), ("XS1", counts["XS1"]),
                 ("CONST", counts["CONST"])] if dc == 0 else [("TA%d" % (dc - 1), 16)]
            c_atp = p.dmat(AFM[:, dc * T:(dc + 1) * T], datt[:, dc * 128:(dc + 1) * 128],
                           waits=w, ch="TA%d" % dc)
            fillerC([("TA%d" % dc, 16)])
            for _ in range(12):
                fillerC()

        c_atp = [("TA%d" % i, 16) for i in range(8)]
        c_wdt = p.dma("gpsimd", WDT[:], dwd, waits=c_atp, ch="WD")
        wg_ld = [0] * NHT
        wu_ld = [0] * NHT
        for ht in range(2):
            wg_ld[ht] = p.dma("gpsimd", WGT[:, ht * D:(ht + 1) * D], dwg[:, ht * D:(ht + 1) * D],
                              ch="WG%d" % (ht % 2))
            wu_ld[ht] = p.dma("gpsimd", WUT[:, ht * D:(ht + 1) * D], dwu[:, ht * D:(ht + 1) * D],
                              ch="WG%d" % (ht % 2))

        gate_done = [0, 0]
        sl_done = [0, 0]
        ha_done = [0, 0]
        ydone = [0] * NTT
        yst = [0, 0]
        xld = [0, 0]
        down_rd = 0
        c_ha_dv = 0

        for half in range(2):
            for ht in range(NHT):
                b = ht % 2
                nxt = ht + 1 if ht + 1 < NHT else None
                if half == 0 and nxt is not None and nxt >= 2:
                    w = [("PE", gate_done[nxt % 2])]
                    wg_ld[nxt] = p.dma("gpsimd", WGT[:, (nxt % 2) * D:((nxt % 2) + 1) * D],
                                       dwg[:, nxt * D:(nxt + 1) * D], waits=w, ch="WG%d" % (nxt % 2))
                    wu_ld[nxt] = p.dma("gpsimd", WUT[:, (nxt % 2) * D:((nxt % 2) + 1) * D],
                                       dwu[:, nxt * D:(nxt + 1) * D], ch="WG%d" % (nxt % 2))
                elif half == 1:
                    w = [("PE", gate_done[b])]
                    wg_ld[ht] = p.dma("gpsimd", WGT[:, b * D:(b + 1) * D],
                                      dwg[:, ht * D:(ht + 1) * D], waits=w, ch="WG%d" % b)
                    wu_ld[ht] = p.dma("gpsimd", WUT[:, b * D:(b + 1) * D],
                                      dwu[:, ht * D:(ht + 1) * D], ch="WG%d" % b)
                for tb in range(2):
                    t0 = half * 1024 + tb * 512
                    c_gm = 0
                    c_um = 0
                    for dcc in range(NDC):
                        wm = []
                        if dcc == 0:
                            wm = [("WG%d" % b, wu_ld[ht]), ("AC", sl_done[tb])]
                            wm.append(("DV", yst[0]))
                        c_gm = p.emit("tensor", lambda b=b, dcc=dcc, t0=t0, tb=tb: nc.tensor.matmul(
                            GPs[tb][:, 0:512], WGT[:, b * D + dcc * 128: b * D + (dcc + 1) * 128],
                            XN[:, dcc * T + t0: dcc * T + t0 + 512], start=(dcc == 0), stop=(dcc == NDC - 1)),
                            waits=wm, inc=(dcc == NDC - 1))
                    for dcc in range(NDC):
                        wm = []
                        if dcc == 0:
                            wm = [("WG%d" % b, wu_ld[ht]), ("DV", max(ha_done[tb], yst[1]))]
                        c_um = p.emit("tensor", lambda b=b, dcc=dcc, t0=t0, tb=tb: nc.tensor.matmul(
                            UPs[tb][:, 0:512], WUT[:, b * D + dcc * 128: b * D + (dcc + 1) * 128],
                            XN[:, dcc * T + t0: dcc * T + t0 + 512], start=(dcc == 0), stop=(dcc == NDC - 1)),
                            waits=wm, inc=(dcc == NDC - 1))
                    gate_done[b] = c_um
                    wsl = [("PE", c_gm), ("DV", ha_done[tb])]
                    sl_done[tb] = p.emit("scalar", lambda tb=tb: nc.scalar.activation(
                        SIL[:, tb * 512:(tb + 1) * 512], GPs[tb][:, 0:512], AF.Silu, scale=S127), waits=wsl)
                    wha = [("AC", sl_done[tb]), ("PE", c_um)]
                    if half == 1 and ht == 0 and tb == 0:
                        wha.append(("PE", down_rd))
                    c_ha_dv = p.emit("vector", lambda ht=ht, tb=tb: nc.vector.tensor_tensor(
                        HA[:, ht * 1024 + tb * 512: ht * 1024 + (tb + 1) * 512],
                        SIL[:, tb * 512:(tb + 1) * 512], UPs[tb][:, 0:512], ALU.mult), waits=wha)
                    ha_done[tb] = c_ha_dv

            for tti in range(8):
                tt = half * 8 + tti
                b = tti % 2
                w = c_atp + [("DV", yst[b])]
                if tt >= 2:
                    w.append(("YS%d" % b, ydone[tt - 2]))
                xld[b] = p.dma("sync", XT1[:, b * D:(b + 1) * D], dx[tt * 128:(tt + 1) * 128, :],
                               waits=w, ch="CX%d" % b)
                cwo = [0, 0]
                for hf2 in range(2):
                    c_mm = 0
                    for dcc in range(NDC):
                        wm = []
                        if dcc == 0:
                            wm = [("WL", c_wo2), ("DV", yst[b])]
                            if tti == 0 and hf2 == 0:
                                wm.append(("AC", max(sl_done)))
                                wm.append(("DV", c_ha_dv))
                        c_mm = p.emit("tensor", lambda tt=tt, hf2=hf2, dcc=dcc, b=b: nc.tensor.matmul(
                            WPS[b][hf2][:, 0:512], AFM[:, dcc * T + tt * 128: dcc * T + (tt + 1) * 128],
                            WO2[:, dcc * D + hf2 * 512: dcc * D + (hf2 + 1) * 512],
                            start=(dcc == 0), stop=(dcc == NDC - 1)), waits=wm, inc=(dcc == NDC - 1))
                    cwo[hf2] = c_mm
                cdn = [0, 0]
                for hd2 in range(2):
                    c_mm = 0
                    for ht in range(NHT):
                        wm = []
                        if ht == 0:
                            wm = [("DV", max(c_ha_dv, yst[b])), ("WD", c_wdt)]
                        c_mm = p.emit("tensor", lambda b=b, hd2=hd2, ht=ht, tti=tti: nc.tensor.matmul(
                            DPS[b][hd2][:, 0:512], HA[:, ht * 1024 + tti * 128: ht * 1024 + (tti + 1) * 128],
                            WDT[:, ht * D + hd2 * 512: ht * D + (hd2 + 1) * 512],
                            start=(ht == 0), stop=(ht == NHT - 1)), waits=wm, inc=(ht == NHT - 1))
                    cdn[hd2] = c_mm
                down_rd = max(down_rd, cdn[1])
                for hf2 in range(2):
                    c_t1 = p.emit("vector", lambda b=b, hf2=hf2: nc.vector.scalar_tensor_tensor(
                        YT[:, b * D + hf2 * 512: b * D + (hf2 + 1) * 512], DPS[b][hf2][:, 0:512], S2,
                        XT1[:, b * D + hf2 * 512: b * D + (hf2 + 1) * 512], ALU.mult, ALU.add),
                        waits=[("PE", cdn[hf2]), ("CX%d" % b, xld[b])])
                    yst[b] = p.emit("vector", lambda b=b, hf2=hf2: nc.vector.scalar_tensor_tensor(
                        YT[:, b * D + hf2 * 512: b * D + (hf2 + 1) * 512], WPS[b][hf2][:, 0:512], 0.0,
                        YT[:, b * D + hf2 * 512: b * D + (hf2 + 1) * 512], ALU.add, ALU.add),
                        waits=[("PE", cwo[hf2]), ("DV", c_t1)])
                ydone[tt] = p.dma("sync", dy[tt * 128:(tt + 1) * 128, :], YT[:, b * D:(b + 1) * D],
                                  waits=[("DV", yst[b])], ch="YS%d" % b)

        p.emit("vector", lambda: nc.vector.memset(SIL[0:1, 0:1], 0.0),
               waits=[("YS0", counts["YS0"]), ("YS1", counts["YS1"]),
                      ("CX0", counts["CX0"]), ("CX1", counts["CX1"]),
                      ("WD", counts["WD"]), ("WG0", counts["WG0"]), ("WG1", counts["WG1"])]
                     + [("TA%d" % i, counts["TA%d" % i]) for i in range(8)]
                     + [("TP%d" % i, counts["TP%d" % i]) for i in range(8)], inc=False)
        p.replay()
        scC.close()

    while ctx:
        ctx.pop().__exit__(None, None, None)
    return nc


def _host_layouts(W_kv_down, W_k_up, W_v_up, W_q, W_q_rope, W_k_rope, W_o, w_gate, w_up, w_down):
    def chunk_pd(w, inner):
        return np.ascontiguousarray(np.asarray(w, np.float32).reshape(NDC, 128, inner)
                                    .transpose(1, 0, 2).reshape(128, NDC * inner))

    wq = chunk_pd(W_q, D)
    wkv = chunk_pd(W_kv_down, DCKV)
    wqr = chunk_pd(W_q_rope, DR)
    wo = chunk_pd(W_o, D)
    wkup = np.ascontiguousarray(np.asarray(W_k_up, np.float32))
    wvup = np.ascontiguousarray(np.asarray(W_v_up, np.float32))
    wkr = np.ascontiguousarray(np.asarray(W_k_rope, np.float32))
    wg = np.ascontiguousarray(np.asarray(w_gate, np.int32).reshape(NHT, 128, NDC, 128)
                              .transpose(3, 0, 2, 1).reshape(128, NHT * D))
    wu = np.ascontiguousarray(np.asarray(w_up, np.int32).reshape(NHT, 128, NDC, 128)
                              .transpose(3, 0, 2, 1).reshape(128, NHT * D))
    wd = np.ascontiguousarray(np.asarray(w_down, np.int32).T.reshape(NHT, 128, D)
                              .transpose(1, 0, 2).reshape(128, NHT * D))
    return dict(wq=wq, wkv=wkv, wqr=wqr, wkup=wkup, wvup=wvup, wkr=wkr, wo=wo, wg=wg, wu=wu, wd=wd)


def _host_consts():
    import jax
    import jax.numpy as jnp
    with jax.default_device(jax.devices("cpu")[0]):
        inv_freq = 1.0 / 10000.0 ** (jnp.arange(0, DR, 2, dtype=jnp.float32) / DR)
        t = jnp.arange(T, dtype=jnp.float32)
        freqs = jnp.outer(t, inv_freq)
        cos = np.asarray(jnp.cos(freqs), np.float32)
        sin = np.asarray(jnp.sin(freqs), np.float32)

    def tile_tab(tab):
        return np.ascontiguousarray(tab.reshape(NTT, 128, 16).transpose(1, 0, 2).reshape(128, NTT * 16))

    mask = (np.arange(128)[None, :] >= np.arange(128)[:, None]).astype(ml_dtypes.bfloat16)
    ident = np.eye(128).astype(ml_dtypes.bfloat16)
    hmask = np.zeros((128, 2), ml_dtypes.bfloat16)
    hmask[0:64, 0] = 1
    hmask[64:128, 1] = 1
    oc2 = np.ones((128, 2), ml_dtypes.bfloat16)
    return dict(tcos=tile_tab(cos), tsin=tile_tab(sin),
                tcosq=tile_tab(cos), tsinq=tile_tab(sin),
                mask=mask, ident=ident, hmask=hmask, oc2=oc2)


def kernel(L0_attn_x, L0_attn_W_kv_down, L0_attn_W_k_up, L0_attn_W_v_up, L0_attn_W_q,
           L0_attn_W_q_rope, L0_attn_W_k_rope, L0_attn_W_o, L0_ffn_w_gate, L0_ffn_w_up,
           L0_ffn_w_down):
    x = np.ascontiguousarray(np.asarray(L0_attn_x, np.float32))
    weights = _host_layouts(L0_attn_W_kv_down, L0_attn_W_k_up, L0_attn_W_v_up, L0_attn_W_q,
                            L0_attn_W_q_rope, L0_attn_W_k_rope, L0_attn_W_o,
                            L0_ffn_w_gate, L0_ffn_w_up, L0_ffn_w_down)
    consts = _host_consts()
    nc = build(debug=False)
    in_maps = []
    for b in range(8):
        m = {"x": np.ascontiguousarray(x[b])}
        m.update(weights)
        m.update(consts)
        in_maps.append(m)
    res = run_bass_kernel_spmd(nc, in_maps, core_ids=list(range(8)))
    out = np.stack([np.asarray(res.results[b]["y"]) for b in range(8)], axis=0)
    return out.astype(np.float32)


# version 28
# speedup vs baseline: 1.0346x; 1.0213x over previous
import math
import numpy as np
import ml_dtypes
import concourse.bass as bass
import concourse.mybir as mybir
from concourse.bass_utils import run_bass_kernel_spmd

F32 = mybir.dt.float32
BF16 = mybir.dt.bfloat16
I32 = mybir.dt.int32
AF = mybir.ActivationFunctionType
ALU = mybir.AluOpType
AX = mybir.AxisListType

T = 2048
D = 1024
NDC = 8
DCKV = 128
DR = 32
NH = 16
HD = 64
HID = 3072
NHT = 24
NTT = 16
SCALE = 1.0 / math.sqrt(HD + DR)
S127 = 1.0 / 127.0
S2 = 1.0 / (127.0 * 127.0)
VSTRIDE = NH * 65


class Scope:
    def __init__(self):
        self.cms = []

    def al(self, cm):
        v = cm.__enter__()
        self.cms.append(cm)
        return v

    def close(self):
        while self.cms:
            self.cms.pop().__exit__(None, None, None)

ENGINES = ("tensor", "vector", "scalar", "gpsimd", "sync")
ENG2CTR = {"tensor": "PE", "vector": "DV", "scalar": "AC", "gpsimd": "GP", "sync": "SY"}


class Plan:

    def __init__(self, nc, sems, counts):
        self.nc = nc
        self.sems = sems
        self.counts = counts
        self.items = []

    def emit(self, eng, fn, waits=(), inc=True):
        own = ENG2CTR[eng]
        amt = 16 if eng in ("sync", "gpsimd") else 1
        ws = []
        for (n, v) in waits:
            if v is None or v <= 0:
                continue
            assert v <= self.counts[n], f"wait on future: {n} {v} > {self.counts[n]}"
            ws.append((n, v))
        if inc:
            self.counts[own] += amt
        self.items.append((eng, tuple(ws), fn, own if inc else None, amt))
        return self.counts[own]

    def dma(self, eng, out, in_, waits=(), ch=None, **kw):
        own = ch or ENG2CTR[eng]
        ws = []
        for (n, v) in waits:
            if v is None or v <= 0:
                continue
            assert v <= self.counts[n], f"wait on future: {n} {v} > {self.counts[n]}"
            ws.append((n, v))
        self.counts[own] += 16
        self.items.append((eng, tuple(ws), lambda: getattr(self.nc, eng).dma_start(out=out, in_=in_, **kw),
                           own, 16))
        return self.counts[own]

    def dmat(self, out, in_, waits=(), ch=None):
        own = ch or "SY"
        ws = []
        for (n, v) in waits:
            if v is None or v <= 0:
                continue
            assert v <= self.counts[n], f"wait on future: {n} {v} > {self.counts[n]}"
            ws.append((n, v))
        self.counts[own] += 16
        self.items.append(("sync", tuple(ws), lambda: self.nc.sync.dma_start_transpose(out, in_), own, 16))
        return self.counts[own]

    def replay(self):
        nc = self.nc
        with nc.Block() as blk:
            for eng_name in ENGINES:
                self._section(blk, eng_name)

    def _section(self, blk, eng_name):
        items = self.items
        sems = self.sems

        def body(eng):
            hi = {}
            for (e, ws, fn, incn, amt) in items:
                if e != eng_name:
                    continue
                for (n, v) in ws:
                    if hi.get(n, 0) < v:
                        eng.wait_ge(sems[n], v)
                        hi[n] = v
                inst = fn()
                if incn:
                    inst.then_inc(sems[incn], amt)

        getattr(blk, eng_name)(body)


def build(debug=False):
    nc = bass.Bass("TRN2", target_bir_lowering=False)

    dx = nc.dram_tensor("x", [T, D], F32, kind="ExternalInput").ap()
    dwq = nc.dram_tensor("wq", [128, NDC * D], F32, kind="ExternalInput").ap()
    dwkv = nc.dram_tensor("wkv", [128, NDC * DCKV], F32, kind="ExternalInput").ap()
    dwqr = nc.dram_tensor("wqr", [128, NDC * DR], F32, kind="ExternalInput").ap()
    dwkup = nc.dram_tensor("wkup", [128, D], F32, kind="ExternalInput").ap()
    dwvup = nc.dram_tensor("wvup", [128, D], F32, kind="ExternalInput").ap()
    dwkr = nc.dram_tensor("wkr", [128, DR], F32, kind="ExternalInput").ap()
    dwo = nc.dram_tensor("wo", [128, NDC * D], F32, kind="ExternalInput").ap()
    dwg = nc.dram_tensor("wg", [128, NHT * D], I32, kind="ExternalInput").ap()
    dwu = nc.dram_tensor("wu", [128, NHT * D], I32, kind="ExternalInput").ap()
    dwd = nc.dram_tensor("wd", [128, NHT * D], I32, kind="ExternalInput").ap()
    dcos = nc.dram_tensor("tcos", [128, NTT * 16], F32, kind="ExternalInput").ap()
    dsin = nc.dram_tensor("tsin", [128, NTT * 16], F32, kind="ExternalInput").ap()
    dcosq = nc.dram_tensor("tcosq", [128, NTT * 16], F32, kind="ExternalInput").ap()
    dsinq = nc.dram_tensor("tsinq", [128, NTT * 16], F32, kind="ExternalInput").ap()
    dmask = nc.dram_tensor("mask", [128, 128], BF16, kind="ExternalInput").ap()
    dident = nc.dram_tensor("ident", [128, 128], BF16, kind="ExternalInput").ap()
    dhmask = nc.dram_tensor("hmask", [128, 2], BF16, kind="ExternalInput").ap()
    doc2 = nc.dram_tensor("oc2", [128, 2], BF16, kind="ExternalInput").ap()

    dy = nc.dram_tensor("y", [T, D], F32, kind="ExternalOutput").ap()
    xnk = "ExternalOutput" if debug else "Internal"
    dxn = nc.dram_tensor("xn_d", [T, D], BF16, kind=xnk).ap()
    datt = nc.dram_tensor("att_d", [T, D], BF16, kind=xnk).ap()
    if debug:
        dqf = nc.dram_tensor("qf_d", [128, NDC * T], BF16, kind="ExternalOutput").ap()
        dkf = nc.dram_tensor("kf_d", [128, NDC * T], BF16, kind="ExternalOutput").ap()
        dck = nc.dram_tensor("ck_d", [128, T], BF16, kind="ExternalOutput").ap()
        dva = nc.dram_tensor("va_d", [128, NTT * VSTRIDE], BF16, kind="ExternalOutput").ap()
        dqr = nc.dram_tensor("qr_d", [32, T], BF16, kind="ExternalOutput").ap()
        dkr = nc.dram_tensor("kr_d", [32, T], BF16, kind="ExternalOutput").ap()
        dnm = nc.dram_tensor("nm_d", [128, 16], F32, kind="ExternalOutput").ap()
        dha = nc.dram_tensor("ha_d", [128, NHT * 1024], BF16, kind="ExternalOutput").ap()

    CHANNELS = tuple(["PE", "DV", "AC", "GP", "SY", "XL0", "XL1", "XS0", "XS1", "CONST",
                     "WL", "NM", "RR", "AT0", "AT1", "WD", "WG0", "WG1",
                     "CX0", "CX1", "YS0", "YS1"]
                    + ["TP%d" % i for i in range(8)] + ["TA%d" % i for i in range(8)])
    counts = {c: 0 for c in CHANNELS}
    ctx = []

    def alloc(cm):
        v = cm.__enter__()
        ctx.append(cm)
        return v

    sems = {}
    for name in CHANNELS:
        sems[name] = alloc(nc.semaphore("s" + name))

    XN = alloc(nc.sbuf_tensor("XN", [128, NDC * T], BF16))
    WO2 = alloc(nc.sbuf_tensor("WO2", [128, NDC * D], BF16))
    MSK = alloc(nc.sbuf_tensor("MSK", [128, 128], BF16))
    IDN = alloc(nc.sbuf_tensor("IDN", [128, 128], BF16))
    HMK = alloc(nc.sbuf_tensor("HMK", [128, 2], BF16))
    OC2 = alloc(nc.sbuf_tensor("OC2", [128, 2], BF16))
    NEGM = alloc(nc.sbuf_tensor("NEGM", [128, 16], F32))

    scA = Scope()
    QF = scA.al(nc.sbuf_tensor("QF", [128, NDC * T], BF16))
    KF = scA.al(nc.sbuf_tensor("KF", [128, NDC * T], BF16))
    CKV = scA.al(nc.sbuf_tensor("CKV", [128, T], BF16))
    VA = scA.al(nc.sbuf_tensor("VA", [128, NTT * VSTRIDE], BF16))
    QRF = scA.al(nc.sbuf_tensor("QRF", [32, T], BF16))
    KRF = scA.al(nc.sbuf_tensor("KRF", [32, T], BF16))
    CT = scA.al(nc.sbuf_tensor("CT", [128, NTT * 16], F32))
    ST = scA.al(nc.sbuf_tensor("ST", [128, NTT * 16], F32))
    CQ = scA.al(nc.sbuf_tensor("CQ", [128, NTT * 16], F32))
    SQ = scA.al(nc.sbuf_tensor("SQ", [128, NTT * 16], F32))
    if True:
        sc1 = Scope()
        WQ = sc1.al(nc.sbuf_tensor("WQ", [128, NDC * D], BF16))
        WKV = sc1.al(nc.sbuf_tensor("WKV", [128, NDC * DCKV], BF16))
        WQR = sc1.al(nc.sbuf_tensor("WQR", [128, NDC * DR], BF16))
        WKUP = sc1.al(nc.sbuf_tensor("WKUP", [128, D], BF16))
        WVUP = sc1.al(nc.sbuf_tensor("WVUP", [128, D], BF16))
        WKR = sc1.al(nc.sbuf_tensor("WKR", [128, DR], BF16))
        LNX = sc1.al(nc.sbuf_tensor("LNX", [128, 2 * D], F32))
        XNT = sc1.al(nc.sbuf_tensor("XNT", [128, 2 * D], BF16))
        BN6 = sc1.al(nc.sbuf_tensor("BN6", [128, 12], F32))
        MV = sc1.al(nc.sbuf_tensor("MV", [128, 2], F32))
        SMT = sc1.al(nc.sbuf_tensor("SMT", [128, 4], F32))
        RSC = sc1.al(nc.sbuf_tensor("RSC", [128, 128], F32))
        QRT = sc1.al(nc.sbuf_tensor("QRT", [128, NTT * DR], BF16))
        KRT = sc1.al(nc.sbuf_tensor("KRT", [128, NTT * DR], BF16))
        SQS = sc1.al(nc.sbuf_tensor("SQS", [128, T], BF16))
        NMX = sc1.al(nc.sbuf_tensor("NMX", [2, 32], F32))
        NSC = sc1.al(nc.sbuf_tensor("NSC", [2, 32], F32))
        PA = sc1.al(nc.psum_tensor("PA", [128, 512], F32))
        PB = sc1.al(nc.psum_tensor("PB", [128, 512], F32))
        PVh = sc1.al(nc.psum_tensor("PVh", [128, 512], F32))
        PR = sc1.al(nc.psum_tensor("PR", [128, 512], F32))
        PT = sc1.al(nc.psum_tensor("PT", [128, 512], BF16))
        PN = sc1.al(nc.psum_tensor("PN", [128, 512], F32))
        PJ = sc1.al(nc.psum_tensor("PJ", [128, 512], F32))
        if True:
            p = Plan(nc, sems, counts)

            p.dma("sync", CT[:], dcos, ch="CONST")
            p.dma("sync", ST[:], dsin, ch="CONST")
            p.dma("sync", CQ[:], dcosq, ch="CONST")
            p.dma("sync", SQ[:], dsinq, ch="CONST")
            p.dma("sync", MSK[:], dmask, ch="CONST")
            p.dma("sync", IDN[:], dident, ch="CONST")
            p.dma("sync", HMK[:], dhmask, ch="CONST")
            c_const = p.dma("sync", OC2[:], doc2, ch="CONST")
            c_tabs = ("CONST", c_const)
            p.dma("gpsimd", WQ[:], dwq, ch="WL")
            p.dma("gpsimd", WKV[:], dwkv, ch="WL")
            p.dma("gpsimd", WQR[:], dwqr, ch="WL")
            p.dma("gpsimd", WKUP[:], dwkup, ch="WL")
            p.dma("gpsimd", WVUP[:], dwvup, ch="WL")
            p.dma("gpsimd", WKR[:], dwkr, ch="WL")
            c_wl = p.dma("gpsimd", WO2[:], dwo, ch="WL")
            c_wo2 = c_wl

            def filler(waits=()):
                p.emit("tensor", lambda: nc.tensor.matmul(
                    PJ[:, 0:128], IDN[:], IDN[:], start=True, stop=True), waits=waits, inc=False)

            for i in range(30):
                filler([("CONST", c_const)] if i == 0 else ())

            c_vaones = p.emit("vector", lambda: nc.vector.memset(VA[:], 1.0))

            ln_xn = [0] * NTT
            ln_store = [0] * NTT
            ln_exp = [0] * NTT
            c_aggr_prev = 0
            for tt in range(NTT):
                b = tt % 2
                w = [("DV", ln_xn[tt - 2])] if tt >= 2 else []
                c_ld = p.dma("sync", LNX[:, b * D:(b + 1) * D], dx[tt * 128:(tt + 1) * 128, :],
                             waits=w, ch="XL%d" % b)
                p.emit("vector", lambda b=b: nc.vector.bn_stats(BN6[:, 0:6], LNX[:, b * D: b * D + 512]),
                       waits=[("XL%d" % b, c_ld), ("DV", c_aggr_prev)])
                c_bn2 = p.emit("vector", lambda b=b: nc.vector.bn_stats(BN6[:, 6:12], LNX[:, b * D + 512: b * D + 1024]))
                c_aggr_prev = p.emit("vector", lambda: nc.vector.bn_aggr(MV[:, 0:2], BN6[:, 0:12]),
                                     waits=[("DV", c_bn2)])
                wv = [("DV", c_aggr_prev)]
                if tt >= 1:
                    wv.append(("AC", ln_exp[tt - 1]))
                c_v = p.emit("vector", lambda: nc.vector.tensor_scalar(SMT[:, 0:1], MV[:, 1:2], 1e-5, None, ALU.add),
                             waits=wv)
                c_ln = p.emit("scalar", lambda: nc.scalar.activation(SMT[:, 1:2], SMT[:, 0:1], AF.Ln),
                              waits=[("DV", c_v)])
                ln_exp[tt] = p.emit("scalar", lambda: nc.scalar.activation(
                    SMT[:, 2:3], SMT[:, 1:2], AF.Exp, scale=-0.5), waits=[("AC", c_ln)])
                c_stt = p.emit("vector", lambda: nc.vector.scalar_tensor_tensor(
                    SMT[:, 3:4], SMT[:, 2:3], -1.0, MV[:, 0:1], ALU.mult, ALU.mult), waits=[("AC", ln_exp[tt])])
                wst = [("DV", c_stt)]
                if tt >= 2:
                    wst.append(("XS%d" % b, ln_store[tt - 2]))
                ln_xn[tt] = p.emit("vector", lambda b=b: nc.vector.tensor_scalar(
                    XNT[:, b * D:(b + 1) * D], LNX[:, b * D:(b + 1) * D], SMT[:, 2:3], SMT[:, 3:4],
                    ALU.mult, ALU.add), waits=wst)
                ln_store[tt] = p.dma("sync", dxn[tt * 128:(tt + 1) * 128, :], XNT[:, b * D:(b + 1) * D],
                                     waits=[("DV", ln_xn[tt])], ch="XS%d" % b)
                filler([("XL%d" % b, c_ld)])
                for _ in range(4):
                    filler()

            c_tp = 0
            for dc in range(NDC):
                w = [("XS0", counts["XS0"]), ("XS1", counts["XS1"]), ("XL0", counts["XL0"]),
                     ("XL1", counts["XL1"]), ("CONST", counts["CONST"]), ("WL", c_wl)] if dc == 0 \
                    else []
                c_tp = p.dmat(XN[:, dc * T:(dc + 1) * T], dxn[:, dc * 128:(dc + 1) * 128],
                              waits=w, ch="TP%d" % dc)
                filler([("TP%d" % dc, 16)])
                for _ in range(12):
                    filler()
            c_xn_ready = [("TP%d" % i, 16) for i in range(8)]

            def proj_group(psum_ap, lhsT_list, rhs_list, first_waits):
                last = 0
                n = len(lhsT_list)
                for i, (lh, rh) in enumerate(zip(lhsT_list, rhs_list)):
                    w = first_waits if i == 0 else []
                    last = p.emit("tensor", lambda lh=lh, rh=rh, i=i, n=n, ps=psum_ap: nc.tensor.matmul(
                        ps, lh, rh, start=(i == 0), stop=(i == n - 1)), waits=w, inc=(i == n - 1))
                return last

            psel = [PA, PB]
            evac = [0, 0]
            for qb in range(4):
                i = qb % 2
                w = c_xn_ready + [("WL", c_wl), ("AC", evac[i])]
                c_mm = proj_group(psel[i][:, 0:512], [WKV[:, dc * 128:(dc + 1) * 128] for dc in range(NDC)],
                                  [XN[:, dc * T + qb * 512: dc * T + (qb + 1) * 512] for dc in range(NDC)], w)
                evac[i] = p.emit("scalar", lambda qb=qb, i=i: nc.scalar.activation(
                    CKV[:, qb * 512:(qb + 1) * 512], psel[i][:, 0:512], AF.Copy), waits=[("PE", c_mm)])
            c_ckv_ready = max(evac)

            for nt in range(NDC):
                for qb in range(4):
                    i = (nt * 4 + qb) % 2
                    c_mm = proj_group(psel[i][:, 0:512],
                                      [WQ[:, dc * D + nt * 128: dc * D + (nt + 1) * 128] for dc in range(NDC)],
                                      [XN[:, dc * T + qb * 512: dc * T + (qb + 1) * 512] for dc in range(NDC)],
                                      [("AC", evac[i])])
                    evac[i] = p.emit("scalar", lambda nt=nt, qb=qb, i=i: nc.scalar.activation(
                        QF[:, nt * T + qb * 512: nt * T + (qb + 1) * 512], psel[i][:, 0:512], AF.Copy,
                        scale=SCALE), waits=[("PE", c_mm)])

            for nt in range(NDC):
                for qb in range(4):
                    i = (nt * 4 + qb) % 2
                    c_mm = proj_group(psel[i][:, 0:512], [WKUP[:, nt * 128:(nt + 1) * 128]],
                                      [CKV[:, qb * 512:(qb + 1) * 512]],
                                      [("AC", max(evac[i], c_ckv_ready))])
                    evac[i] = p.emit("scalar", lambda nt=nt, qb=qb, i=i: nc.scalar.activation(
                        KF[:, nt * T + qb * 512: nt * T + (qb + 1) * 512], psel[i][:, 0:512], AF.Copy),
                        waits=[("PE", c_mm)])

            va_v = VA[:].rearrange("p (tt h j) -> p tt h j", tt=NTT, h=NH)
            c_v_dv = 0
            for tt in range(NTT):
                w0 = [("AC", c_ckv_ready), ("DV", c_vaones)]
                if tt >= 1:
                    w0.append(("DV", c_v_dv))
                c_mm1 = p.emit("tensor", lambda tt=tt: nc.tensor.matmul(
                    PVh[:, 0:512], CKV[:, tt * 128:(tt + 1) * 128], WVUP[:, 0:512],
                    start=True, stop=True), waits=w0)
                c_mm2 = p.emit("tensor", lambda tt=tt: nc.tensor.matmul(
                    PR[:, 0:512], CKV[:, tt * 128:(tt + 1) * 128], WVUP[:, 512:1024],
                    start=True, stop=True))
                wv1 = [("PE", c_mm1)]
                if tt == 0:
                    wv1.append(("DV", c_vaones))
                p.emit("vector", lambda tt=tt: nc.vector.tensor_copy(
                    va_v[:, tt, 0:8, 0:64], PVh[:, 0:512]), waits=wv1)
                c_v_dv = p.emit("vector", lambda tt=tt: nc.vector.tensor_copy(
                    va_v[:, tt, 8:16, 0:64], PR[:, 0:512]), waits=[("PE", c_mm2)])

            def rope_block(tt, psum, rb, ctab, stab, dstT, w_first, qscale=None):
                x1 = psum[:, 0:16]
                x2 = psum[:, 16:32]
                c = ctab[:, tt * 16:(tt + 1) * 16]
                s = stab[:, tt * 16:(tt + 1) * 16]
                o = rb * 64

                def mul(dst, xs, tab):
                    if qscale is None:
                        return lambda: nc.vector.tensor_tensor(dst, xs, tab, ALU.mult)
                    return lambda: nc.vector.scalar_tensor_tensor(dst, xs, qscale, tab,
                                                                  ALU.mult, ALU.mult)
                p.emit("vector", mul(RSC[:, o:o + 16], x1, c), waits=w_first)
                c_b = p.emit("vector", mul(RSC[:, o + 16:o + 32], x2, s))
                p.emit("vector", lambda dstT=dstT, tt=tt: nc.vector.tensor_tensor(
                    dstT[:, tt * 32: tt * 32 + 16], RSC[:, o:o + 16], RSC[:, o + 16:o + 32], ALU.subtract),
                    waits=[("DV", c_b)])
                p.emit("vector", mul(RSC[:, o + 32:o + 48], x2, c))
                c_d = p.emit("vector", mul(RSC[:, o + 48:o + 64], x1, s))
                return p.emit("vector", lambda dstT=dstT, tt=tt: nc.vector.tensor_tensor(
                    dstT[:, tt * 32 + 16: tt * 32 + 32], RSC[:, o + 32:o + 48], RSC[:, o + 48:o + 64],
                    ALU.add), waits=[("DV", c_d)])

            c_rq = [0] * (NTT + 1)
            c_rk = [0] * (NTT + 1)
            c_rf_dv = 0
            QPS = [PR, PA]
            KPS = [PVh, PB]
            for tt in range(NTT):
                qp = QPS[tt % 2]
                kp = KPS[tt % 2]
                wq = [("DV", c_rq[tt - 2])] if tt >= 2 else (
                    [c_tabs, ("DV", c_v_dv), ("AC", counts["AC"])])
                c_mmq = proj_group(qp[:, 0:32],
                                   [XN[:, dc * T + tt * 128: dc * T + (tt + 1) * 128] for dc in range(NDC)],
                                   [WQR[:, dc * DR:(dc + 1) * DR] for dc in range(NDC)], wq)
                wk = [("DV", c_rk[tt - 2])] if tt >= 2 else [("DV", c_v_dv), ("AC", counts["AC"])]
                c_mmk = proj_group(kp[:, 0:32], [CKV[:, tt * 128:(tt + 1) * 128]], [WKR[:, 0:DR]], wk)
                if tt >= 1:
                    c_t1 = p.emit("tensor", lambda tt=tt: nc.tensor.transpose(
                        PT[0:32, 0:128], QRT[:, (tt - 1) * 32: tt * 32], IDN[:]),
                        waits=[("DV", c_rq[tt - 1]), ("DV", c_rf_dv)])
                    c_t2 = p.emit("tensor", lambda tt=tt: nc.tensor.transpose(
                        PT[0:32, 128:256], KRT[:, (tt - 1) * 32: tt * 32], IDN[:]),
                        waits=[("DV", c_rk[tt - 1])])
                c_rq[tt] = rope_block(tt, qp, 0, CQ, SQ, QRT, [("PE", c_mmq), ("DV", c_rq[tt - 1])],
                                      qscale=SCALE)
                c_rk[tt] = rope_block(tt, kp, 1, CT, ST, KRT, [("PE", c_mmk), ("DV", c_rk[tt - 1])])
                if tt >= 1:
                    p.emit("vector", lambda tt=tt: nc.vector.tensor_copy(
                        QRF[:, (tt - 1) * 128: tt * 128], PT[0:32, 0:128]), waits=[("PE", c_t1)])
                    c_rf_dv = p.emit("vector", lambda tt=tt: nc.vector.tensor_copy(
                        KRF[:, (tt - 1) * 128: tt * 128], PT[0:32, 128:256]), waits=[("PE", c_t2)])
            c_t1 = p.emit("tensor", lambda: nc.tensor.transpose(
                PT[0:32, 0:128], QRT[:, (NTT - 1) * 32: NTT * 32], IDN[:]),
                waits=[("DV", c_rq[NTT - 1]), ("DV", c_rf_dv)])
            c_t2 = p.emit("tensor", lambda: nc.tensor.transpose(
                PT[0:32, 128:256], KRT[:, (NTT - 1) * 32: NTT * 32], IDN[:]),
                waits=[("DV", c_rk[NTT - 1])])
            p.emit("vector", lambda: nc.vector.tensor_copy(
                QRF[:, (NTT - 1) * 128: NTT * 128], PT[0:32, 0:128]), waits=[("PE", c_t1)])
            c_rf_dv = p.emit("vector", lambda: nc.vector.tensor_copy(
                KRF[:, (NTT - 1) * 128: NTT * 128], PT[0:32, 128:256]), waits=[("PE", c_t2)])

            c_sqs_rd_pe = 0

            def norm_max(SRC, dst, w_src_dv):
                nonlocal c_sqs_rd_pe
                cdv = counts["DV"]
                for dc in range(NDC):
                    csq = p.emit("vector", lambda dc=dc, SRC=SRC: nc.vector.tensor_tensor(
                        SQS[:, 0:T], SRC[:, dc * T:(dc + 1) * T], SRC[:, dc * T:(dc + 1) * T], ALU.mult),
                        waits=[("PE", c_sqs_rd_pe), ("AC", w_src_dv), ("DV", cdv)])
                    for j in range(4):
                        c_mm = p.emit("tensor", lambda j=j: nc.tensor.matmul(
                            PN[0:2, 0:512], HMK[:, 0:2], SQS[:, j * 512:(j + 1) * 512], start=True, stop=True),
                            waits=[("DV", csq)] if j == 0 else [("DV", cdv)])
                        c_sqs_rd_pe = c_mm
                        cdv = p.emit("vector", lambda dc=dc, j=j: nc.vector.tensor_reduce(
                            NMX[0:2, dc * 4 + j: dc * 4 + j + 1], PN[0:2, 0:512], AX.X, ALU.max),
                            waits=[("PE", c_mm)])
                nmx_v = NMX[0:2, 0:32].rearrange("p (dc j) -> p dc j", dc=8)
                return p.emit("vector", lambda nmx_v=nmx_v, dst=dst: nc.vector.tensor_reduce(
                    dst, nmx_v, AX.X, ALU.max), waits=[("DV", cdv)])

            c_qn = norm_max(QF, NSC[0:2, 0:8], counts["AC"])
            c_kn = norm_max(KF, NSC[0:2, 8:16], counts["AC"])

            def rope_norm_max(SRC, dst, w_src_dv):
                nonlocal c_sqs_rd_pe
                cdv = counts["DV"]
                csq = p.emit("vector", lambda SRC=SRC: nc.vector.tensor_tensor(
                    SQS[0:32, 0:T], SRC[0:32, 0:T], SRC[0:32, 0:T], ALU.mult),
                    waits=[("PE", c_sqs_rd_pe), ("DV", max(w_src_dv, cdv))])
                for j in range(4):
                    c_mm = p.emit("tensor", lambda j=j: nc.tensor.matmul(
                        PN[0:2, 0:512], OC2[0:32, 0:2], SQS[0:32, j * 512:(j + 1) * 512], start=True, stop=True),
                        waits=[("DV", csq), ("CONST", c_const)] if j == 0 else [("DV", cdv)])
                    c_sqs_rd_pe = c_mm
                    cdv = p.emit("vector", lambda j=j: nc.vector.tensor_reduce(
                        NMX[0:2, j:j + 1], PN[0:2, 0:512], AX.X, ALU.max), waits=[("PE", c_mm)])
                return p.emit("vector", lambda dst=dst: nc.vector.tensor_reduce(
                    dst, NMX[0:2, 0:4], AX.X, ALU.max), waits=[("DV", cdv)])

            c_qr2 = rope_norm_max(QRF, NSC[0:2, 16:17], c_rf_dv)
            c_kr2 = rope_norm_max(KRF, NSC[0:2, 17:18], c_rf_dv)

            c1 = p.emit("vector", lambda: nc.vector.tensor_tensor(
                NSC[0:2, 18:26], NSC[0:2, 0:8], NSC[0:2, 8:16], ALU.mult), waits=[("DV", max(c_qn, c_kn))])
            c2 = p.emit("vector", lambda: nc.vector.tensor_tensor(
                NSC[0:2, 26:27], NSC[0:2, 16:17], NSC[0:2, 17:18], ALU.mult),
                waits=[("DV", max(c_qr2, c_kr2))])
            c3 = p.emit("scalar", lambda: nc.scalar.activation(NSC[0:2, 18:26], NSC[0:2, 18:26], AF.Sqrt),
                        waits=[("DV", c2)])
            c4 = p.emit("scalar", lambda: nc.scalar.activation(NSC[0:2, 26:27], NSC[0:2, 26:27], AF.Sqrt))
            c5 = p.emit("vector", lambda: nc.vector.tensor_scalar(
                NSC[0:2, 18:26], NSC[0:2, 18:26], NSC[0:2, 26:27], -1.0, ALU.add, ALU.mult),
                waits=[("AC", c4)])
            p.dma("gpsimd", NEGM[32:33, 0:8], NSC[0:1, 18:26], waits=[("DV", c5)], ch="NM")
            p.dma("gpsimd", NEGM[32:33, 8:16], NSC[1:2, 18:26], ch="NM")
            p.dma("gpsimd", NEGM[96:97, 0:8], NSC[0:1, 18:26], ch="NM")
            c_negm = p.dma("gpsimd", NEGM[96:97, 8:16], NSC[1:2, 18:26], ch="NM")

            if debug:
                wdbg = [("DV", counts["DV"]), ("AC", counts["AC"]), ("NM", counts["NM"])]
                p.dma("sync", dqf, QF[:], waits=wdbg, ch="CONST")
                p.dma("sync", dkf, KF[:], ch="CONST")
                p.dma("sync", dck, CKV[:], ch="CONST")
                p.dma("sync", dva, VA[:], ch="CONST")
                p.dma("sync", dqr, QRF[:], ch="CONST")
                p.dma("sync", dkr, KRF[:], ch="CONST")
                p.dma("sync", dnm, NEGM[:], ch="CONST")

            p.replay()
        sc1.close()

        c_a1_dv = counts["DV"]
        c_a1_ac = counts["AC"]
        c_a1_pe = counts["PE"]
        c_negm = counts["NM"]

        sc2 = Scope()
        QC = sc2.al(nc.sbuf_tensor("QC", [128, 2 * T], BF16))
        KC = sc2.al(nc.sbuf_tensor("KC", [128, 2 * T], BF16))
        ET = sc2.al(nc.sbuf_tensor("ET", [128, 5 * 512], BF16))
        ATS = sc2.al(nc.sbuf_tensor("ATS", [128, 2 * 1024], BF16))
        RCP = sc2.al(nc.sbuf_tensor("RCP", [128, 16], F32))
        STR0 = sc2.al(nc.psum_tensor("STR0", [128, 512], F32))
        STR1 = sc2.al(nc.psum_tensor("STR1", [128, 512], F32))
        STR2 = sc2.al(nc.psum_tensor("STR2", [128, 512], F32))
        STR3 = sc2.al(nc.psum_tensor("STR3", [128, 512], F32))
        STR4 = sc2.al(nc.psum_tensor("STR4", [128, 512], F32))
        AVB0 = sc2.al(nc.psum_tensor("AVB0", [128, 512], F32))
        AVB1 = sc2.al(nc.psum_tensor("AVB1", [128, 512], F32))
        AVB2 = sc2.al(nc.psum_tensor("AVB2", [128, 512], F32))
        if True:
            STRIPS = [STR0, STR1, STR2, STR3, STR4]
            AVB = [AVB0, AVB1, AVB2]
            p = Plan(nc, sems, counts)

            cz1 = p.emit("vector", lambda: nc.vector.memset(QC[:], 0.0),
                         waits=[("DV", c_a1_dv), ("AC", c_a1_ac), ("PE", c_a1_pe)])
            cz2 = p.emit("vector", lambda: nc.vector.memset(KC[:], 0.0))
            p.emit("vector", lambda: nc.vector.memset(KC[96:97, 0:T], 1.0), waits=[("DV", cz2)])
            p.emit("vector", lambda: nc.vector.memset(KC[32:33, T:2 * T], 1.0))
            p.dma("gpsimd", QC[64:96, 0:T], QRF[0:32, 0:T], waits=[("DV", cz1)], ch="RR")
            c_gp_rope = p.dma("gpsimd", KC[64:96, 0:T], KRF[0:32, 0:T], waits=[("DV", cz2)], ch="RR")
            p.emit("vector", lambda: nc.vector.tensor_copy(QC[0:32, T:2 * T], QRF[0:32, 0:T]),
                   waits=[("DV", cz2)])
            c_setup_dv = p.emit("vector", lambda: nc.vector.tensor_copy(KC[0:32, T:2 * T], KRF[0:32, 0:T]))

            slot_exp = [0] * 5
            slot_avdone = [0] * 5
            qk_done = [0] * (NH + 1)
            head_evac_dv = 0
            att_dma = [0, 0]
            datt_v = datt.rearrange("(qt p) n -> p qt n", p=128)
            AVLAG = 3

            def emit_assembly(h):
                s = h % 2
                dc = h // 2
                col = s * 8 + dc
                rows = slice(64 * s, 64 * s + 64)
                mrow = 96 if s == 0 else 32
                base = s * T
                w_cp = [("PE", qk_done[h - 2] if h >= 2 else 0), ("RR", c_gp_rope), ("DV", c_setup_dv)]
                p.emit("vector", lambda rows=rows, base=base, dc=dc: nc.vector.tensor_copy(
                    QC[rows, base: base + T], QF[rows, dc * T:(dc + 1) * T]), waits=w_cp)
                p.emit("vector", lambda rows=rows, base=base, dc=dc: nc.vector.tensor_copy(
                    KC[rows, base: base + T], KF[rows, dc * T:(dc + 1) * T]))
                return p.emit("vector", lambda mrow=mrow, base=base, col=col: nc.vector.tensor_scalar(
                    QC[mrow:mrow + 1, base: base + T], KC[mrow:mrow + 1, base: base + T],
                    NEGM[mrow:mrow + 1, col:col + 1], None, ALU.mult),
                    waits=[("NM", c_negm), ("DV", c_setup_dv)])

            c_asm_arr = [0] * NH
            c_asm_arr[0] = emit_assembly(0)
            evac_bank_prev = [0, 0, 0]

            for h in range(NH):
                s = h % 2
                base = s * T
                av_started = [False, False, False]
                pend = []

                def flush_av(info):
                    slot, kt, c_exp, c_mask, qts, first, ebase = info
                    for idx, qt in enumerate(qts):
                        bank = qt // 7
                        off7 = (qt % 7) * 66
                        eoff = ebase + qt * 128
                        wav = []
                        if idx == 0:
                            wav = [("AC", c_exp)]
                            if c_mask:
                                wav.append(("DV", c_mask))
                        st = not av_started[bank]
                        av_started[bank] = True
                        if st:
                            wav.append(("DV", evac_bank_prev[bank]))
                        is_last = (idx == len(qts) - 1)
                        c_av = p.emit("tensor",
                                      lambda bank=bank, off7=off7, eoff=eoff, st=st, kt=kt, h=h, qt=qt:
                                      nc.tensor.matmul(
                                          AVB[bank][:, off7: off7 + 65], ET[:, eoff: eoff + 128],
                                          VA[:, kt * VSTRIDE + h * 65: kt * VSTRIDE + (h + 1) * 65],
                                          start=st, stop=(qt == kt), skip_group_check=True),
                                      waits=wav, inc=is_last)
                        if is_last:
                            slot_avdone[slot] = c_av

                chunks = [(kt, jb) for kt in range(NTT) for jb in range(kt // 4, 4)]
                for ci, (kt, jb) in enumerate(chunks):
                    slot = ci % 5
                    wqk = [("DV", c_asm_arr[h]), ("AC", slot_exp[slot])]
                    c_qk = p.emit("tensor", lambda slot=slot, kt=kt, jb=jb, base=base: nc.tensor.matmul(
                        STRIPS[slot][:, 0:512], KC[:, base + kt * 128: base + (kt + 1) * 128],
                        QC[:, base + jb * 512: base + (jb + 1) * 512], start=True, stop=True), waits=wqk)
                    c_exp = p.emit("scalar", lambda slot=slot: nc.scalar.activation(
                        ET[:, slot * 512:(slot + 1) * 512], STRIPS[slot][:, 0:512], AF.Exp),
                        waits=[("PE", max(c_qk, slot_avdone[slot]))])
                    slot_exp[slot] = c_exp
                    c_mask = 0
                    if jb == kt // 4:
                        off = slot * 512 + kt * 128 - jb * 512
                        c_mask = p.emit("vector", lambda off=off: nc.vector.tensor_tensor(
                            ET[:, off: off + 128], ET[:, off: off + 128], MSK[:], ALU.mult),
                            waits=[("AC", c_exp)])
                    qts = [qt for qt in range(jb * 4, jb * 4 + 4) if qt >= kt]
                    pend.append((slot, kt, c_exp, c_mask, qts, (ci == 0), slot * 512 - jb * 512))
                    if ci == 1 and h + 1 < NH:
                        c_asm_arr[h + 1] = emit_assembly(h + 1)
                    if len(pend) > AVLAG:
                        flush_av(pend.pop(0))
                while pend:
                    flush_av(pend.pop(0))
                qk_done[h] = counts["PE"]
                c_av_all = counts["PE"]

                c_ats_dv = 0
                for qt in range(NTT):
                    bank = qt // 7
                    off7 = (qt % 7) * 66
                    w = [("PE", c_av_all)]
                    if qt == 0:
                        w.append(("DV", head_evac_dv))
                        if h >= 2:
                            w.append(("AT%d" % s, att_dma[s]))
                    c_rc = p.emit("vector", lambda bank=bank, off7=off7, qt=qt: nc.vector.reciprocal(
                        RCP[:, qt:qt + 1], AVB[bank][:, off7 + 64: off7 + 65]), waits=w)
                    c_ats_dv = p.emit("vector", lambda bank=bank, off7=off7, qt=qt, s=s: nc.vector.tensor_scalar(
                        ATS[:, s * 1024 + qt * 64: s * 1024 + (qt + 1) * 64],
                        AVB[bank][:, off7: off7 + 64], RCP[:, qt:qt + 1], None, ALU.mult),
                        waits=[("DV", c_rc)])
                    if qt in (6, 13, 15):
                        evac_bank_prev[qt // 7] = c_ats_dv
                head_evac_dv = c_ats_dv
                ats_v = ATS[:, s * 1024:(s + 1) * 1024].rearrange("p (qt c) -> p qt c", qt=NTT)
                att_dma[s] = p.dma("sync", datt_v[:, :, h * 64:(h + 1) * 64], ats_v,
                                   waits=[("DV", c_ats_dv)], ch="AT%d" % s)

            p.replay()
            sc2.close()
    scA.close()

    c_att_sy = counts["SY"]
    c_a2_dv = counts["DV"]
    c_a2_pe = counts["PE"]
    c_a2_ac = counts["AC"]
    c_a2_gp = counts["GP"]

    scC = Scope()
    AFM = scC.al(nc.sbuf_tensor("AFM", [128, NDC * T], BF16))
    XT1 = scC.al(nc.sbuf_tensor("XT1", [128, 2 * D], F32))
    YT = scC.al(nc.sbuf_tensor("YT", [128, 2 * D], F32))
    WGT = scC.al(nc.sbuf_tensor("WGT", [128, 2 * D], BF16))
    WUT = scC.al(nc.sbuf_tensor("WUT", [128, 2 * D], BF16))
    WDT = scC.al(nc.sbuf_tensor("WDT", [128, NHT * D], BF16))
    HA = scC.al(nc.sbuf_tensor("HA", [128, NHT * 1024], BF16))
    SIL = scC.al(nc.sbuf_tensor("SIL", [128, 1024], BF16))
    PS = [scC.al(nc.psum_tensor(f"PC{i}", [128, 512], F32)) for i in range(8)]
    if True:
        GPs = [PS[0], PS[1]]
        UPs = [PS[2], PS[3]]
        WPS = [[PS[0], PS[1]], [PS[2], PS[3]]]
        DPS = [[PS[4], PS[5]], [PS[6], PS[7]]]
        p = Plan(nc, sems, counts)

        def fillerC(waits=()):
            p.emit("tensor", lambda: nc.tensor.matmul(
                PS[4][:, 0:128], WO2[:, 0:128], WO2[:, 0:128], start=True, stop=True),
                waits=waits, inc=False)

        c_atp = 0
        for dc in range(NDC):
            w = [("AT0", counts["AT0"]), ("AT1", counts["AT1"]), ("NM", counts["NM"]),
                 ("RR", counts["RR"]), ("WL", counts["WL"]), ("XL0", counts["XL0"]),
                 ("XL1", counts["XL1"]), ("XS0", counts["XS0"]), ("XS1", counts["XS1"]),
                 ("CONST", counts["CONST"])] if dc == 0 else []
            c_atp = p.dmat(AFM[:, dc * T:(dc + 1) * T], datt[:, dc * 128:(dc + 1) * 128],
                           waits=w, ch="TA%d" % dc)
            fillerC([("TA%d" % dc, 16)])
            for _ in range(12):
                fillerC()

        c_atp = [("TA%d" % i, 16) for i in range(8)]
        c_wdt = p.dma("gpsimd", WDT[:], dwd, waits=c_atp, ch="WD")
        wg_ld = [0] * NHT
        wu_ld = [0] * NHT
        for ht in range(2):
            wg_ld[ht] = p.dma("gpsimd", WGT[:, ht * D:(ht + 1) * D], dwg[:, ht * D:(ht + 1) * D],
                              ch="WG%d" % (ht % 2))
            wu_ld[ht] = p.dma("gpsimd", WUT[:, ht * D:(ht + 1) * D], dwu[:, ht * D:(ht + 1) * D],
                              ch="WG%d" % (ht % 2))

        gate_done = [0, 0]
        sl_done = [0, 0]
        ha_done = [0, 0]
        ydone = [0] * NTT
        yst = [0, 0]
        xld = [0, 0]
        down_rd = 0
        c_ha_dv = 0

        for half in range(2):
            for ht in range(NHT):
                b = ht % 2
                nxt = ht + 1 if ht + 1 < NHT else None
                if half == 0 and nxt is not None and nxt >= 2:
                    w = [("PE", gate_done[nxt % 2])]
                    wg_ld[nxt] = p.dma("gpsimd", WGT[:, (nxt % 2) * D:((nxt % 2) + 1) * D],
                                       dwg[:, nxt * D:(nxt + 1) * D], waits=w, ch="WG%d" % (nxt % 2))
                    wu_ld[nxt] = p.dma("gpsimd", WUT[:, (nxt % 2) * D:((nxt % 2) + 1) * D],
                                       dwu[:, nxt * D:(nxt + 1) * D], ch="WG%d" % (nxt % 2))
                elif half == 1:
                    w = [("PE", gate_done[b])]
                    wg_ld[ht] = p.dma("gpsimd", WGT[:, b * D:(b + 1) * D],
                                      dwg[:, ht * D:(ht + 1) * D], waits=w, ch="WG%d" % b)
                    wu_ld[ht] = p.dma("gpsimd", WUT[:, b * D:(b + 1) * D],
                                      dwu[:, ht * D:(ht + 1) * D], ch="WG%d" % b)
                for tb in range(2):
                    t0 = half * 1024 + tb * 512
                    c_gm = 0
                    c_um = 0
                    for dcc in range(NDC):
                        wm = []
                        if dcc == 0:
                            wm = [("WG%d" % b, wu_ld[ht]), ("AC", sl_done[tb])]
                            wm.append(("DV", yst[0]))
                        c_gm = p.emit("tensor", lambda b=b, dcc=dcc, t0=t0, tb=tb: nc.tensor.matmul(
                            GPs[tb][:, 0:512], WGT[:, b * D + dcc * 128: b * D + (dcc + 1) * 128],
                            XN[:, dcc * T + t0: dcc * T + t0 + 512], start=(dcc == 0), stop=(dcc == NDC - 1)),
                            waits=wm, inc=(dcc == NDC - 1))
                    for dcc in range(NDC):
                        wm = []
                        if dcc == 0:
                            wm = [("WG%d" % b, wu_ld[ht]), ("DV", max(ha_done[tb], yst[1]))]
                        c_um = p.emit("tensor", lambda b=b, dcc=dcc, t0=t0, tb=tb: nc.tensor.matmul(
                            UPs[tb][:, 0:512], WUT[:, b * D + dcc * 128: b * D + (dcc + 1) * 128],
                            XN[:, dcc * T + t0: dcc * T + t0 + 512], start=(dcc == 0), stop=(dcc == NDC - 1)),
                            waits=wm, inc=(dcc == NDC - 1))
                    gate_done[b] = c_um
                    wsl = [("PE", c_gm), ("DV", ha_done[tb])]
                    sl_done[tb] = p.emit("scalar", lambda tb=tb: nc.scalar.activation(
                        SIL[:, tb * 512:(tb + 1) * 512], GPs[tb][:, 0:512], AF.Silu, scale=S127), waits=wsl)
                    wha = [("AC", sl_done[tb]), ("PE", c_um)]
                    if half == 1 and ht == 0 and tb == 0:
                        wha.append(("PE", down_rd))
                    c_ha_dv = p.emit("vector", lambda ht=ht, tb=tb: nc.vector.tensor_tensor(
                        HA[:, ht * 1024 + tb * 512: ht * 1024 + (tb + 1) * 512],
                        SIL[:, tb * 512:(tb + 1) * 512], UPs[tb][:, 0:512], ALU.mult), waits=wha)
                    ha_done[tb] = c_ha_dv

            for tti in range(8):
                tt = half * 8 + tti
                b = tti % 2
                w = c_atp + [("DV", yst[b])]
                if tt >= 2:
                    w.append(("YS%d" % b, ydone[tt - 2]))
                xld[b] = p.dma("sync", XT1[:, b * D:(b + 1) * D], dx[tt * 128:(tt + 1) * 128, :],
                               waits=w, ch="CX%d" % b)
                cwo = [0, 0]
                for hf2 in range(2):
                    c_mm = 0
                    for dcc in range(NDC):
                        wm = []
                        if dcc == 0:
                            wm = [("WL", c_wo2), ("DV", yst[b])]
                            if tti == 0 and hf2 == 0:
                                wm.append(("AC", max(sl_done)))
                                wm.append(("DV", c_ha_dv))
                        c_mm = p.emit("tensor", lambda tt=tt, hf2=hf2, dcc=dcc, b=b: nc.tensor.matmul(
                            WPS[b][hf2][:, 0:512], AFM[:, dcc * T + tt * 128: dcc * T + (tt + 1) * 128],
                            WO2[:, dcc * D + hf2 * 512: dcc * D + (hf2 + 1) * 512],
                            start=(dcc == 0), stop=(dcc == NDC - 1)), waits=wm, inc=(dcc == NDC - 1))
                    cwo[hf2] = c_mm
                cdn = [0, 0]
                for hd2 in range(2):
                    c_mm = 0
                    for ht in range(NHT):
                        wm = []
                        if ht == 0:
                            wm = [("DV", max(c_ha_dv, yst[b])), ("WD", c_wdt)]
                        c_mm = p.emit("tensor", lambda b=b, hd2=hd2, ht=ht, tti=tti: nc.tensor.matmul(
                            DPS[b][hd2][:, 0:512], HA[:, ht * 1024 + tti * 128: ht * 1024 + (tti + 1) * 128],
                            WDT[:, ht * D + hd2 * 512: ht * D + (hd2 + 1) * 512],
                            start=(ht == 0), stop=(ht == NHT - 1)), waits=wm, inc=(ht == NHT - 1))
                    cdn[hd2] = c_mm
                down_rd = max(down_rd, cdn[1])
                for hf2 in range(2):
                    c_t1 = p.emit("vector", lambda b=b, hf2=hf2: nc.vector.scalar_tensor_tensor(
                        YT[:, b * D + hf2 * 512: b * D + (hf2 + 1) * 512], DPS[b][hf2][:, 0:512], S2,
                        XT1[:, b * D + hf2 * 512: b * D + (hf2 + 1) * 512], ALU.mult, ALU.add),
                        waits=[("PE", cdn[hf2]), ("CX%d" % b, xld[b])])
                    yst[b] = p.emit("vector", lambda b=b, hf2=hf2: nc.vector.scalar_tensor_tensor(
                        YT[:, b * D + hf2 * 512: b * D + (hf2 + 1) * 512], WPS[b][hf2][:, 0:512], 0.0,
                        YT[:, b * D + hf2 * 512: b * D + (hf2 + 1) * 512], ALU.add, ALU.add),
                        waits=[("PE", cwo[hf2]), ("DV", c_t1)])
                ydone[tt] = p.dma("sync", dy[tt * 128:(tt + 1) * 128, :], YT[:, b * D:(b + 1) * D],
                                  waits=[("DV", yst[b])], ch="YS%d" % b)

        p.emit("vector", lambda: nc.vector.memset(SIL[0:1, 0:1], 0.0),
               waits=[("YS0", counts["YS0"]), ("YS1", counts["YS1"]),
                      ("CX0", counts["CX0"]), ("CX1", counts["CX1"]),
                      ("WD", counts["WD"]), ("WG0", counts["WG0"]), ("WG1", counts["WG1"])]
                     + [("TA%d" % i, counts["TA%d" % i]) for i in range(8)]
                     + [("TP%d" % i, counts["TP%d" % i]) for i in range(8)], inc=False)
        p.replay()
        scC.close()

    while ctx:
        ctx.pop().__exit__(None, None, None)
    return nc


def _host_layouts(W_kv_down, W_k_up, W_v_up, W_q, W_q_rope, W_k_rope, W_o, w_gate, w_up, w_down):
    def chunk_pd(w, inner):
        return np.ascontiguousarray(np.asarray(w, np.float32).reshape(NDC, 128, inner)
                                    .transpose(1, 0, 2).reshape(128, NDC * inner))

    wq = chunk_pd(W_q, D)
    wkv = chunk_pd(W_kv_down, DCKV)
    wqr = chunk_pd(W_q_rope, DR)
    wo = chunk_pd(W_o, D)
    wkup = np.ascontiguousarray(np.asarray(W_k_up, np.float32))
    wvup = np.ascontiguousarray(np.asarray(W_v_up, np.float32))
    wkr = np.ascontiguousarray(np.asarray(W_k_rope, np.float32))
    wg = np.ascontiguousarray(np.asarray(w_gate, np.int32).reshape(NHT, 128, NDC, 128)
                              .transpose(3, 0, 2, 1).reshape(128, NHT * D))
    wu = np.ascontiguousarray(np.asarray(w_up, np.int32).reshape(NHT, 128, NDC, 128)
                              .transpose(3, 0, 2, 1).reshape(128, NHT * D))
    wd = np.ascontiguousarray(np.asarray(w_down, np.int32).T.reshape(NHT, 128, D)
                              .transpose(1, 0, 2).reshape(128, NHT * D))
    return dict(wq=wq, wkv=wkv, wqr=wqr, wkup=wkup, wvup=wvup, wkr=wkr, wo=wo, wg=wg, wu=wu, wd=wd)


def _host_consts():
    import jax
    import jax.numpy as jnp
    with jax.default_device(jax.devices("cpu")[0]):
        inv_freq = 1.0 / 10000.0 ** (jnp.arange(0, DR, 2, dtype=jnp.float32) / DR)
        t = jnp.arange(T, dtype=jnp.float32)
        freqs = jnp.outer(t, inv_freq)
        cos = np.asarray(jnp.cos(freqs), np.float32)
        sin = np.asarray(jnp.sin(freqs), np.float32)

    def tile_tab(tab):
        return np.ascontiguousarray(tab.reshape(NTT, 128, 16).transpose(1, 0, 2).reshape(128, NTT * 16))

    mask = (np.arange(128)[None, :] >= np.arange(128)[:, None]).astype(ml_dtypes.bfloat16)
    ident = np.eye(128).astype(ml_dtypes.bfloat16)
    hmask = np.zeros((128, 2), ml_dtypes.bfloat16)
    hmask[0:64, 0] = 1
    hmask[64:128, 1] = 1
    oc2 = np.ones((128, 2), ml_dtypes.bfloat16)
    return dict(tcos=tile_tab(cos), tsin=tile_tab(sin),
                tcosq=tile_tab(cos), tsinq=tile_tab(sin),
                mask=mask, ident=ident, hmask=hmask, oc2=oc2)


def kernel(L0_attn_x, L0_attn_W_kv_down, L0_attn_W_k_up, L0_attn_W_v_up, L0_attn_W_q,
           L0_attn_W_q_rope, L0_attn_W_k_rope, L0_attn_W_o, L0_ffn_w_gate, L0_ffn_w_up,
           L0_ffn_w_down):
    x = np.ascontiguousarray(np.asarray(L0_attn_x, np.float32))
    weights = _host_layouts(L0_attn_W_kv_down, L0_attn_W_k_up, L0_attn_W_v_up, L0_attn_W_q,
                            L0_attn_W_q_rope, L0_attn_W_k_rope, L0_attn_W_o,
                            L0_ffn_w_gate, L0_ffn_w_up, L0_ffn_w_down)
    consts = _host_consts()
    nc = build(debug=False)
    in_maps = []
    for b in range(8):
        m = {"x": np.ascontiguousarray(x[b])}
        m.update(weights)
        m.update(consts)
        in_maps.append(m)
    res = run_bass_kernel_spmd(nc, in_maps, core_ids=list(range(8)))
    out = np.stack([np.asarray(res.results[b]["y"]) for b in range(8)], axis=0)
    return out.astype(np.float32)


# version 29 (final)
# speedup vs baseline: 1.0541x; 1.0189x over previous
import math
import numpy as np
import ml_dtypes
import concourse.bass as bass
import concourse.mybir as mybir
from concourse.bass_utils import run_bass_kernel_spmd

F32 = mybir.dt.float32
BF16 = mybir.dt.bfloat16
I32 = mybir.dt.int32
AF = mybir.ActivationFunctionType
ALU = mybir.AluOpType
AX = mybir.AxisListType

T = 2048
D = 1024
NDC = 8
DCKV = 128
DR = 32
NH = 16
HD = 64
HID = 3072
NHT = 24
NTT = 16
SCALE = 1.0 / math.sqrt(HD + DR)
S127 = 1.0 / 127.0
S2 = 1.0 / (127.0 * 127.0)
VSTRIDE = NH * 65


class Scope:
    def __init__(self):
        self.cms = []

    def al(self, cm):
        v = cm.__enter__()
        self.cms.append(cm)
        return v

    def close(self):
        while self.cms:
            self.cms.pop().__exit__(None, None, None)

ENGINES = ("tensor", "vector", "scalar", "gpsimd", "sync")
ENG2CTR = {"tensor": "PE", "vector": "DV", "scalar": "AC", "gpsimd": "GP", "sync": "SY"}


class Plan:

    def __init__(self, nc, sems, counts):
        self.nc = nc
        self.sems = sems
        self.counts = counts
        self.items = []

    def emit(self, eng, fn, waits=(), inc=True):
        own = ENG2CTR[eng]
        amt = 16 if eng in ("sync", "gpsimd") else 1
        ws = []
        for (n, v) in waits:
            if v is None or v <= 0:
                continue
            assert v <= self.counts[n], f"wait on future: {n} {v} > {self.counts[n]}"
            ws.append((n, v))
        if inc:
            self.counts[own] += amt
        self.items.append((eng, tuple(ws), fn, own if inc else None, amt))
        return self.counts[own]

    def dma(self, eng, out, in_, waits=(), ch=None, **kw):
        own = ch or ENG2CTR[eng]
        ws = []
        for (n, v) in waits:
            if v is None or v <= 0:
                continue
            assert v <= self.counts[n], f"wait on future: {n} {v} > {self.counts[n]}"
            ws.append((n, v))
        self.counts[own] += 16
        self.items.append((eng, tuple(ws), lambda: getattr(self.nc, eng).dma_start(out=out, in_=in_, **kw),
                           own, 16))
        return self.counts[own]

    def dmat(self, out, in_, waits=(), ch=None):
        own = ch or "SY"
        ws = []
        for (n, v) in waits:
            if v is None or v <= 0:
                continue
            assert v <= self.counts[n], f"wait on future: {n} {v} > {self.counts[n]}"
            ws.append((n, v))
        self.counts[own] += 16
        self.items.append(("sync", tuple(ws), lambda: self.nc.sync.dma_start_transpose(out, in_), own, 16))
        return self.counts[own]

    def replay(self):
        nc = self.nc
        with nc.Block() as blk:
            for eng_name in ENGINES:
                self._section(blk, eng_name)

    def _section(self, blk, eng_name):
        items = self.items
        sems = self.sems

        def body(eng):
            hi = {}
            for (e, ws, fn, incn, amt) in items:
                if e != eng_name:
                    continue
                for (n, v) in ws:
                    if hi.get(n, 0) < v:
                        eng.wait_ge(sems[n], v)
                        hi[n] = v
                inst = fn()
                if incn:
                    inst.then_inc(sems[incn], amt)

        getattr(blk, eng_name)(body)


def build(debug=False):
    nc = bass.Bass("TRN2", target_bir_lowering=False)

    dx = nc.dram_tensor("x", [T, D], F32, kind="ExternalInput").ap()
    dwq = nc.dram_tensor("wq", [128, NDC * D], F32, kind="ExternalInput").ap()
    dwkv = nc.dram_tensor("wkv", [128, NDC * DCKV], F32, kind="ExternalInput").ap()
    dwqr = nc.dram_tensor("wqr", [128, NDC * DR], F32, kind="ExternalInput").ap()
    dwkup = nc.dram_tensor("wkup", [128, D], F32, kind="ExternalInput").ap()
    dwvup = nc.dram_tensor("wvup", [128, D], F32, kind="ExternalInput").ap()
    dwkr = nc.dram_tensor("wkr", [128, DR], F32, kind="ExternalInput").ap()
    dwo = nc.dram_tensor("wo", [128, NDC * D], F32, kind="ExternalInput").ap()
    dwg = nc.dram_tensor("wg", [128, NHT * D], I32, kind="ExternalInput").ap()
    dwu = nc.dram_tensor("wu", [128, NHT * D], I32, kind="ExternalInput").ap()
    dwd = nc.dram_tensor("wd", [128, NHT * D], I32, kind="ExternalInput").ap()
    dcos = nc.dram_tensor("tcos", [128, NTT * 16], F32, kind="ExternalInput").ap()
    dsin = nc.dram_tensor("tsin", [128, NTT * 16], F32, kind="ExternalInput").ap()
    dcosq = nc.dram_tensor("tcosq", [128, NTT * 16], F32, kind="ExternalInput").ap()
    dsinq = nc.dram_tensor("tsinq", [128, NTT * 16], F32, kind="ExternalInput").ap()
    dmask = nc.dram_tensor("mask", [128, 128], BF16, kind="ExternalInput").ap()
    dident = nc.dram_tensor("ident", [128, 128], BF16, kind="ExternalInput").ap()
    dhmask = nc.dram_tensor("hmask", [128, 2], BF16, kind="ExternalInput").ap()
    doc2 = nc.dram_tensor("oc2", [128, 2], BF16, kind="ExternalInput").ap()

    dy = nc.dram_tensor("y", [T, D], F32, kind="ExternalOutput").ap()
    xnk = "ExternalOutput" if debug else "Internal"
    dxn = nc.dram_tensor("xn_d", [T, D], BF16, kind=xnk).ap()
    datt = nc.dram_tensor("att_d", [T, D], BF16, kind=xnk).ap()
    if debug:
        dqf = nc.dram_tensor("qf_d", [128, NDC * T], BF16, kind="ExternalOutput").ap()
        dkf = nc.dram_tensor("kf_d", [128, NDC * T], BF16, kind="ExternalOutput").ap()
        dck = nc.dram_tensor("ck_d", [128, T], BF16, kind="ExternalOutput").ap()
        dva = nc.dram_tensor("va_d", [128, NTT * VSTRIDE], BF16, kind="ExternalOutput").ap()
        dqr = nc.dram_tensor("qr_d", [32, T], BF16, kind="ExternalOutput").ap()
        dkr = nc.dram_tensor("kr_d", [32, T], BF16, kind="ExternalOutput").ap()
        dnm = nc.dram_tensor("nm_d", [128, 16], F32, kind="ExternalOutput").ap()
        dha = nc.dram_tensor("ha_d", [128, NHT * 1024], BF16, kind="ExternalOutput").ap()

    CHANNELS = tuple(["PE", "DV", "AC", "GP", "SY", "XL0", "XL1", "XS0", "XS1", "CONST",
                     "WL", "NM", "RR", "AT0", "AT1", "WD", "WG0", "WG1",
                     "CX0", "CX1", "YS0", "YS1"]
                    + ["TP%d" % i for i in range(8)] + ["TA%d" % i for i in range(8)])
    counts = {c: 0 for c in CHANNELS}
    ctx = []

    def alloc(cm):
        v = cm.__enter__()
        ctx.append(cm)
        return v

    sems = {}
    for name in CHANNELS:
        sems[name] = alloc(nc.semaphore("s" + name))

    XN = alloc(nc.sbuf_tensor("XN", [128, NDC * T], BF16))
    WO2 = alloc(nc.sbuf_tensor("WO2", [128, NDC * D], BF16))
    MSK = alloc(nc.sbuf_tensor("MSK", [128, 128], BF16))
    IDN = alloc(nc.sbuf_tensor("IDN", [128, 128], BF16))
    HMK = alloc(nc.sbuf_tensor("HMK", [128, 2], BF16))
    OC2 = alloc(nc.sbuf_tensor("OC2", [128, 2], BF16))
    NEGM = alloc(nc.sbuf_tensor("NEGM", [128, 16], F32))

    scA = Scope()
    QF = scA.al(nc.sbuf_tensor("QF", [128, NDC * T], BF16))
    KF = scA.al(nc.sbuf_tensor("KF", [128, NDC * T], BF16))
    CKV = scA.al(nc.sbuf_tensor("CKV", [128, T], BF16))
    VA = scA.al(nc.sbuf_tensor("VA", [128, NTT * VSTRIDE], BF16))
    QRF = scA.al(nc.sbuf_tensor("QRF", [32, T], BF16))
    KRF = scA.al(nc.sbuf_tensor("KRF", [32, T], BF16))
    CT = scA.al(nc.sbuf_tensor("CT", [128, NTT * 16], F32))
    ST = scA.al(nc.sbuf_tensor("ST", [128, NTT * 16], F32))
    CQ = scA.al(nc.sbuf_tensor("CQ", [128, NTT * 16], F32))
    SQ = scA.al(nc.sbuf_tensor("SQ", [128, NTT * 16], F32))
    if True:
        sc1 = Scope()
        WQ = sc1.al(nc.sbuf_tensor("WQ", [128, NDC * D], BF16))
        WKV = sc1.al(nc.sbuf_tensor("WKV", [128, NDC * DCKV], BF16))
        WQR = sc1.al(nc.sbuf_tensor("WQR", [128, NDC * DR], BF16))
        WKUP = sc1.al(nc.sbuf_tensor("WKUP", [128, D], BF16))
        WVUP = sc1.al(nc.sbuf_tensor("WVUP", [128, D], BF16))
        WKR = sc1.al(nc.sbuf_tensor("WKR", [128, DR], BF16))
        LNX = sc1.al(nc.sbuf_tensor("LNX", [128, 2 * D], F32))
        XNT = sc1.al(nc.sbuf_tensor("XNT", [128, 2 * D], BF16))
        BN6 = sc1.al(nc.sbuf_tensor("BN6", [128, 12], F32))
        MV = sc1.al(nc.sbuf_tensor("MV", [128, 2], F32))
        SMT = sc1.al(nc.sbuf_tensor("SMT", [128, 4], F32))
        RSC = sc1.al(nc.sbuf_tensor("RSC", [128, 128], F32))
        QRT = sc1.al(nc.sbuf_tensor("QRT", [128, NTT * DR], BF16))
        KRT = sc1.al(nc.sbuf_tensor("KRT", [128, NTT * DR], BF16))
        SQS = sc1.al(nc.sbuf_tensor("SQS", [128, T], BF16))
        NMX = sc1.al(nc.sbuf_tensor("NMX", [2, 32], F32))
        NSC = sc1.al(nc.sbuf_tensor("NSC", [2, 32], F32))
        PA = sc1.al(nc.psum_tensor("PA", [128, 512], F32))
        PB = sc1.al(nc.psum_tensor("PB", [128, 512], F32))
        PVh = sc1.al(nc.psum_tensor("PVh", [128, 512], F32))
        PR = sc1.al(nc.psum_tensor("PR", [128, 512], F32))
        PT = sc1.al(nc.psum_tensor("PT", [128, 512], BF16))
        PN = sc1.al(nc.psum_tensor("PN", [128, 512], F32))
        PN2 = sc1.al(nc.psum_tensor("PN2", [128, 512], F32))
        PJ = sc1.al(nc.psum_tensor("PJ", [128, 512], F32))
        if True:
            p = Plan(nc, sems, counts)

            p.dma("sync", CT[:], dcos, ch="CONST")
            p.dma("sync", ST[:], dsin, ch="CONST")
            p.dma("sync", CQ[:], dcosq, ch="CONST")
            p.dma("sync", SQ[:], dsinq, ch="CONST")
            p.dma("sync", MSK[:], dmask, ch="CONST")
            p.dma("sync", IDN[:], dident, ch="CONST")
            p.dma("sync", HMK[:], dhmask, ch="CONST")
            c_const = p.dma("sync", OC2[:], doc2, ch="CONST")
            c_tabs = ("CONST", c_const)
            p.dma("gpsimd", WQ[:], dwq, ch="WL")
            p.dma("gpsimd", WKV[:], dwkv, ch="WL")
            p.dma("gpsimd", WQR[:], dwqr, ch="WL")
            p.dma("gpsimd", WKUP[:], dwkup, ch="WL")
            p.dma("gpsimd", WVUP[:], dwvup, ch="WL")
            p.dma("gpsimd", WKR[:], dwkr, ch="WL")
            c_wl = p.dma("gpsimd", WO2[:], dwo, ch="WL")
            c_wo2 = c_wl

            def filler(waits=()):
                p.emit("tensor", lambda: nc.tensor.matmul(
                    PJ[:, 0:128], IDN[:], IDN[:], start=True, stop=True), waits=waits, inc=False)

            for i in range(30):
                filler([("CONST", c_const)] if i == 0 else ())

            c_vaones = p.emit("vector", lambda: nc.vector.memset(VA[:], 1.0))

            ln_xn = [0] * NTT
            ln_store = [0] * NTT
            ln_exp = [0] * NTT
            c_aggr_prev = 0
            for tt in range(NTT):
                b = tt % 2
                w = [("DV", ln_xn[tt - 2])] if tt >= 2 else []
                c_ld = p.dma("sync", LNX[:, b * D:(b + 1) * D], dx[tt * 128:(tt + 1) * 128, :],
                             waits=w, ch="XL%d" % b)
                p.emit("vector", lambda b=b: nc.vector.bn_stats(BN6[:, 0:6], LNX[:, b * D: b * D + 512]),
                       waits=[("XL%d" % b, c_ld), ("DV", c_aggr_prev)])
                c_bn2 = p.emit("vector", lambda b=b: nc.vector.bn_stats(BN6[:, 6:12], LNX[:, b * D + 512: b * D + 1024]))
                c_aggr_prev = p.emit("vector", lambda: nc.vector.bn_aggr(MV[:, 0:2], BN6[:, 0:12]),
                                     waits=[("DV", c_bn2)])
                wv = [("DV", c_aggr_prev)]
                if tt >= 1:
                    wv.append(("AC", ln_exp[tt - 1]))
                c_v = p.emit("vector", lambda: nc.vector.tensor_scalar(SMT[:, 0:1], MV[:, 1:2], 1e-5, None, ALU.add),
                             waits=wv)
                c_ln = p.emit("scalar", lambda: nc.scalar.activation(SMT[:, 1:2], SMT[:, 0:1], AF.Ln),
                              waits=[("DV", c_v)])
                ln_exp[tt] = p.emit("scalar", lambda: nc.scalar.activation(
                    SMT[:, 2:3], SMT[:, 1:2], AF.Exp, scale=-0.5), waits=[("AC", c_ln)])
                c_stt = p.emit("vector", lambda: nc.vector.scalar_tensor_tensor(
                    SMT[:, 3:4], SMT[:, 2:3], -1.0, MV[:, 0:1], ALU.mult, ALU.mult), waits=[("AC", ln_exp[tt])])
                wst = [("DV", c_stt)]
                if tt >= 2:
                    wst.append(("XS%d" % b, ln_store[tt - 2]))
                ln_xn[tt] = p.emit("vector", lambda b=b: nc.vector.tensor_scalar(
                    XNT[:, b * D:(b + 1) * D], LNX[:, b * D:(b + 1) * D], SMT[:, 2:3], SMT[:, 3:4],
                    ALU.mult, ALU.add), waits=wst)
                ln_store[tt] = p.dma("sync", dxn[tt * 128:(tt + 1) * 128, :], XNT[:, b * D:(b + 1) * D],
                                     waits=[("DV", ln_xn[tt])], ch="XS%d" % b)
                filler([("XL%d" % b, c_ld)])
                for _ in range(4):
                    filler()

            c_tp = 0
            for dc in range(NDC):
                w = [("XS0", counts["XS0"]), ("XS1", counts["XS1"]), ("XL0", counts["XL0"]),
                     ("XL1", counts["XL1"]), ("CONST", counts["CONST"]), ("WL", c_wl)] if dc == 0 \
                    else []
                c_tp = p.dmat(XN[:, dc * T:(dc + 1) * T], dxn[:, dc * 128:(dc + 1) * 128],
                              waits=w, ch="TP%d" % dc)
                filler([("TP%d" % dc, 16)])
                for _ in range(12):
                    filler()
            c_xn_ready = [("TP%d" % i, 16) for i in range(8)]

            def proj_group(psum_ap, lhsT_list, rhs_list, first_waits):
                last = 0
                n = len(lhsT_list)
                for i, (lh, rh) in enumerate(zip(lhsT_list, rhs_list)):
                    w = first_waits if i == 0 else []
                    last = p.emit("tensor", lambda lh=lh, rh=rh, i=i, n=n, ps=psum_ap: nc.tensor.matmul(
                        ps, lh, rh, start=(i == 0), stop=(i == n - 1)), waits=w, inc=(i == n - 1))
                return last

            psel = [PA, PB]
            evac = [0, 0]
            for qb in range(4):
                i = qb % 2
                w = c_xn_ready + [("WL", c_wl), ("AC", evac[i])]
                c_mm = proj_group(psel[i][:, 0:512], [WKV[:, dc * 128:(dc + 1) * 128] for dc in range(NDC)],
                                  [XN[:, dc * T + qb * 512: dc * T + (qb + 1) * 512] for dc in range(NDC)], w)
                evac[i] = p.emit("scalar", lambda qb=qb, i=i: nc.scalar.activation(
                    CKV[:, qb * 512:(qb + 1) * 512], psel[i][:, 0:512], AF.Copy), waits=[("PE", c_mm)])
            c_ckv_ready = max(evac)

            for nt in range(NDC):
                for qb in range(4):
                    i = (nt * 4 + qb) % 2
                    c_mm = proj_group(psel[i][:, 0:512],
                                      [WQ[:, dc * D + nt * 128: dc * D + (nt + 1) * 128] for dc in range(NDC)],
                                      [XN[:, dc * T + qb * 512: dc * T + (qb + 1) * 512] for dc in range(NDC)],
                                      [("AC", evac[i])])
                    evac[i] = p.emit("scalar", lambda nt=nt, qb=qb, i=i: nc.scalar.activation(
                        QF[:, nt * T + qb * 512: nt * T + (qb + 1) * 512], psel[i][:, 0:512], AF.Copy,
                        scale=SCALE), waits=[("PE", c_mm)])

            for nt in range(NDC):
                for qb in range(4):
                    i = (nt * 4 + qb) % 2
                    c_mm = proj_group(psel[i][:, 0:512], [WKUP[:, nt * 128:(nt + 1) * 128]],
                                      [CKV[:, qb * 512:(qb + 1) * 512]],
                                      [("AC", max(evac[i], c_ckv_ready))])
                    evac[i] = p.emit("scalar", lambda nt=nt, qb=qb, i=i: nc.scalar.activation(
                        KF[:, nt * T + qb * 512: nt * T + (qb + 1) * 512], psel[i][:, 0:512], AF.Copy),
                        waits=[("PE", c_mm)])

            va_v = VA[:].rearrange("p (tt h j) -> p tt h j", tt=NTT, h=NH)
            c_v_dv = 0
            for tt in range(NTT):
                w0 = [("AC", c_ckv_ready), ("DV", c_vaones)]
                if tt >= 1:
                    w0.append(("DV", c_v_dv))
                c_mm1 = p.emit("tensor", lambda tt=tt: nc.tensor.matmul(
                    PVh[:, 0:512], CKV[:, tt * 128:(tt + 1) * 128], WVUP[:, 0:512],
                    start=True, stop=True), waits=w0)
                c_mm2 = p.emit("tensor", lambda tt=tt: nc.tensor.matmul(
                    PR[:, 0:512], CKV[:, tt * 128:(tt + 1) * 128], WVUP[:, 512:1024],
                    start=True, stop=True))
                wv1 = [("PE", c_mm1)]
                if tt == 0:
                    wv1.append(("DV", c_vaones))
                p.emit("vector", lambda tt=tt: nc.vector.tensor_copy(
                    va_v[:, tt, 0:8, 0:64], PVh[:, 0:512]), waits=wv1)
                c_v_dv = p.emit("vector", lambda tt=tt: nc.vector.tensor_copy(
                    va_v[:, tt, 8:16, 0:64], PR[:, 0:512]), waits=[("PE", c_mm2)])

            def rope_block(tt, psum, rb, ctab, stab, dstT, w_first, qscale=None):
                x1 = psum[:, 0:16]
                x2 = psum[:, 16:32]
                c = ctab[:, tt * 16:(tt + 1) * 16]
                s = stab[:, tt * 16:(tt + 1) * 16]
                o = rb * 64

                def mul(dst, xs, tab):
                    if qscale is None:
                        return lambda: nc.vector.tensor_tensor(dst, xs, tab, ALU.mult)
                    return lambda: nc.vector.scalar_tensor_tensor(dst, xs, qscale, tab,
                                                                  ALU.mult, ALU.mult)
                p.emit("vector", mul(RSC[:, o:o + 16], x1, c), waits=w_first)
                c_b = p.emit("vector", mul(RSC[:, o + 16:o + 32], x2, s))
                p.emit("vector", lambda dstT=dstT, tt=tt: nc.vector.tensor_tensor(
                    dstT[:, tt * 32: tt * 32 + 16], RSC[:, o:o + 16], RSC[:, o + 16:o + 32], ALU.subtract),
                    waits=[("DV", c_b)])
                p.emit("vector", mul(RSC[:, o + 32:o + 48], x2, c))
                c_d = p.emit("vector", mul(RSC[:, o + 48:o + 64], x1, s))
                return p.emit("vector", lambda dstT=dstT, tt=tt: nc.vector.tensor_tensor(
                    dstT[:, tt * 32 + 16: tt * 32 + 32], RSC[:, o + 32:o + 48], RSC[:, o + 48:o + 64],
                    ALU.add), waits=[("DV", c_d)])

            c_rq = [0] * (NTT + 1)
            c_rk = [0] * (NTT + 1)
            c_rf_dv = 0
            QPS = [PR, PA]
            KPS = [PVh, PB]
            for tt in range(NTT):
                qp = QPS[tt % 2]
                kp = KPS[tt % 2]
                wq = [("DV", c_rq[tt - 2])] if tt >= 2 else (
                    [c_tabs, ("DV", c_v_dv), ("AC", counts["AC"])])
                c_mmq = proj_group(qp[:, 0:32],
                                   [XN[:, dc * T + tt * 128: dc * T + (tt + 1) * 128] for dc in range(NDC)],
                                   [WQR[:, dc * DR:(dc + 1) * DR] for dc in range(NDC)], wq)
                wk = [("DV", c_rk[tt - 2])] if tt >= 2 else [("DV", c_v_dv), ("AC", counts["AC"])]
                c_mmk = proj_group(kp[:, 0:32], [CKV[:, tt * 128:(tt + 1) * 128]], [WKR[:, 0:DR]], wk)
                if tt >= 1:
                    c_t1 = p.emit("tensor", lambda tt=tt: nc.tensor.transpose(
                        PT[0:32, 0:128], QRT[:, (tt - 1) * 32: tt * 32], IDN[:]),
                        waits=[("DV", c_rq[tt - 1]), ("DV", c_rf_dv)])
                    c_t2 = p.emit("tensor", lambda tt=tt: nc.tensor.transpose(
                        PT[0:32, 128:256], KRT[:, (tt - 1) * 32: tt * 32], IDN[:]),
                        waits=[("DV", c_rk[tt - 1])])
                c_rq[tt] = rope_block(tt, qp, 0, CQ, SQ, QRT, [("PE", c_mmq), ("DV", c_rq[tt - 1])],
                                      qscale=SCALE)
                c_rk[tt] = rope_block(tt, kp, 1, CT, ST, KRT, [("PE", c_mmk), ("DV", c_rk[tt - 1])])
                if tt >= 1:
                    p.emit("vector", lambda tt=tt: nc.vector.tensor_copy(
                        QRF[:, (tt - 1) * 128: tt * 128], PT[0:32, 0:128]), waits=[("PE", c_t1)])
                    c_rf_dv = p.emit("vector", lambda tt=tt: nc.vector.tensor_copy(
                        KRF[:, (tt - 1) * 128: tt * 128], PT[0:32, 128:256]), waits=[("PE", c_t2)])
            c_t1 = p.emit("tensor", lambda: nc.tensor.transpose(
                PT[0:32, 0:128], QRT[:, (NTT - 1) * 32: NTT * 32], IDN[:]),
                waits=[("DV", c_rq[NTT - 1]), ("DV", c_rf_dv)])
            c_t2 = p.emit("tensor", lambda: nc.tensor.transpose(
                PT[0:32, 128:256], KRT[:, (NTT - 1) * 32: NTT * 32], IDN[:]),
                waits=[("DV", c_rk[NTT - 1])])
            p.emit("vector", lambda: nc.vector.tensor_copy(
                QRF[:, (NTT - 1) * 128: NTT * 128], PT[0:32, 0:128]), waits=[("PE", c_t1)])
            c_rf_dv = p.emit("vector", lambda: nc.vector.tensor_copy(
                KRF[:, (NTT - 1) * 128: NTT * 128], PT[0:32, 128:256]), waits=[("PE", c_t2)])

            c_sqs_rd_pe = 0

            def norm_max(SRC, dst, w_src_dv):
                nonlocal c_sqs_rd_pe
                cdv = counts["DV"]
                for dc in range(NDC):
                    csq = p.emit("vector", lambda dc=dc, SRC=SRC: nc.vector.tensor_tensor(
                        SQS[:, 0:T], SRC[:, dc * T:(dc + 1) * T], SRC[:, dc * T:(dc + 1) * T], ALU.mult),
                        waits=[("PE", c_sqs_rd_pe), ("AC", w_src_dv), ("DV", cdv)])
                    c_red = [csq, csq]
                    for j in range(4):
                        bank = [PN, PN2][j % 2]
                        c_mm = p.emit("tensor", lambda j=j, bank=bank: nc.tensor.matmul(
                            bank[0:2, 0:512], HMK[:, 0:2], SQS[:, j * 512:(j + 1) * 512], start=True, stop=True),
                            waits=[("DV", csq)] if j < 2 else [("DV", c_red[j % 2])])
                        c_sqs_rd_pe = c_mm
                        cdv = p.emit("vector", lambda dc=dc, j=j, bank=bank: nc.vector.tensor_reduce(
                            NMX[0:2, dc * 4 + j: dc * 4 + j + 1], bank[0:2, 0:512], AX.X, ALU.max),
                            waits=[("PE", c_mm)])
                        c_red[j % 2] = cdv
                nmx_v = NMX[0:2, 0:32].rearrange("p (dc j) -> p dc j", dc=8)
                return p.emit("vector", lambda nmx_v=nmx_v, dst=dst: nc.vector.tensor_reduce(
                    dst, nmx_v, AX.X, ALU.max), waits=[("DV", cdv)])

            c_qn = norm_max(QF, NSC[0:2, 0:8], counts["AC"])
            c_kn = norm_max(KF, NSC[0:2, 8:16], counts["AC"])

            def rope_norm_max(SRC, dst, w_src_dv):
                nonlocal c_sqs_rd_pe
                cdv = counts["DV"]
                csq = p.emit("vector", lambda SRC=SRC: nc.vector.tensor_tensor(
                    SQS[0:32, 0:T], SRC[0:32, 0:T], SRC[0:32, 0:T], ALU.mult),
                    waits=[("PE", c_sqs_rd_pe), ("DV", max(w_src_dv, cdv))])
                c_red = [csq, csq]
                for j in range(4):
                    bank = [PN, PN2][j % 2]
                    c_mm = p.emit("tensor", lambda j=j, bank=bank: nc.tensor.matmul(
                        bank[0:2, 0:512], OC2[0:32, 0:2], SQS[0:32, j * 512:(j + 1) * 512], start=True, stop=True),
                        waits=([("CONST", c_const)] if j == 0 else [])
                        + ([("DV", csq)] if j < 2 else [("DV", c_red[j % 2])]))
                    c_sqs_rd_pe = c_mm
                    cdv = p.emit("vector", lambda j=j, bank=bank: nc.vector.tensor_reduce(
                        NMX[0:2, j:j + 1], bank[0:2, 0:512], AX.X, ALU.max), waits=[("PE", c_mm)])
                    c_red[j % 2] = cdv
                return p.emit("vector", lambda dst=dst: nc.vector.tensor_reduce(
                    dst, NMX[0:2, 0:4], AX.X, ALU.max), waits=[("DV", cdv)])

            c_qr2 = rope_norm_max(QRF, NSC[0:2, 16:17], c_rf_dv)
            c_kr2 = rope_norm_max(KRF, NSC[0:2, 17:18], c_rf_dv)

            c1 = p.emit("vector", lambda: nc.vector.tensor_tensor(
                NSC[0:2, 18:26], NSC[0:2, 0:8], NSC[0:2, 8:16], ALU.mult), waits=[("DV", max(c_qn, c_kn))])
            c2 = p.emit("vector", lambda: nc.vector.tensor_tensor(
                NSC[0:2, 26:27], NSC[0:2, 16:17], NSC[0:2, 17:18], ALU.mult),
                waits=[("DV", max(c_qr2, c_kr2))])
            c3 = p.emit("scalar", lambda: nc.scalar.activation(NSC[0:2, 18:26], NSC[0:2, 18:26], AF.Sqrt),
                        waits=[("DV", c2)])
            c4 = p.emit("scalar", lambda: nc.scalar.activation(NSC[0:2, 26:27], NSC[0:2, 26:27], AF.Sqrt))
            c5 = p.emit("vector", lambda: nc.vector.tensor_scalar(
                NSC[0:2, 18:26], NSC[0:2, 18:26], NSC[0:2, 26:27], -1.0, ALU.add, ALU.mult),
                waits=[("AC", c4)])
            p.dma("gpsimd", NEGM[32:33, 0:8], NSC[0:1, 18:26], waits=[("DV", c5)], ch="NM")
            p.dma("gpsimd", NEGM[32:33, 8:16], NSC[1:2, 18:26], ch="NM")
            p.dma("gpsimd", NEGM[96:97, 0:8], NSC[0:1, 18:26], ch="NM")
            c_negm = p.dma("gpsimd", NEGM[96:97, 8:16], NSC[1:2, 18:26], ch="NM")

            if debug:
                wdbg = [("DV", counts["DV"]), ("AC", counts["AC"]), ("NM", counts["NM"])]
                p.dma("sync", dqf, QF[:], waits=wdbg, ch="CONST")
                p.dma("sync", dkf, KF[:], ch="CONST")
                p.dma("sync", dck, CKV[:], ch="CONST")
                p.dma("sync", dva, VA[:], ch="CONST")
                p.dma("sync", dqr, QRF[:], ch="CONST")
                p.dma("sync", dkr, KRF[:], ch="CONST")
                p.dma("sync", dnm, NEGM[:], ch="CONST")

            p.replay()
        sc1.close()

        c_a1_dv = counts["DV"]
        c_a1_ac = counts["AC"]
        c_a1_pe = counts["PE"]
        c_negm = counts["NM"]

        sc2 = Scope()
        QC = sc2.al(nc.sbuf_tensor("QC", [128, 2 * T], BF16))
        KC = sc2.al(nc.sbuf_tensor("KC", [128, 2 * T], BF16))
        ET = sc2.al(nc.sbuf_tensor("ET", [128, 5 * 512], BF16))
        ATS = sc2.al(nc.sbuf_tensor("ATS", [128, 2 * 1024], BF16))
        RCP = sc2.al(nc.sbuf_tensor("RCP", [128, 16], F32))
        STR0 = sc2.al(nc.psum_tensor("STR0", [128, 512], F32))
        STR1 = sc2.al(nc.psum_tensor("STR1", [128, 512], F32))
        STR2 = sc2.al(nc.psum_tensor("STR2", [128, 512], F32))
        STR3 = sc2.al(nc.psum_tensor("STR3", [128, 512], F32))
        STR4 = sc2.al(nc.psum_tensor("STR4", [128, 512], F32))
        AVB0 = sc2.al(nc.psum_tensor("AVB0", [128, 512], F32))
        AVB1 = sc2.al(nc.psum_tensor("AVB1", [128, 512], F32))
        AVB2 = sc2.al(nc.psum_tensor("AVB2", [128, 512], F32))
        if True:
            STRIPS = [STR0, STR1, STR2, STR3, STR4]
            AVB = [AVB0, AVB1, AVB2]
            p = Plan(nc, sems, counts)

            cz1 = p.emit("vector", lambda: nc.vector.memset(QC[:], 0.0),
                         waits=[("DV", c_a1_dv), ("AC", c_a1_ac), ("PE", c_a1_pe)])
            cz2 = p.emit("vector", lambda: nc.vector.memset(KC[:], 0.0))
            p.emit("vector", lambda: nc.vector.memset(KC[96:97, 0:T], 1.0), waits=[("DV", cz2)])
            p.emit("vector", lambda: nc.vector.memset(KC[32:33, T:2 * T], 1.0))
            p.dma("gpsimd", QC[64:96, 0:T], QRF[0:32, 0:T], waits=[("DV", cz1)], ch="RR")
            c_gp_rope = p.dma("gpsimd", KC[64:96, 0:T], KRF[0:32, 0:T], waits=[("DV", cz2)], ch="RR")
            p.emit("vector", lambda: nc.vector.tensor_copy(QC[0:32, T:2 * T], QRF[0:32, 0:T]),
                   waits=[("DV", cz2)])
            c_setup_dv = p.emit("vector", lambda: nc.vector.tensor_copy(KC[0:32, T:2 * T], KRF[0:32, 0:T]))

            slot_exp = [0] * 5
            slot_avdone = [0] * 5
            qk_done = [0] * (NH + 1)
            head_evac_dv = 0
            att_dma = [0, 0]
            datt_v = datt.rearrange("(qt p) n -> p qt n", p=128)
            AVLAG = 3

            def emit_assembly(h):
                s = h % 2
                dc = h // 2
                col = s * 8 + dc
                rows = slice(64 * s, 64 * s + 64)
                mrow = 96 if s == 0 else 32
                base = s * T
                w_cp = [("PE", qk_done[h - 2] if h >= 2 else 0), ("RR", c_gp_rope), ("DV", c_setup_dv)]
                p.emit("vector", lambda rows=rows, base=base, dc=dc: nc.vector.tensor_copy(
                    QC[rows, base: base + T], QF[rows, dc * T:(dc + 1) * T]), waits=w_cp)
                p.emit("vector", lambda rows=rows, base=base, dc=dc: nc.vector.tensor_copy(
                    KC[rows, base: base + T], KF[rows, dc * T:(dc + 1) * T]))
                return p.emit("vector", lambda mrow=mrow, base=base, col=col: nc.vector.tensor_scalar(
                    QC[mrow:mrow + 1, base: base + T], KC[mrow:mrow + 1, base: base + T],
                    NEGM[mrow:mrow + 1, col:col + 1], None, ALU.mult),
                    waits=[("NM", c_negm), ("DV", c_setup_dv)])

            c_asm_arr = [0] * NH
            c_asm_arr[0] = emit_assembly(0)
            evac_bank_prev = [0, 0, 0]

            for h in range(NH):
                s = h % 2
                base = s * T
                av_started = [False, False, False]
                pend = []

                def flush_av(info):
                    slot, kt, c_exp, c_mask, qts, first, ebase = info
                    for idx, qt in enumerate(qts):
                        bank = qt // 7
                        off7 = (qt % 7) * 66
                        eoff = ebase + qt * 128
                        wav = []
                        if idx == 0:
                            wav = [("AC", c_exp)]
                            if c_mask:
                                wav.append(("DV", c_mask))
                        st = not av_started[bank]
                        av_started[bank] = True
                        if st:
                            wav.append(("DV", evac_bank_prev[bank]))
                        is_last = (idx == len(qts) - 1)
                        c_av = p.emit("tensor",
                                      lambda bank=bank, off7=off7, eoff=eoff, st=st, kt=kt, h=h, qt=qt:
                                      nc.tensor.matmul(
                                          AVB[bank][:, off7: off7 + 65], ET[:, eoff: eoff + 128],
                                          VA[:, kt * VSTRIDE + h * 65: kt * VSTRIDE + (h + 1) * 65],
                                          start=st, stop=(qt == kt), skip_group_check=True),
                                      waits=wav, inc=is_last)
                        if is_last:
                            slot_avdone[slot] = c_av

                chunks = [(kt, jb) for kt in range(NTT) for jb in range(kt // 4, 4)]
                for ci, (kt, jb) in enumerate(chunks):
                    slot = ci % 5
                    wqk = [("DV", c_asm_arr[h]), ("AC", slot_exp[slot])]
                    c_qk = p.emit("tensor", lambda slot=slot, kt=kt, jb=jb, base=base: nc.tensor.matmul(
                        STRIPS[slot][:, 0:512], KC[:, base + kt * 128: base + (kt + 1) * 128],
                        QC[:, base + jb * 512: base + (jb + 1) * 512], start=True, stop=True), waits=wqk)
                    c_exp = p.emit("scalar", lambda slot=slot: nc.scalar.activation(
                        ET[:, slot * 512:(slot + 1) * 512], STRIPS[slot][:, 0:512], AF.Exp),
                        waits=[("PE", max(c_qk, slot_avdone[slot]))])
                    slot_exp[slot] = c_exp
                    c_mask = 0
                    if jb == kt // 4:
                        off = slot * 512 + kt * 128 - jb * 512
                        c_mask = p.emit("vector", lambda off=off: nc.vector.tensor_tensor(
                            ET[:, off: off + 128], ET[:, off: off + 128], MSK[:], ALU.mult),
                            waits=[("AC", c_exp)])
                    qts = [qt for qt in range(jb * 4, jb * 4 + 4) if qt >= kt]
                    pend.append((slot, kt, c_exp, c_mask, qts, (ci == 0), slot * 512 - jb * 512))
                    if ci == 1 and h + 1 < NH:
                        c_asm_arr[h + 1] = emit_assembly(h + 1)
                    if len(pend) > AVLAG:
                        flush_av(pend.pop(0))
                while pend:
                    flush_av(pend.pop(0))
                qk_done[h] = counts["PE"]
                c_av_all = counts["PE"]

                c_ats_dv = 0
                for qt in range(NTT):
                    bank = qt // 7
                    off7 = (qt % 7) * 66
                    w = [("PE", c_av_all)]
                    if qt == 0:
                        w.append(("DV", head_evac_dv))
                        if h >= 2:
                            w.append(("AT%d" % s, att_dma[s]))
                    c_rc = p.emit("vector", lambda bank=bank, off7=off7, qt=qt: nc.vector.reciprocal(
                        RCP[:, qt:qt + 1], AVB[bank][:, off7 + 64: off7 + 65]), waits=w)
                    c_ats_dv = p.emit("vector", lambda bank=bank, off7=off7, qt=qt, s=s: nc.vector.tensor_scalar(
                        ATS[:, s * 1024 + qt * 64: s * 1024 + (qt + 1) * 64],
                        AVB[bank][:, off7: off7 + 64], RCP[:, qt:qt + 1], None, ALU.mult),
                        waits=[("DV", c_rc)])
                    if qt in (6, 13, 15):
                        evac_bank_prev[qt // 7] = c_ats_dv
                head_evac_dv = c_ats_dv
                ats_v = ATS[:, s * 1024:(s + 1) * 1024].rearrange("p (qt c) -> p qt c", qt=NTT)
                att_dma[s] = p.dma("sync", datt_v[:, :, h * 64:(h + 1) * 64], ats_v,
                                   waits=[("DV", c_ats_dv)], ch="AT%d" % s)

            p.replay()
            sc2.close()
    scA.close()

    c_att_sy = counts["SY"]
    c_a2_dv = counts["DV"]
    c_a2_pe = counts["PE"]
    c_a2_ac = counts["AC"]
    c_a2_gp = counts["GP"]

    scC = Scope()
    AFM = scC.al(nc.sbuf_tensor("AFM", [128, NDC * T], BF16))
    XT1 = scC.al(nc.sbuf_tensor("XT1", [128, 2 * D], F32))
    YT = scC.al(nc.sbuf_tensor("YT", [128, 2 * D], F32))
    WGT = scC.al(nc.sbuf_tensor("WGT", [128, 2 * D], BF16))
    WUT = scC.al(nc.sbuf_tensor("WUT", [128, 2 * D], BF16))
    WDT = scC.al(nc.sbuf_tensor("WDT", [128, NHT * D], BF16))
    HA = scC.al(nc.sbuf_tensor("HA", [128, NHT * 1024], BF16))
    SIL = scC.al(nc.sbuf_tensor("SIL", [128, 1024], BF16))
    PS = [scC.al(nc.psum_tensor(f"PC{i}", [128, 512], F32)) for i in range(8)]
    if True:
        GPs = [PS[0], PS[1]]
        UPs = [PS[2], PS[3]]
        WPS = [[PS[0], PS[1]], [PS[2], PS[3]]]
        DPS = [[PS[4], PS[5]], [PS[6], PS[7]]]
        p = Plan(nc, sems, counts)

        def fillerC(waits=()):
            p.emit("tensor", lambda: nc.tensor.matmul(
                PS[4][:, 0:128], WO2[:, 0:128], WO2[:, 0:128], start=True, stop=True),
                waits=waits, inc=False)

        c_atp = 0
        for dc in range(NDC):
            w = [("AT0", counts["AT0"]), ("AT1", counts["AT1"]), ("NM", counts["NM"]),
                 ("RR", counts["RR"]), ("WL", counts["WL"]), ("XL0", counts["XL0"]),
                 ("XL1", counts["XL1"]), ("XS0", counts["XS0"]), ("XS1", counts["XS1"]),
                 ("CONST", counts["CONST"])] if dc == 0 else []
            c_atp = p.dmat(AFM[:, dc * T:(dc + 1) * T], datt[:, dc * 128:(dc + 1) * 128],
                           waits=w, ch="TA%d" % dc)
            fillerC([("TA%d" % dc, 16)])
            for _ in range(12):
                fillerC()

        c_atp = [("TA%d" % i, 16) for i in range(8)]
        c_wdt = p.dma("gpsimd", WDT[:], dwd, waits=c_atp, ch="WD")
        wg_ld = [0] * NHT
        wu_ld = [0] * NHT
        for ht in range(2):
            wg_ld[ht] = p.dma("gpsimd", WGT[:, ht * D:(ht + 1) * D], dwg[:, ht * D:(ht + 1) * D],
                              ch="WG%d" % (ht % 2))
            wu_ld[ht] = p.dma("gpsimd", WUT[:, ht * D:(ht + 1) * D], dwu[:, ht * D:(ht + 1) * D],
                              ch="WG%d" % (ht % 2))

        gate_done = [0, 0]
        sl_done = [0, 0]
        ha_done = [0, 0]
        ydone = [0] * NTT
        yst = [0, 0]
        xld = [0, 0]
        down_rd = 0
        c_ha_dv = 0

        for half in range(2):
            for ht in range(NHT):
                b = ht % 2
                nxt = ht + 1 if ht + 1 < NHT else None
                if half == 0 and nxt is not None and nxt >= 2:
                    w = [("PE", gate_done[nxt % 2])]
                    wg_ld[nxt] = p.dma("gpsimd", WGT[:, (nxt % 2) * D:((nxt % 2) + 1) * D],
                                       dwg[:, nxt * D:(nxt + 1) * D], waits=w, ch="WG%d" % (nxt % 2))
                    wu_ld[nxt] = p.dma("gpsimd", WUT[:, (nxt % 2) * D:((nxt % 2) + 1) * D],
                                       dwu[:, nxt * D:(nxt + 1) * D], ch="WG%d" % (nxt % 2))
                elif half == 1:
                    w = [("PE", gate_done[b])]
                    wg_ld[ht] = p.dma("gpsimd", WGT[:, b * D:(b + 1) * D],
                                      dwg[:, ht * D:(ht + 1) * D], waits=w, ch="WG%d" % b)
                    wu_ld[ht] = p.dma("gpsimd", WUT[:, b * D:(b + 1) * D],
                                      dwu[:, ht * D:(ht + 1) * D], ch="WG%d" % b)
                for tb in range(2):
                    t0 = half * 1024 + tb * 512
                    c_gm = 0
                    c_um = 0
                    for dcc in range(NDC):
                        wm = []
                        if dcc == 0:
                            wm = [("WG%d" % b, wu_ld[ht]), ("AC", sl_done[tb])]
                            wm.append(("DV", yst[0]))
                        c_gm = p.emit("tensor", lambda b=b, dcc=dcc, t0=t0, tb=tb: nc.tensor.matmul(
                            GPs[tb][:, 0:512], WGT[:, b * D + dcc * 128: b * D + (dcc + 1) * 128],
                            XN[:, dcc * T + t0: dcc * T + t0 + 512], start=(dcc == 0), stop=(dcc == NDC - 1)),
                            waits=wm, inc=(dcc == NDC - 1))
                    for dcc in range(NDC):
                        wm = []
                        if dcc == 0:
                            wm = [("WG%d" % b, wu_ld[ht]), ("DV", max(ha_done[tb], yst[1]))]
                        c_um = p.emit("tensor", lambda b=b, dcc=dcc, t0=t0, tb=tb: nc.tensor.matmul(
                            UPs[tb][:, 0:512], WUT[:, b * D + dcc * 128: b * D + (dcc + 1) * 128],
                            XN[:, dcc * T + t0: dcc * T + t0 + 512], start=(dcc == 0), stop=(dcc == NDC - 1)),
                            waits=wm, inc=(dcc == NDC - 1))
                    gate_done[b] = c_um
                    wsl = [("PE", c_gm), ("DV", ha_done[tb])]
                    sl_done[tb] = p.emit("scalar", lambda tb=tb: nc.scalar.activation(
                        SIL[:, tb * 512:(tb + 1) * 512], GPs[tb][:, 0:512], AF.Silu, scale=S127), waits=wsl)
                    wha = [("AC", sl_done[tb]), ("PE", c_um)]
                    if half == 1 and ht == 0 and tb == 0:
                        wha.append(("PE", down_rd))
                    c_ha_dv = p.emit("vector", lambda ht=ht, tb=tb: nc.vector.tensor_tensor(
                        HA[:, ht * 1024 + tb * 512: ht * 1024 + (tb + 1) * 512],
                        SIL[:, tb * 512:(tb + 1) * 512], UPs[tb][:, 0:512], ALU.mult), waits=wha)
                    ha_done[tb] = c_ha_dv

            for tti in range(8):
                tt = half * 8 + tti
                b = tti % 2
                w = c_atp + [("DV", yst[b])]
                if tt >= 2:
                    w.append(("YS%d" % b, ydone[tt - 2]))
                xld[b] = p.dma("sync", XT1[:, b * D:(b + 1) * D], dx[tt * 128:(tt + 1) * 128, :],
                               waits=w, ch="CX%d" % b)
                cwo = [0, 0]
                for hf2 in range(2):
                    c_mm = 0
                    for dcc in range(NDC):
                        wm = []
                        if dcc == 0:
                            wm = [("WL", c_wo2), ("DV", yst[b])]
                            if tti == 0 and hf2 == 0:
                                wm.append(("AC", max(sl_done)))
                                wm.append(("DV", c_ha_dv))
                        c_mm = p.emit("tensor", lambda tt=tt, hf2=hf2, dcc=dcc, b=b: nc.tensor.matmul(
                            WPS[b][hf2][:, 0:512], AFM[:, dcc * T + tt * 128: dcc * T + (tt + 1) * 128],
                            WO2[:, dcc * D + hf2 * 512: dcc * D + (hf2 + 1) * 512],
                            start=(dcc == 0), stop=(dcc == NDC - 1)), waits=wm, inc=(dcc == NDC - 1))
                    cwo[hf2] = c_mm
                cdn = [0, 0]
                for hd2 in range(2):
                    c_mm = 0
                    for ht in range(NHT):
                        wm = []
                        if ht == 0:
                            wm = [("DV", max(c_ha_dv, yst[b])), ("WD", c_wdt)]
                        c_mm = p.emit("tensor", lambda b=b, hd2=hd2, ht=ht, tti=tti: nc.tensor.matmul(
                            DPS[b][hd2][:, 0:512], HA[:, ht * 1024 + tti * 128: ht * 1024 + (tti + 1) * 128],
                            WDT[:, ht * D + hd2 * 512: ht * D + (hd2 + 1) * 512],
                            start=(ht == 0), stop=(ht == NHT - 1)), waits=wm, inc=(ht == NHT - 1))
                    cdn[hd2] = c_mm
                down_rd = max(down_rd, cdn[1])
                for hf2 in range(2):
                    c_t1 = p.emit("vector", lambda b=b, hf2=hf2: nc.vector.scalar_tensor_tensor(
                        YT[:, b * D + hf2 * 512: b * D + (hf2 + 1) * 512], DPS[b][hf2][:, 0:512], S2,
                        XT1[:, b * D + hf2 * 512: b * D + (hf2 + 1) * 512], ALU.mult, ALU.add),
                        waits=[("PE", cdn[hf2]), ("CX%d" % b, xld[b])])
                    yst[b] = p.emit("vector", lambda b=b, hf2=hf2: nc.vector.scalar_tensor_tensor(
                        YT[:, b * D + hf2 * 512: b * D + (hf2 + 1) * 512], WPS[b][hf2][:, 0:512], 0.0,
                        YT[:, b * D + hf2 * 512: b * D + (hf2 + 1) * 512], ALU.add, ALU.add),
                        waits=[("PE", cwo[hf2]), ("DV", c_t1)])
                ydone[tt] = p.dma("sync", dy[tt * 128:(tt + 1) * 128, :], YT[:, b * D:(b + 1) * D],
                                  waits=[("DV", yst[b])], ch="YS%d" % b)

        p.emit("vector", lambda: nc.vector.memset(SIL[0:1, 0:1], 0.0),
               waits=[("YS0", counts["YS0"]), ("YS1", counts["YS1"]),
                      ("CX0", counts["CX0"]), ("CX1", counts["CX1"]),
                      ("WD", counts["WD"]), ("WG0", counts["WG0"]), ("WG1", counts["WG1"])]
                     + [("TA%d" % i, counts["TA%d" % i]) for i in range(8)]
                     + [("TP%d" % i, counts["TP%d" % i]) for i in range(8)], inc=False)
        p.replay()
        scC.close()

    while ctx:
        ctx.pop().__exit__(None, None, None)
    return nc


def _host_layouts(W_kv_down, W_k_up, W_v_up, W_q, W_q_rope, W_k_rope, W_o, w_gate, w_up, w_down):
    def chunk_pd(w, inner):
        return np.ascontiguousarray(np.asarray(w, np.float32).reshape(NDC, 128, inner)
                                    .transpose(1, 0, 2).reshape(128, NDC * inner))

    wq = chunk_pd(W_q, D)
    wkv = chunk_pd(W_kv_down, DCKV)
    wqr = chunk_pd(W_q_rope, DR)
    wo = chunk_pd(W_o, D)
    wkup = np.ascontiguousarray(np.asarray(W_k_up, np.float32))
    wvup = np.ascontiguousarray(np.asarray(W_v_up, np.float32))
    wkr = np.ascontiguousarray(np.asarray(W_k_rope, np.float32))
    wg = np.ascontiguousarray(np.asarray(w_gate, np.int32).reshape(NHT, 128, NDC, 128)
                              .transpose(3, 0, 2, 1).reshape(128, NHT * D))
    wu = np.ascontiguousarray(np.asarray(w_up, np.int32).reshape(NHT, 128, NDC, 128)
                              .transpose(3, 0, 2, 1).reshape(128, NHT * D))
    wd = np.ascontiguousarray(np.asarray(w_down, np.int32).T.reshape(NHT, 128, D)
                              .transpose(1, 0, 2).reshape(128, NHT * D))
    return dict(wq=wq, wkv=wkv, wqr=wqr, wkup=wkup, wvup=wvup, wkr=wkr, wo=wo, wg=wg, wu=wu, wd=wd)


def _host_consts():
    import jax
    import jax.numpy as jnp
    with jax.default_device(jax.devices("cpu")[0]):
        inv_freq = 1.0 / 10000.0 ** (jnp.arange(0, DR, 2, dtype=jnp.float32) / DR)
        t = jnp.arange(T, dtype=jnp.float32)
        freqs = jnp.outer(t, inv_freq)
        cos = np.asarray(jnp.cos(freqs), np.float32)
        sin = np.asarray(jnp.sin(freqs), np.float32)

    def tile_tab(tab):
        return np.ascontiguousarray(tab.reshape(NTT, 128, 16).transpose(1, 0, 2).reshape(128, NTT * 16))

    mask = (np.arange(128)[None, :] >= np.arange(128)[:, None]).astype(ml_dtypes.bfloat16)
    ident = np.eye(128).astype(ml_dtypes.bfloat16)
    hmask = np.zeros((128, 2), ml_dtypes.bfloat16)
    hmask[0:64, 0] = 1
    hmask[64:128, 1] = 1
    oc2 = np.ones((128, 2), ml_dtypes.bfloat16)
    return dict(tcos=tile_tab(cos), tsin=tile_tab(sin),
                tcosq=tile_tab(cos), tsinq=tile_tab(sin),
                mask=mask, ident=ident, hmask=hmask, oc2=oc2)


def kernel(L0_attn_x, L0_attn_W_kv_down, L0_attn_W_k_up, L0_attn_W_v_up, L0_attn_W_q,
           L0_attn_W_q_rope, L0_attn_W_k_rope, L0_attn_W_o, L0_ffn_w_gate, L0_ffn_w_up,
           L0_ffn_w_down):
    x = np.ascontiguousarray(np.asarray(L0_attn_x, np.float32))
    weights = _host_layouts(L0_attn_W_kv_down, L0_attn_W_k_up, L0_attn_W_v_up, L0_attn_W_q,
                            L0_attn_W_q_rope, L0_attn_W_k_rope, L0_attn_W_o,
                            L0_ffn_w_gate, L0_ffn_w_up, L0_ffn_w_down)
    consts = _host_consts()
    nc = build(debug=False)
    in_maps = []
    for b in range(8):
        m = {"x": np.ascontiguousarray(x[b])}
        m.update(weights)
        m.update(consts)
        in_maps.append(m)
    res = run_bass_kernel_spmd(nc, in_maps, core_ids=list(range(8)))
    out = np.stack([np.asarray(res.results[b]["y"]) for b in range(8)], axis=0)
    return out.astype(np.float32)
